# version 17
# baseline (speedup 1.0000x reference)
import contextlib
import numpy as np
import ml_dtypes
import concourse.bass as bass
import concourse.mybir as mybir
from concourse.bass_utils import run_bass_kernel_spmd
from concourse.aot_env import allow_ici_all_to_all

F32 = mybir.dt.float32
BF16 = mybir.dt.bfloat16
U32 = mybir.dt.uint32
AF = mybir.ActivationFunctionType
ALU = mybir.AluOpType
AX = mybir.AxisListType
IOA = bass.IndirectOffsetOnAxis

NT = 32
D = 1024
NE = 8
SA2A = 1152
CAP = 4096
EPS_LN = 1e-5
SENT = float(2 ** 16)


class Script:

    def __init__(self, sem, dsem=None):
        self.sem = sem
        self.dsem = dsem
        self.ops = []

    def add(self, eng, fn, inc=1):
        self.ops.append((eng, fn, inc))

    def total(self):
        return sum(i for (_, _, i) in self.ops if i == 1)

    def emit(self, name, eng):
        self.emit_range(name, eng, 0, len(self.ops))

    def emit_range(self, name, eng, lo, hi):
        cum_c = 0
        cum_d = 0
        for idx, (en, fn, inc) in enumerate(self.ops):
            if en == name and lo <= idx < hi:
                if cum_c:
                    eng.wait_ge(self.sem, cum_c)
                if cum_d:
                    eng.wait_ge(self.dsem, cum_d)
                h = fn(eng)
                assert h is not None, f"script op on {name} returned no handle"
                h.then_inc(self.sem if inc == 1 else self.dsem, inc)
            if inc == 1:
                cum_c += inc
            else:
                cum_d += inc


def build_nc(debug=False, do_c=True, c_stage=3):
    nc = bass.Bass("TRN2", target_bir_lowering=False)

    xsh = nc.dram_tensor("xsh", [NT * 128, D], F32, kind="ExternalInput").ap()
    winx = nc.dram_tensor("winx", [32, D], F32, kind="ExternalInput").ap()
    winap = nc.dram_tensor("winap", [128, 1], F32, kind="ExternalInput").ap()
    coresel = nc.dram_tensor("coresel", [8, 1], F32, kind="ExternalInput").ap()
    wq = nc.dram_tensor("wq", [D, D], F32, kind="ExternalInput").ap()
    wk = nc.dram_tensor("wk", [D, D], F32, kind="ExternalInput").ap()
    wkp = nc.dram_tensor("wkp", [D, D], F32, kind="ExternalInput").ap()
    wv = nc.dram_tensor("wv", [D, D], F32, kind="ExternalInput").ap()
    wo = nc.dram_tensor("wo", [D, D], F32, kind="ExternalInput").ap()
    gam = nc.dram_tensor("gam", [1, D], F32, kind="ExternalInput").ap()
    bet = nc.dram_tensor("bet", [1, D], F32, kind="ExternalInput").ap()
    wr = nc.dram_tensor("wr", [D, 8], F32, kind="ExternalInput").ap()
    w1my = nc.dram_tensor("w1my", [D, 4096], F32, kind="ExternalInput").ap()
    w2my = nc.dram_tensor("w2my", [4096, D], F32, kind="ExternalInput").ap()
    myexp = nc.dram_tensor("myexp", [8, 8], F32, kind="ExternalInput").ap()
    idbf = nc.dram_tensor("idbf", [128, 128], BF16, kind="ExternalInput").ap()
    idf32 = nc.dram_tensor("idf32", [128, 128], F32, kind="ExternalInput").ap()
    ltm = nc.dram_tensor("ltm", [128, 128], F32, kind="ExternalInput").ap()
    iota8 = nc.dram_tensor("iota8", [128, 8], F32, kind="ExternalInput").ap()
    iom9c = nc.dram_tensor("iom9c", [128, 8], F32, kind="ExternalInput").ap()
    io8p16c = nc.dram_tensor("io8p16c", [128, 8], F32, kind="ExternalInput").ap()
    ones1 = nc.dram_tensor("ones1", [1, 128], F32, kind="ExternalInput").ap()
    wmask = nc.dram_tensor("wmask", [32, 1024], F32, kind="ExternalInput").ap()
    rrep = nc.dram_tensor("rrep", [32, 128], F32, kind="ExternalInput").ap()
    rj = nc.dram_tensor("rj", [32, 1024], F32, kind="ExternalInput").ap()

    out = nc.dram_tensor("out", [NT * 128, D], F32, kind="ExternalOutput").ap()
    dbg = dict(kind="ExternalOutput") if debug else {}
    snd_d = nc.dram_tensor("snd_d", [NE * SA2A, D], BF16, **dbg).ap()
    rcv_d = nc.dram_tensor("rcv_d", [NE * SA2A, D], BF16).ap()
    ysnd_d = nc.dram_tensor("ysnd_d", [NE * SA2A, D], BF16).ap()
    yrcv_d = nc.dram_tensor("yrcv_d", [NE * SA2A + 128, D], BF16, **dbg).ap()
    cc_in = nc.dram_tensor("cc_in", [1, 128], F32)
    cc_out = nc.dram_tensor("cc_out", [8, 128], F32)
    if debug:
        dbg_sml = nc.dram_tensor("dbg_sml", [128, 32], F32, kind="ExternalOutput").ap()
        dbg2 = nc.dram_tensor("dbg2", [128, 8], F32, kind="ExternalOutput").ap()
        dbg_win = nc.dram_tensor("dbg_win", [128, D], F32, kind="ExternalOutput").ap()

    with contextlib.ExitStack() as ctx:
        E = ctx.enter_context

        id_bf = E(nc.sbuf_tensor("id_bf", [128, 128], BF16))
        id_f = E(nc.sbuf_tensor("id_f", [128, 128], F32))
        lt_sb = E(nc.sbuf_tensor("lt_sb", [128, 128], F32))
        io8_sb = E(nc.sbuf_tensor("io8_sb", [128, 8], F32))
        iom9 = E(nc.sbuf_tensor("iom9", [128, 8], F32))
        io8p16 = E(nc.sbuf_tensor("io8p16", [128, 8], F32))
        on1_sb = E(nc.sbuf_tensor("on1_sb", [1, 128], F32))
        onesc = E(nc.sbuf_tensor("onesc", [128, 1], F32))
        pidx = E(nc.sbuf_tensor("pidx", [128, 1], F32))
        eps_t = E(nc.sbuf_tensor("eps_t", [128, 1], F32))
        tiny_t = E(nc.sbuf_tensor("tiny_t", [128, 1], F32))
        gam_bc = E(nc.sbuf_tensor("gam_bc", [128, D], F32))
        bet_bc = E(nc.sbuf_tensor("bet_bc", [128, D], F32))
        wr_sb = E(nc.sbuf_tensor("wr_sb", [128, 8, 8], F32))
        wap_sb = E(nc.sbuf_tensor("wap_sb", [128, 1], F32))
        csel_sb = E(nc.sbuf_tensor("csel_sb", [8, 1], F32))
        winfull = E(nc.sbuf_tensor("winfull", [128, D], F32))
        bases_bc = E(nc.sbuf_tensor("bases_bc", [128, 8], F32))
        cnt_all = E(nc.sbuf_tensor("cnt_all", [8, 8], F32))
        basel = E(nc.sbuf_tensor("basel", [8, 8], F32))
        bcol = E(nc.sbuf_tensor("bcol", [8, 1], F32))
        brow = E(nc.sbuf_tensor("brow", [1, 8], F32))
        myx_sb = E(nc.sbuf_tensor("myx_sb", [8, 8], F32))
        tot_sb = E(nc.sbuf_tensor("tot_sb", [1, 8], F32))
        brun = E(nc.sbuf_tensor("brun", [1, 8], F32))
        tsl = E(nc.sbuf_tensor("tsl", [1, 8], F32))
        zrow = E(nc.sbuf_tensor("zrow", [1, 128], F32))
        zbig = E(nc.sbuf_tensor("zbig", [128, 1024], BF16))
        slot_sb = E(nc.sbuf_tensor("slot_sb", [128, NT], U32))
        fslot_sb = E(nc.sbuf_tensor("fslot_sb", [128, NT], U32))
        srcq_sb = E(nc.sbuf_tensor("srcq_sb", [128, 32], U32))
        dbgt = E(nc.sbuf_tensor("dbgt", [128, 8], F32))

        sem_names = [
            "w", "post", "sX", "sGd",
            "xld0", "xld1", "xld2", "xld3", "dve1", "act1", "dvexn", "w31",
            "acta", "act2", "dve2", "dvean", "dvgb", "x1st0", "x1st1",
            "x1st2", "x1st3", "act3a", "dvev3", "act3", "dvexn2", "petr",
            "aev", "pelg", "dvemx", "actsm", "dvpr", "pecnt", "petot",
            "dvbr", "dverk", "dvxb", "x31",
            "poolsc0", "poolsc1", "poolsc2", "poolsc3",
            "poolsc4", "poolsc5", "poolsc6", "poolsc7",
            "zsnd", "wdm", "cidx", "cg0", "cg1", "cme", "cxt", "cxv",
            "chm", "cha", "cym", "cyv", "cys0", "cys1", "cys2", "cys3", "caa",
            "fld0", "fld1", "fld2", "fld3", "fms", "fyg0", "fyg1", "fyg2",
            "fyg3", "ffa", "fo0", "fo1", "fo2", "fo3",
            "fin", "sGp", "postd", "dbgs", "wdx0", "wdx1", "wdx2",
        ]
        sems = {n: E(nc.semaphore(n)) for n in sem_names}

        def S(n):
            return sems[n]

        n_sX = [0]
        n_sGd = [0]

        def _man_alloc(base):
            off = [base]

            def alloc(name, shape, dtype):
                nbytes = 1
                for s in shape[1:]:
                    nbytes *= s
                nbytes *= 2 if dtype == BF16 else 4
                off[0] = (off[0] + 31) // 32 * 32
                h = nc.alloc_sbuf_tensor_at(name, shape, dtype, offset=off[0])
                off[0] += nbytes
                return h.ap()
            return alloc, off

        sweep_base = nc.sbuf_base
        sctx = E(contextlib.ExitStack())
        Se = sctx.enter_context
        NB = 2
        xin = [Se(nc.sbuf_tensor(f"xin{i}", [128, D], F32)) for i in range(NB)]
        xn_t = [Se(nc.sbuf_tensor(f"xn{i}", [128, D], F32)) for i in range(NB)]
        pre31 = Se(nc.sbuf_tensor("pre31", [128, D], F32))
        x1_t = [Se(nc.sbuf_tensor(f"x1_{i}", [128, D], F32)) for i in range(NB)]
        xn2_t = [Se(nc.sbuf_tensor(f"xn2_{i}", [128, D], F32)) for i in range(NB)]
        xn2f = [Se(nc.sbuf_tensor(f"xn2f{i}", [128, 8, 128], F32)) for i in range(NB)]
        NXB = 6
        xb_t = [Se(nc.sbuf_tensor(f"xb_{i}", [128, D], BF16)) for i in range(NXB)]
        sq_scr = Se(nc.sbuf_tensor("sq_scr", [128, D], F32))
        sv = [Se(nc.sbuf_tensor(f"sv{i}", [128, 16], F32)) for i in range(NB)]
        rmur2 = [Se(nc.sbuf_tensor(f"rmur2_{i}", [128, 4], F32)) for i in range(NB)]
        e8_t = [Se(nc.sbuf_tensor(f"e8_{i}", [128, 8], F32)) for i in range(NB)]
        oh_t = [Se(nc.sbuf_tensor(f"oh_{i}", [128, 8], F32)) for i in range(NB)]
        cnd_t = [Se(nc.sbuf_tensor(f"cnd{i}", [128, 8], F32)) for i in range(NB)]
        rkm_t = [Se(nc.sbuf_tensor(f"rkm{i}", [128, 8], F32)) for i in range(NB)]
        sl_t = [Se(nc.sbuf_tensor(f"sl{i}", [128, 4], F32)) for i in range(NB)]
        pctx = contextlib.ExitStack()
        tr8 = pctx.enter_context(nc.psum_tensor("tr8", [128, 8, 128], F32))
        lgb = pctx.enter_context(nc.psum_tensor("lgb", [128, 16], F32))
        cntt = pctx.enter_context(nc.psum_tensor("cntt", [128, 512], F32))
        window_base = nc.sbuf_base

        with contextlib.ExitStack() as wctx:
            We = wctx.enter_context
            wqb = We(nc.sbuf_tensor("wqb", [128, 8, 1024], BF16))
            wkb = We(nc.sbuf_tensor("wkb", [128, 8, 1024], BF16))
            wkpb = wqb
            wvb = wkb
            wob = wqb
            gam_r = We(nc.sbuf_tensor("gam_r", [1, D], F32))
            bet_r = We(nc.sbuf_tensor("bet_r", [1, D], F32))
            winx_sb = We(nc.sbuf_tensor("winx_sb", [32, D], F32))
            wn = We(nc.sbuf_tensor("wn", [32, D], F32))
            wnT = We(nc.sbuf_tensor("wnT", [128, 8, 32], BF16))
            q_sb = We(nc.sbuf_tensor("q_sb", [32, D], F32))
            k_sb = We(nc.sbuf_tensor("k_sb", [32, D], F32))
            kp_sb = We(nc.sbuf_tensor("kp_sb", [32, D], F32))
            v_sb = We(nc.sbuf_tensor("v_sb", [32, D], F32))
            qt_bf = We(nc.sbuf_tensor("qt_bf", [128, 8, 32], BF16))
            kt_f = We(nc.sbuf_tensor("kt_f", [128, 8, 32], F32))
            kpt_f = We(nc.sbuf_tensor("kpt_f", [128, 8, 32], F32))
            rrep_sb = We(nc.sbuf_tensor("rrep_sb", [32, 128], F32))
            rj_sb = We(nc.sbuf_tensor("rj_sb", [32, 1024], F32))
            vp4 = We(nc.sbuf_tensor("vp4", [128, D], F32))
            vj_s = We(nc.sbuf_tensor("vj_s", [128, D], F32))
            vv_b = We(nc.sbuf_tensor("vv_b", [128, 8, 1024], BF16))
            kkp_b = We(nc.sbuf_tensor("kkp_b", [128, 1024], BF16))
            wms = We(nc.sbuf_tensor("wms", [32, 1024], F32))
            a_sb = We(nc.sbuf_tensor("a_sb", [32, 1024], F32))
            p_bf = We(nc.sbuf_tensor("p_bf", [32, 1024], BF16))
            pt_bf = We(nc.sbuf_tensor("pt_bf", [128, 8, 32], BF16))
            z_sb = We(nc.sbuf_tensor("z_sb", [32, 1024], F32))
            zb_sb = We(nc.sbuf_tensor("zb_sb", [32, 1024], BF16))
            zt_bf = We(nc.sbuf_tensor("zt_bf", [128, 8, 128], BF16))
            wsc1 = We(nc.sbuf_tensor("wsc1", [32, 1024], F32))
            wsc2 = We(nc.sbuf_tensor("wsc2", [32, 16], F32))
            wsm = We(nc.sbuf_tensor("wsm", [32, 1], F32))
            wsm2 = We(nc.sbuf_tensor("wsm2", [32, 1], F32))
            wsm3 = We(nc.sbuf_tensor("wsm3", [32, 1], F32))
            wps = [We(nc.psum_tensor(f"wps{i}", [128, 512], F32)) for i in range(3)]
            wpb = [We(nc.psum_tensor("wpb0", [128, 8, 32], BF16))]
            

            with nc.Block() as blk:
                @blk.sync
                def _(sy):
                    loads = [
                        (id_bf[:], idbf), (id_f[:], idf32), (lt_sb[:], ltm),
                        (io8_sb[:], iota8), (iom9[:], iom9c), (io8p16[:], io8p16c),
                        (on1_sb[:], ones1),
                        (gam_r[:], gam), (bet_r[:], bet),
                        (wr_sb[:], wr.rearrange("(c p) e -> p c e", p=128)),
                        (winx_sb[:], winx), (wap_sb[:], winap), (myx_sb[:], myexp),
                        (csel_sb[:], coresel), (wms[:], wmask),
                        (rrep_sb[:], rrep), (rj_sb[:], rj),
                    ]
                    for dst, src in loads:
                        sy.dma_start(dst, src).then_inc(S("sX"), 16)
                        n_sX[0] += 16

                @blk.gpsimd
                def _(g):
                    g.iota(pidx[:], [[0, 1]], base=0, channel_multiplier=1,
                           allow_small_or_imprecise_dtypes=True)
                    g.memset(zbig[:], 0.0).then_inc(S("sGp"), 1)
                    g.memset(zrow[:], 0.0).then_inc(S("sGp"), 1)
                    g.memset(brun[:], 0.0)
                    g.memset(eps_t[:], EPS_LN)
                    g.memset(tiny_t[:], 1e-38)
                    g.memset(onesc[:], 1.0)
                    g.memset(winfull[:], 0.0)
                    g.memset(zt_bf[:], 0.0)
                    if debug:
                        g.memset(dbgt[:], 0.0)
                    g.wait_ge(S("sGp"), 2)
                    g.dma_start(cc_in.ap(), zrow[:]).then_inc(S("sGd"), 16)
                    for wsrc, wdst in [(wq, wqb), (wk, wkb)]:
                        g.dma_start(wdst[:], wsrc.rearrange("(c p) f -> p c f", p=128)
                                    ).then_inc(S("sGd"), 16)
                    n_sGd[0] += 3 * 16

            W = Script(S("w"), S("wdx0"))

            def mm(t, ps, lhsT, rhs, start=True, stop=True, sg=False):
                return t.matmul(ps, lhsT, rhs, start=start, stop=stop,
                                skip_group_check=sg)

            for half in range(2):
                W.add("tensor", lambda t, h=half: mm(t, wps[0][:, 0:512], on1_sb[:],
                                                     gam_r[0:1, 512 * h:512 * h + 512]))
                W.add("vector", lambda v, h=half: v.tensor_copy(
                    gam_bc[:, 512 * h:512 * h + 512], wps[0][:, 0:512]))
                W.add("tensor", lambda t, h=half: mm(t, wps[1][:, 0:512], on1_sb[:],
                                                     bet_r[0:1, 512 * h:512 * h + 512]))
                W.add("vector", lambda v, h=half: v.tensor_copy(
                    bet_bc[:, 512 * h:512 * h + 512], wps[1][:, 0:512]))

            W.add("scalar", lambda a: a.activation(wsc1[:], winx_sb[:], AF.Square,
                                                   accum_out=wsm[:]))
            W.add("scalar", lambda a: a.activation(wsc1[:], winx_sb[:], AF.Identity,
                                                   accum_out=wsm2[:]))
            W.add("vector", lambda v: v.tensor_scalar(wsm2[:], wsm2[:], 1.0 / D,
                                                      None, ALU.mult))
            W.add("vector", lambda v: v.tensor_scalar(wsm[:], wsm[:], 1.0 / D,
                                                      None, ALU.mult))
            W.add("vector", lambda v: v.scalar_tensor_tensor(
                wsm3[:], wsm2[:], wsm2[0:32, :], wsm[:], ALU.mult, ALU.subtract))
            W.add("scalar", lambda a: a.activation(wsm[:], wsm3[:], AF.Ln,
                                                   bias=eps_t[0:32, :], scale=-1.0))
            W.add("scalar", lambda a: a.activation(wsm[:], wsm[:], AF.Exp,
                                                   bias=0.0, scale=-0.5))
            W.add("vector", lambda v: v.tensor_scalar(wsm2[:], wsm2[:], wsm[0:32, :],
                                                      -1.0, ALU.mult, ALU.mult))
            W.add("vector", lambda v: v.tensor_scalar(wn[:], winx_sb[:], wsm[0:32, :],
                                                      wsm2[0:32, :], ALU.mult, ALU.add))

            for c in range(8):
                W.add("tensor", lambda t, c=c: t.transpose(
                    wps[2][:, 32 * c:32 * c + 32], wn[0:32, 128 * c:128 * c + 128],
                    id_f[0:32, 0:32]))
            W.add("vector", lambda v: v.tensor_copy(wnT[:], wps[2][:, 0:256]))

            def wait_wdma(t):
                t.wait_ge(S("sGd"), n_sGd[0])
                return t.nop()
            W.add("tensor", wait_wdma)

            def _proj(wbuf, dest):
                for fh in range(2):
                    for dc in range(8):
                        W.add("tensor", lambda t, fh=fh, dc=dc, wbuf=wbuf: mm(
                            t, wps[2][0:32, 0:512], wnT[:, dc, :],
                            wbuf[:, dc, 512 * fh:512 * fh + 512],
                            start=(dc == 0), stop=(dc == 7)))
                    W.add("vector", lambda v, fh=fh, dest=dest: v.tensor_copy(
                        dest[:, 512 * fh:512 * fh + 512], wps[2][0:32, 0:512]))

            wmarks = []
            _proj(wqb, q_sb)
            wmarks.append(W.total())
            _proj(wkb, k_sb)
            wmarks.append(W.total())
            W.add("tensor", lambda t: (t.wait_ge(S("wdx0"), 16), t.nop())[1])
            _proj(wkpb, kp_sb)
            wmarks.append(W.total())
            W.add("tensor", lambda t: (t.wait_ge(S("wdx1"), 16), t.nop())[1])
            _proj(wvb, v_sb)

            for dest in (q_sb, k_sb, kp_sb):
                W.add("vector", lambda v, d=dest: v.tensor_tensor(
                    wsc1[:], d[:], d[:], ALU.mult))
                W.add("vector", lambda v: v.tensor_reduce(
                    wsc2[:], wsc1[0:32, :].rearrange("p (h d) -> p h d", d=64),
                    AX.X, ALU.add))
                W.add("scalar", lambda a: a.activation(wsc2[:], wsc2[:], AF.Ln,
                                                       bias=tiny_t[0:32, :], scale=1.0))
                W.add("scalar", lambda a: a.activation(wsc2[:], wsc2[:], AF.Exp,
                                                       bias=0.0, scale=0.5))
                W.add("vector", lambda v: v.tensor_scalar(wsc2[:], wsc2[:], 1e-7,
                                                          None, ALU.add))
                W.add("vector", lambda v: v.reciprocal(wsc2[:], wsc2[:]))
                for h in range(16):
                    W.add("vector", lambda v, h=h, d=dest: v.tensor_scalar(
                        d[:, 64 * h:64 * h + 64], d[:, 64 * h:64 * h + 64],
                        wsc2[0:32, h:h + 1], None, ALU.mult))

            for c in range(8):
                W.add("tensor", lambda t, c=c: t.transpose(
                    wps[2][:, 32 * c:32 * c + 32], q_sb[0:32, 128 * c:128 * c + 128],
                    id_f[0:32, 0:32]))
            W.add("scalar", lambda a: a.activation(
                qt_bf[:], wps[2][:, 0:256], AF.Identity, bias=0.0, scale=0.125))
            for src, dst in [(k_sb, kt_f), (kp_sb, kpt_f)]:
                for c in range(8):
                    W.add("tensor", lambda t, c=c, src=src: t.transpose(
                        wps[2][:, 32 * c:32 * c + 32], src[0:32, 128 * c:128 * c + 128],
                        id_f[0:32, 0:32]))
                W.add("vector", lambda v, dst=dst: v.tensor_copy(
                    dst[:], wps[2][:, 0:256]))

            for fh in range(2):
                W.add("tensor", lambda t, fh=fh: mm(t, wps[2][:, 0:512], rrep_sb[:],
                                                    v_sb[0:32, 512 * fh:512 * fh + 512]))
                W.add("vector", lambda v, fh=fh: v.tensor_copy(
                    vp4[:, 512 * fh:512 * fh + 512], wps[2][:, 0:512]))
            for c in range(8):
                for fh in range(2):
                    W.add("tensor", lambda t, c=c, fh=fh: mm(
                        t, wps[2][:, 0:512], rj_sb[0:32, 128 * c:128 * c + 128],
                        v_sb[0:32, 512 * fh:512 * fh + 512]))
                    W.add("vector", lambda v, fh=fh: v.tensor_copy(
                        vj_s[:, 512 * fh:512 * fh + 512], wps[2][:, 0:512]))
                W.add("vector", lambda v, c=c: v.tensor_tensor(
                    vv_b[:, c, :], vj_s[:], vp4[:], ALU.mult))

            zps_banks = (wps[0], wps[1])
            for hc in range(8):
                W.add("vector", lambda v, hc=hc: v.tensor_tensor(
                    kkp_b[:].rearrange("p (j k) -> p j k", k=32),
                    bass.AP(kt_f, hc * 32, [[256, 128], [1, 32], [0, 32]]),
                    bass.AP(kpt_f, hc * 32, [[256, 128], [0, 32], [1, 32]]),
                    ALU.mult))
                for u in range(2):
                    h = 2 * hc + u
                    for fh in range(2):
                        W.add("tensor", lambda t, u=u, hc=hc, fh=fh: mm(
                            t, wps[2][0:32, 0:512],
                            qt_bf[64 * u:64 * u + 64, hc, :],
                            kkp_b[64 * u:64 * u + 64, 512 * fh:512 * fh + 512]))
                        W.add("vector", lambda v, fh=fh: v.tensor_tensor(
                            a_sb[:, 512 * fh:512 * fh + 512], wps[2][0:32, 0:512],
                            wms[:, 512 * fh:512 * fh + 512], ALU.add))
                    W.add("vector", lambda v: v.reduce_max(wsm[:], a_sb[:], AX.X))
                    W.add("vector", lambda v: v.tensor_scalar(wsm[:], wsm[:], -1.0,
                                                              None, ALU.mult))
                    W.add("scalar", lambda a: a.activation(wsc1[:], a_sb[:], AF.Exp,
                                                           bias=wsm[0:32, :], scale=1.0,
                                                           accum_out=wsm2[:]))
                    W.add("vector", lambda v: v.reciprocal(wsm2[:], wsm2[:]))
                    W.add("vector", lambda v: v.tensor_scalar(p_bf[:], wsc1[:],
                                                              wsm2[0:32, :], None,
                                                              ALU.mult))
                    for c in range(8):
                        W.add("tensor", lambda t, c=c: t.transpose(
                            wpb[0][:, c, :], p_bf[0:32, 128 * c:128 * c + 128],
                            id_bf[0:32, 0:32]))
                    W.add("vector", lambda v: v.tensor_copy(pt_bf[:], wpb[0][:]))
                    for c in range(8):
                        W.add("tensor", lambda t, c=c, h=h: mm(
                            t, zps_banks[h // 8][0:32, 64 * (h % 8):64 * (h % 8) + 64],
                            pt_bf[:, c, :], vv_b[:, c, 64 * h:64 * h + 64],
                            start=(c == 0), stop=(c == 7), sg=True))

            for b2 in range(2):
                W.add("vector", lambda v, b2=b2: v.tensor_copy(
                    z_sb[:, 512 * b2:512 * b2 + 512], zps_banks[b2][0:32, 0:512]))
            W.add("scalar", lambda a: a.activation(zb_sb[:], z_sb[:], AF.Identity,
                                                   bias=0.0, scale=1.0))
            for c in range(8):
                W.add("tensor", lambda t, c=c: t.transpose(
                    wpb[0][:, c, :], zb_sb[0:32, 128 * c:128 * c + 128],
                    id_bf[0:32, 0:32]))
            W.add("vector", lambda v: v.tensor_copy(zt_bf[:, :, 96:128], wpb[0][:]))
            W.add("tensor", lambda t: (t.wait_ge(S("wdx2"), 16), t.nop())[1])
            for fh in range(2):
                for dc in range(8):
                    W.add("tensor", lambda t, fh=fh, dc=dc: mm(
                        t, wps[2][:, 0:512], zt_bf[:, dc, :],
                        wob[:, dc, 512 * fh:512 * fh + 512],
                        start=(dc == 0), stop=(dc == 7)))
                W.add("vector", lambda v, fh=fh: v.tensor_copy(
                    winfull[:, 512 * fh:512 * fh + 512], wps[2][:, 0:512]))

            W_TOTAL = W.total()
            W_LEN = len(W.ops)

            def SEG(k):
                return min(W_LEN, (W_LEN * (k + 1) + 61) // 62)

            assert SEG(2 * (NXB + 2)) >= wmarks[2], (wmarks, W_LEN)

            wptr = {"scalar": 0, "vector": 0, "tensor": 0}

            def wseg(name, eng, hi):
                hi = min(hi, W_LEN)
                if wptr[name] < hi:
                    W.emit_range(name, eng, wptr[name], hi)
                    wptr[name] = hi

        w1b_r = E(nc.sbuf_tensor("w1b_r", [128, 8, 4096], BF16))
        w2b_r = E(nc.sbuf_tensor("w2b_r", [128, 32, D], BF16))

        if True:
            xtile = xsh.rearrange("(t p) f -> t p f", p=128)
            otile = out.rearrange("(t p) f -> t p f", p=128)

            with nc.Block() as blk:
                @blk.sync
                def _(sy):
                    for t in range(NT):
                        if t >= NB:
                            sy.wait_ge(S("dvgb"), t - NB + 1)
                        sy.dma_start(xin[t % NB][:], xtile[t]).then_inc(S(f"xld{t % 4}"), 16)

                @blk.scalar
                def _(a):
                    def rstd_smalls(b, scol, qcol, t1col, ocol, sem, base):
                        a.wait_ge(S(sem), base + 2)
                        a.activation(sv[b][:, t1col:t1col + 1], sv[b][:, scol:scol + 1],
                                     AF.Square, scale=1.0 / D).then_inc(S(sem), 1)
                        a.wait_ge(S(sem), base + 3)
                        a.activation(sv[b][:, t1col:t1col + 1], sv[b][:, t1col:t1col + 1],
                                     AF.Identity, bias=eps_t[:],
                                     scale=-1.0).then_inc(S(sem), 1)
                        a.wait_ge(S(sem), base + 4)
                        a.activation(sv[b][:, ocol:ocol + 1], sv[b][:, qcol:qcol + 1],
                                     AF.Ln, bias=sv[b][:, t1col:t1col + 1],
                                     scale=1.0 / D).then_inc(S(sem), 1)
                        a.wait_ge(S(sem), base + 5)
                        a.activation(sv[b][:, ocol:ocol + 1], sv[b][:, ocol:ocol + 1],
                                     AF.Exp, bias=0.0, scale=-0.5).then_inc(S(sem), 1)

                    def stage_a1(t):
                        b = t % NB
                        a.wait_ge(S(f"xld{t % 4}"), 16 * (t // 4 + 1))
                        if t >= NB:
                            a.wait_ge(S("dvexn"), 2 * (t - NB) + 2)
                        a.activation(sq_scr[:], xin[b][:], AF.Square,
                                     accum_out=sv[b][:, 1:2]).then_inc(S("act1"), 1)
                        a.wait_ge(S("act1"), 6 * t + 1)
                        a.activation(sq_scr[:], xin[b][:], AF.Identity,
                                     accum_out=sv[b][:, 0:1]).then_inc(S("act1"), 1)
                        rstd_smalls(b, 0, 1, 2, 3, "act1", 6 * t)

                    def stage_a2(t):
                        b = t % NB
                        pre = pre31 if t == NT - 1 else xn_t[b]
                        a.wait_ge(S("dvexn"), 2 * t + 2)
                        if t == NT - 1:
                            a.wait_ge(S("w31"), 1)
                        if t >= NB:
                            a.wait_ge(S("dvean"), 3 * (t - NB) + 3)
                        a.activation(sq_scr[:], pre[:], AF.Square,
                                     accum_out=rmur2[b][:, 3:4]).then_inc(S("acta"), 1)
                        if t == NT - 1:
                            a.wait_ge(S("acta"), 3 * t + 1)
                            a.activation(sq_scr[:], pre[:], AF.Identity,
                                         accum_out=rmur2[b][:, 2:3]).then_inc(S("x31"), 1)
                            a.wait_ge(S("x31"), 1)
                            a.activation(rmur2[b][:, 1:2], rmur2[b][:, 2:3],
                                         AF.Square, scale=1.0 / D).then_inc(S("x31"), 1)
                            a.wait_ge(S("x31"), 2)
                            a.activation(rmur2[b][:, 1:2], rmur2[b][:, 1:2],
                                         AF.Identity, bias=eps_t[:],
                                         scale=-1.0).then_inc(S("x31"), 1)
                            a.wait_ge(S("x31"), 3)
                            a.wait_ge(S("acta"), 3 * t + 1)
                            a.activation(rmur2[b][:, 0:1], rmur2[b][:, 3:4], AF.Ln,
                                         bias=rmur2[b][:, 1:2],
                                         scale=1.0 / D).then_inc(S("acta"), 1)
                        else:
                            a.wait_ge(S("acta"), 3 * t + 1)
                            a.activation(rmur2[b][:, 0:1], rmur2[b][:, 3:4], AF.Ln,
                                         bias=eps_t[:], scale=1.0 / D).then_inc(S("acta"), 1)
                        a.wait_ge(S("acta"), 3 * t + 2)
                        a.activation(rmur2[b][:, 0:1], rmur2[b][:, 0:1], AF.Exp,
                                     bias=0.0, scale=-0.5).then_inc(S("acta"), 1)

                    def stage_b1(u):
                        bu = u % NB
                        a.wait_ge(S("dvgb"), u + 1)
                        a.dma_start(otile[u], x1_t[bu][:]).then_inc(S(f"x1st{u % 4}"), 16)
                        if u >= NB:
                            a.wait_ge(S("dvexn2"), 2 * (u - NB) + 2)
                        a.activation(sq_scr[:], x1_t[bu][:], AF.Square,
                                     accum_out=sv[bu][:, 6:7]).then_inc(S("act3"), 1)
                        a.wait_ge(S("act3"), 6 * u + 1)
                        a.activation(sq_scr[:], x1_t[bu][:], AF.Identity,
                                     accum_out=sv[bu][:, 5:6]).then_inc(S("act3"), 1)
                        rstd_smalls(bu, 5, 6, 7, 8, "act3", 6 * u)

                    def stage_b2a(u):
                        bu = u % NB
                        a.wait_ge(S("petr"), 8 * u + 8)
                        if u >= NB:
                            a.wait_ge(S("pelg"), u - 1)
                        a.copy(xn2f[bu][:], tr8[:]).then_inc(S("aev"), 1)

                    def stage_b2b(u):
                        bu = u % NB
                        a.wait_ge(S("dvemx"), 2 * u + 2)
                        if u >= NB:
                            a.wait_ge(S("dvpr"), 8 * (u - NB) + 5)
                        a.activation(e8_t[bu][:], lgb[:, 8 * (u % 2):8 * (u % 2) + 8], AF.Exp,
                                     bias=sv[bu][:, 10:11], scale=1.0).then_inc(S("actsm"), 1)

                    a.wait_ge(S("sX"), n_sX[0])
                    a.wait_ge(S("sGp"), 1)
                    zsrc = bass.AP(zbig, 0, [[1024, 128], [0, 4], [1, 1024]])
                    for i in range(NE * SA2A // 512):
                        for dst in (snd_d, ysnd_d):
                            a.dma_start(
                                dst.rearrange("(n a p) f -> n p a f", a=4, p=128)[i],
                                zsrc).then_inc(S("zsnd"), 16)
                    a.dma_start(yrcv_d[NE * SA2A:NE * SA2A + 128, :],
                                zbig[:]).then_inc(S("zsnd"), 16)
                    for t in range(NT + 2):
                        wseg("scalar", a, SEG(2 * t))
                        if t < NT:
                            stage_a1(t)
                        if 1 <= t <= NT:
                            stage_b1(t - 1)
                        if t < NT:
                            stage_a2(t)
                        if t >= 2:
                            stage_b2a(t - 2)
                            stage_b2b(t - 2)
                        wseg("scalar", a, SEG(2 * t + 1))

                @blk.vector
                def _(v):
                    v.wait_ge(S("sX"), n_sX[0])
                    def stage_a(t):
                        b = t % NB
                        v.wait_ge(S("act1"), 6 * t + 6)
                        v.tensor_scalar(sv[b][:, 4:5], sv[b][:, 0:1], sv[b][:, 3:4],
                                        -1.0 / D, ALU.mult, ALU.mult).then_inc(S("dvexn"), 1)
                        v.wait_ge(S("dvexn"), 2 * t + 1)
                        if t >= NB:
                            v.wait_ge(S("acta"), 3 * (t - NB) + 3)
                        v.tensor_scalar(xn_t[b][:], xin[b][:], sv[b][:, 3:4],
                                        sv[b][:, 4:5], ALU.mult,
                                        ALU.add).then_inc(S("dvexn"), 1)
                        if t == NT - 1:
                            v.wait_ge(S("dvexn"), 2 * t + 2)
                            v.wait_ge(S("w"), W_TOTAL)
                            v.scalar_tensor_tensor(pre31[:], winfull[:], wap_sb[:],
                                                   xn_t[b][:], ALU.mult,
                                                   ALU.add).then_inc(S("w31"), 1)
                        v.wait_ge(S("acta"), 3 * t + 3)
                        v.wait_ge(S("w"), 8)
                        if t == NT - 1:
                            v.wait_ge(S("x31"), 1)
                            v.tensor_scalar(rmur2[b][:, 1:2], rmur2[b][:, 2:3],
                                            rmur2[b][:, 0:1], -1.0 / D, ALU.mult,
                                            ALU.mult).then_inc(S("w31"), 1)
                            v.wait_ge(S("w31"), 2)
                            v.tensor_scalar(xn_t[b][:], pre31[:], rmur2[b][:, 0:1],
                                            rmur2[b][:, 1:2], ALU.mult,
                                            ALU.add).then_inc(S("dvean"), 1)
                        else:
                            v.tensor_scalar(xn_t[b][:], xn_t[b][:], rmur2[b][:, 0:1],
                                            None, ALU.mult).then_inc(S("dvean"), 1)
                        v.wait_ge(S("dvean"), 3 * t + 1)
                        v.tensor_tensor(xn_t[b][:], xn_t[b][:], gam_bc[:],
                                        ALU.mult).then_inc(S("dvean"), 1)
                        v.wait_ge(S("dvean"), 3 * t + 2)
                        v.tensor_tensor(xn_t[b][:], xn_t[b][:], bet_bc[:],
                                        ALU.add).then_inc(S("dvean"), 1)
                        v.wait_ge(S("dvean"), 3 * t + 3)
                        if t >= NB:
                            v.wait_ge(S("act3"), 6 * (t - NB) + 2)
                            v.wait_ge(S(f"x1st{(t - NB) % 4}"),
                                      16 * ((t - NB) // 4 + 1))
                        v.tensor_tensor(x1_t[b][:], xin[b][:], xn_t[b][:],
                                        ALU.add).then_inc(S("dvgb"), 1)

                    def stage_b1(u):
                        bu = u % NB
                        v.wait_ge(S("act3"), 6 * u + 6)
                        v.tensor_scalar(sv[bu][:, 9:10], sv[bu][:, 5:6], sv[bu][:, 8:9],
                                        -1.0 / D, ALU.mult, ALU.mult).then_inc(S("dvexn2"), 1)
                        v.wait_ge(S("dvexn2"), 2 * u + 1)
                        if u >= NB:
                            v.wait_ge(S("dvxb"), u - NB + 1)
                        v.tensor_scalar(xn2_t[bu][:], x1_t[bu][:], sv[bu][:, 8:9],
                                        sv[bu][:, 9:10], ALU.mult,
                                        ALU.add).then_inc(S("dvexn2"), 1)

                    def stage_b2(t):
                        b = t % NB
                        v.wait_ge(S("pelg"), t + 1)
                        if t >= NB:
                            v.wait_ge(S("actsm"), t - NB + 1)
                        v.reduce_max(sv[b][:, 10:11], lgb[:, 8 * (t % 2):8 * (t % 2) + 8], AX.X).then_inc(S("dvemx"), 1)
                        v.wait_ge(S("dvemx"), 2 * t + 1)
                        v.tensor_scalar(sv[b][:, 10:11], sv[b][:, 10:11], -1.0,
                                        None, ALU.mult).then_inc(S("dvemx"), 1)
                        v.wait_ge(S("actsm"), t + 1)
                        v.tensor_reduce(sv[b][:, 11:12], e8_t[b][:], AX.X,
                                        ALU.add).then_inc(S("dvpr"), 1)
                        v.wait_ge(S("dvpr"), 8 * t + 1)
                        v.reciprocal(sv[b][:, 12:13], sv[b][:, 11:12]).then_inc(S("dvpr"), 1)
                        v.wait_ge(S("dvpr"), 8 * t + 2)
                        v.reduce_max(sv[b][:, 15:16], e8_t[b][:], AX.X).then_inc(S("dvpr"), 1)
                        v.wait_ge(S("dvpr"), 8 * t + 3)
                        v.tensor_scalar(sv[b][:, 13:14], sv[b][:, 15:16], sv[b][:, 12:13],
                                        None, ALU.mult).then_inc(S("dvpr"), 1)
                        v.tensor_scalar(cnd_t[b][:], e8_t[b][:], sv[b][:, 15:16],
                                        None, ALU.is_ge).then_inc(S("dvpr"), 1)
                        v.wait_ge(S("dvpr"), 8 * t + 5)
                        v.scalar_tensor_tensor(cnd_t[b][:], cnd_t[b][:], -16.0,
                                               io8p16[:], ALU.mult,
                                               ALU.add).then_inc(S("dvpr"), 1)
                        v.wait_ge(S("dvpr"), 8 * t + 6)
                        v.tensor_reduce(sv[b][:, 14:15], cnd_t[b][:], AX.X,
                                        ALU.min).then_inc(S("dvpr"), 1)
                        v.wait_ge(S("dvpr"), 8 * t + 7)
                        if t >= NB:
                            v.wait_ge(S("petot"), t - NB + 1)
                            v.wait_ge(S("pecnt"), t - NB + 1)
                        v.tensor_scalar(oh_t[b][:], io8_sb[:], sv[b][:, 14:15],
                                        None, ALU.is_equal).then_inc(S("dvpr"), 1)
                        v.wait_ge(S("dvexn2"), 2 * t + 2)
                        if t >= NXB:
                            v.wait_ge(S(f"poolsc{(t - NXB) % 8}"),
                                      16 * ((t - NXB) // 8 + 1))
                        v.tensor_scalar(xb_t[t % NXB][:], xn2_t[b][:], sv[b][:, 13:14],
                                        None, ALU.mult).then_inc(S("dvxb"), 1)

                    def stage_b3(t):
                        b = t % NB
                        v.wait_ge(S("pecnt"), t + 1)
                        v.wait_ge(S("petot"), t + 1)
                        v.tensor_tensor(rkm_t[b][:], cntt[:, 8 * (t % 2):8 * (t % 2) + 8], oh_t[b][:],
                                        ALU.mult).then_inc(S("dverk"), 1)
                        v.wait_ge(S("dverk"), 8 * t + 1)
                        v.tensor_reduce(sl_t[b][:, 0:1], rkm_t[b][:], AX.X,
                                        ALU.add).then_inc(S("dverk"), 1)
                        v.wait_ge(S("dverk"), 8 * t + 2)
                        v.scalar_tensor_tensor(sl_t[b][:, 1:2], sv[b][:, 14:15],
                                               float(SA2A), sl_t[b][:, 0:1],
                                               ALU.mult, ALU.add).then_inc(S("dverk"), 1)
                        v.wait_ge(S("dverk"), 8 * t + 3)
                        v.tensor_scalar(sl_t[b][:, 2:3], sl_t[b][:, 0:1], float(SA2A),
                                        None, ALU.is_ge).then_inc(S("dverk"), 1)
                        v.wait_ge(S("dverk"), 8 * t + 4)
                        v.scalar_tensor_tensor(sl_t[b][:, 3:4], sl_t[b][:, 2:3], SENT,
                                               sl_t[b][:, 1:2], ALU.mult,
                                               ALU.add).then_inc(S("dverk"), 1)
                        v.wait_ge(S("dverk"), 8 * t + 5)
                        v.tensor_copy(slot_sb[:, t:t + 1], sl_t[b][:, 3:4]).then_inc(S("dverk"), 1)
                        v.tensor_scalar(sl_t[b][:, 2:3], sl_t[b][:, 3:4],
                                        float(NE * SA2A), None,
                                        ALU.min).then_inc(S("dverk"), 1)
                        v.wait_ge(S("dverk"), 8 * t + 7)
                        v.tensor_copy(fslot_sb[:, t:t + 1], sl_t[b][:, 2:3]).then_inc(S("dverk"), 1)
                        v.wait_ge(S("petot"), t + 1)
                        v.tensor_copy(tsl[:], cntt[0:1, 256 + 8 * t:256 + 8 * t + 8]).then_inc(S("dvbr"), 1)
                        v.wait_ge(S("dvbr"), 2 * t + 1)
                        v.wait_ge(S("pecnt"), t + 1)
                        v.tensor_tensor(brun[:], brun[:], tsl[:],
                                        ALU.add).then_inc(S("dvbr"), 1)

                    for t in range(NT + 3):
                        wseg("vector", v, SEG(2 * t))
                        if t < NT:
                            stage_a(t)
                        if 1 <= t <= NT:
                            stage_b1(t - 1)
                        if 2 <= t <= NT + 1:
                            stage_b2(t - 2)
                        wseg("vector", v, SEG(2 * t + 1))
                        if t >= 3:
                            stage_b3(t - 3)

                @blk.tensor
                def _(t_eng):
                    t_eng.wait_ge(S("sX"), n_sX[0])
                    for it in range(NT + 2):
                        wseg("tensor", t_eng, SEG(2 * it + 2))
                        if 1 <= it <= NT:
                            v2 = it - 1
                            bv = v2 % NB
                            t_eng.wait_ge(S("aev"), v2 + 1)
                            if v2 >= 2:
                                t_eng.wait_ge(S("actsm"), v2 - 1)
                                t_eng.wait_ge(S("dvemx"), 2 * (v2 - 1))
                            h = None
                            for c in range(8):
                                h = t_eng.matmul(lgb[:, 8 * (v2 % 2):8 * (v2 % 2) + 8], xn2f[bv][:, c, :],
                                                 wr_sb[:, c, :],
                                                 start=(c == 0), stop=(c == 7),
                                                 skip_group_check=True)
                            h.then_inc(S("pelg"), 1)
                        if it < NT:
                            t = it
                            b = t % NB
                            for c in range(8):
                                if c == 0:
                                    t_eng.wait_ge(S("dvexn2"), 2 * t + 2)
                                    if t >= 1:
                                        t_eng.wait_ge(S("aev"), t)
                                t_eng.transpose(tr8[:, c, :],
                                                xn2_t[b][:, 128 * c:128 * c + 128],
                                                id_f[:]).then_inc(S("petr"), 1)
                        if it >= 2:
                            u = it - 2
                            bu = u % NB
                            t_eng.wait_ge(S("dvpr"), 8 * u + 8)
                            t_eng.wait_ge(S("dvbr"), 2 * u)
                            if u >= 2:
                                t_eng.wait_ge(S("dverk"), 8 * (u - 2) + 1)
                            t_eng.matmul(cntt[:, 8 * (u % 2):8 * (u % 2) + 8], lt_sb[:], oh_t[bu][:],
                                         start=True, stop=False,
                                         skip_group_check=True)
                            t_eng.matmul(cntt[:, 8 * (u % 2):8 * (u % 2) + 8], on1_sb[:], brun[:],
                                         start=False, stop=True,
                                         skip_group_check=True).then_inc(S("pecnt"), 1)
                            t_eng.matmul(cntt[0:1, 256 + 8 * u:256 + 8 * u + 8], onesc[:], oh_t[bu][:],
                                         start=True, stop=True,
                                         skip_group_check=True).then_inc(S("petot"), 1)

                @blk.gpsimd
                def _(g):
                    with g.register("bc_disp") as bc_disp:
                        g.reg_mov(bc_disp, NE * SA2A - 1)
                        g.wait_ge(S("w"), wmarks[0])
                        g.dma_start(wkpb[:],
                                    wkp.rearrange("(c p) f -> p c f", p=128)
                                    ).then_inc(S("wdx0"), 16)
                        g.wait_ge(S("w"), wmarks[1])
                        g.dma_start(wvb[:],
                                    wv.rearrange("(c p) f -> p c f", p=128)
                                    ).then_inc(S("wdx1"), 16)
                        g.wait_ge(S("w"), wmarks[2])
                        g.dma_start(wob[:],
                                    wo.rearrange("(c p) f -> p c f", p=128)
                                    ).then_inc(S("wdx2"), 16)
                        g.wait_ge(S("sGd"), n_sGd[0])
                        g.wait_ge(S("zsnd"), 16 * (2 * (NE * SA2A // 512) + 1))
                        for t in range(NT):
                            g.wait_ge(S("dverk"), 8 * t + 6)
                            g.wait_ge(S("dvxb"), t + 1)
                            g.indirect_dma_start(snd_d, IOA(slot_sb[:, t:t + 1], 0),
                                                 xb_t[t % NXB][:], None,
                                                 bounds_check=bc_disp,
                                                 oob_is_err=False).then_inc(S(f"poolsc{t % 8}"), 16)
                        g.wait_ge(S("w"), W_TOTAL)
                        for ss in range(8):
                            g.dma_start(w1b_r[:, :, 512 * ss:512 * ss + 512],
                                        w1my[:, 512 * ss:512 * ss + 512]
                                        .rearrange("(c p) f -> p c f", p=128)
                                        ).then_inc(S("wdm"), 16)
                        for qq in range(4):
                            g.dma_start(w2b_r[:, 8 * qq:8 * qq + 8, :],
                                        w2my[1024 * qq:1024 * qq + 1024, :]
                                        .rearrange("(c p) f -> p c f", p=128)
                                        ).then_inc(S("wdm"), 16)

            P = Script(S("post"), S("postd"))
            P.add("vector", lambda v: v.tensor_copy(tot_sb[:], brun[:]))
            P.add("gpsimd", lambda g: g.dma_start(cc_in.ap()[0:1, 0:8], tot_sb[:]),
                  inc=16)
            P.add("gpsimd", lambda g: g.collective_compute(
                "AllGather", ALU.bypass, replica_groups=[list(range(8))],
                ins=[cc_in.ap().opt()], outs=[cc_out.ap().opt()]))
            P.add("gpsimd", lambda g: g.dma_start(cnt_all[:], cc_out.ap()[0:8, 0:8]),
                  inc=16)

            def _a2a1(g):
                with allow_ici_all_to_all():
                    return g.collective_compute(
                        "AllToAll", ALU.bypass, replica_groups=[list(range(8))],
                        ins=[snd_d.opt()], outs=[rcv_d.opt()])
            P.add("gpsimd", _a2a1)
            P.add("tensor", lambda t: t.matmul(cntt[0:8, 0:8], lt_sb[0:8, 0:8],
                                               cnt_all[:], start=True, stop=True,
                                               skip_group_check=True))
            P.add("vector", lambda v: v.tensor_copy(basel[:], cntt[0:8, 0:8]))
            P.add("vector", lambda v: v.tensor_tensor(basel[:], basel[:], myx_sb[:],
                                                      ALU.mult))
            P.add("vector", lambda v: v.tensor_reduce(bcol[:], basel[:], AX.X,
                                                      ALU.add))
            P.add("tensor", lambda t: t.transpose(cntt[0:1, 8:16], bcol[:],
                                                  id_f[0:8, 0:8]))
            P.add("vector", lambda v: v.tensor_copy(brow[:], cntt[0:1, 8:16]))
            P.add("tensor", lambda t: t.matmul(cntt[:, 0:8], on1_sb[:], brow[:],
                                               start=True, stop=True,
                                               skip_group_check=True))
            P.add("vector", lambda v: v.tensor_copy(bases_bc[:], cntt[:, 0:8]))
            P_TOTAL = P.total()

            with nc.Block() as blk:
                @blk.vector
                def _(v):
                    v.wait_ge(S("dvbr"), 2 * NT)
                    P.emit("vector", v)

                @blk.tensor
                def _(t):
                    P.emit("tensor", t)

                @blk.gpsimd
                def _(g):
                    g.wait_ge(S("sGd"), n_sGd[0])
                    for pp in range(8):
                        g.wait_ge(S(f"poolsc{pp}"), 16 * (NT // 8))
                    P.emit("gpsimd", g)

            pctx.close()

        if do_c:
            with contextlib.ExitStack() as cctx:
                Ce = cctx.enter_context
                CM, _coff = _man_alloc(sweep_base)
                xe_t = [CM(f"xe{i}", [128, D], BF16) for i in range(2)]
                xeT = [CM(f"xeT{i}", [128, 8, 512], BF16) for i in range(2)]
                h_sb = CM("h_sb", [128, 32, 512], BF16)
                ysnd_t = [CM(f"ysnd{i}", [128, D], BF16) for i in range(4)]
                cj_t = CM("cj_t", [128, 8], F32)
                cmp8_t = CM("cmp8_t", [128, 8], F32)
                bm8_t = CM("bm8_t", [128, 8], F32)
                assert _coff[0] <= window_base, (_coff[0], window_base)
                ht_ps = [Ce(nc.psum_tensor(f"ht_ps{i}", [128, 128], BF16)) for i in range(2)]
                h_ps = [Ce(nc.psum_tensor(f"h_ps{i}", [128, 512], F32)) for i in range(2)]
                y_ps = [[Ce(nc.psum_tensor(f"y_ps{i}_{j}", [128, 512], F32))
                         for j in range(2)] for i in range(2)]

                NHALF = CAP // 512
                NGRP = CAP // 128

                def _sweep_dma_gate(eng):
                    for pp in range(4):
                        eng.wait_ge(S(f"x1st{pp}"), 16 * (NT // 4))
                        eng.wait_ge(S(f"xld{pp}"), 16 * (NT // 4))
                    for pp in range(8):
                        eng.wait_ge(S(f"poolsc{pp}"), 16 * (NT // 8))

                with nc.Block() as blk:
                    @blk.vector
                    def _(v):
                        _sweep_dma_gate(v)
                        v.wait_ge(S("post"), P_TOTAL)
                        for g in range(NGRP):
                            b8 = 8 * g
                            v.tensor_scalar(cj_t[:, 0:1], pidx[:], float(128 * g),
                                            None, ALU.add).then_inc(S("cidx"), 1)
                            v.wait_ge(S("cidx"), b8 + 1)
                            v.tensor_scalar(cmp8_t[:], bases_bc[:], cj_t[:, 0:1],
                                            None, ALU.is_le).then_inc(S("cidx"), 1)
                            v.wait_ge(S("cidx"), b8 + 2)
                            v.tensor_reduce(cj_t[:, 1:2], cmp8_t[:], AX.X,
                                            ALU.add).then_inc(S("cidx"), 1)
                            v.wait_ge(S("cidx"), b8 + 3)
                            v.tensor_tensor(bm8_t[:], bases_bc[:], cmp8_t[:],
                                            ALU.mult).then_inc(S("cidx"), 1)
                            v.wait_ge(S("cidx"), b8 + 4)
                            v.tensor_reduce(cj_t[:, 2:3], bm8_t[:], AX.X,
                                            ALU.max).then_inc(S("cidx"), 1)
                            v.wait_ge(S("cidx"), b8 + 5)
                            v.scalar_tensor_tensor(cj_t[:, 3:4], cj_t[:, 1:2],
                                                   float(SA2A), cj_t[:, 2:3],
                                                   ALU.mult,
                                                   ALU.subtract).then_inc(S("cidx"), 1)
                            v.wait_ge(S("cidx"), b8 + 6)
                            v.tensor_scalar(cj_t[:, 3:4], cj_t[:, 3:4], cj_t[:, 0:1],
                                            -float(SA2A), ALU.add,
                                            ALU.add).then_inc(S("cidx"), 1)
                            v.wait_ge(S("cidx"), b8 + 7)
                            v.tensor_copy(srcq_sb[:, g:g + 1],
                                          cj_t[:, 3:4]).then_inc(S("cidx"), 1)

                    @blk.scalar
                    def _(a):
                        _sweep_dma_gate(a)
                        for hq in range(NHALF):
                            for k in range(32):
                                kk = 32 * hq + k
                                a.wait_ge(S("chm"), kk + 1)
                                s_, fc4 = divmod(k, 4)
                                fc = 4 * s_ + fc4
                                a.activation(h_sb[:, fc, :], h_ps[kk % 2][:],
                                             AF.Relu, bias=0.0,
                                             scale=1.0).then_inc(S("cha"), 1)

                    @blk.tensor
                    def _(t):
                        def emit_tr(hq):
                            if hq >= 2:
                                t.wait_ge(S("chm"), 32 * (hq - 1))
                            for gg in range(4):
                                gi = 4 * hq + gg
                                t.wait_ge(S(f"cg{gi % 2}"), 16 * (gi // 2 + 1))
                                for c in range(8):
                                    j = 8 * gi + c
                                    if j >= 2:
                                        t.wait_ge(S("cxv"), j - 1)
                                    t.transpose(ht_ps[j % 2][:],
                                                xe_t[gi % 2][:, 128 * c:128 * c + 128],
                                                id_bf[:]).then_inc(S("cxt"), 1)
                        _sweep_dma_gate(t)
                        t.wait_ge(S("wdm"), 16 * 8)
                        emit_tr(0)
                        for hq in range(NHALF):
                            if hq == 0:
                                pass
                            t.wait_ge(S("cxv"), 32 * (hq + 1))
                            for s_ in range(8):
                                for fc4 in range(4):
                                    k = 4 * s_ + fc4
                                    kk = 32 * hq + k
                                    if kk >= 2:
                                        t.wait_ge(S("cha"), kk - 1)
                                    h = None
                                    for dc in range(8):
                                        h = t.matmul(
                                            h_ps[kk % 2][:],
                                            w1b_r[:, dc, 512 * s_ + 128 * fc4:
                                                  512 * s_ + 128 * fc4 + 128],
                                            xeT[hq % 2][:, dc, :],
                                            start=(dc == 0), stop=(dc == 7))
                                    h.then_inc(S("chm"), 1)
                            if hq + 1 < NHALF:
                                emit_tr(hq + 1)
                            if hq == 0:
                                t.wait_ge(S("wdm"), 16 * 12)
                            t.wait_ge(S("cha"), 32 * (hq + 1))
                            for ts in range(4):
                                tsg = 4 * hq + ts
                                for dh in range(2):
                                    if tsg >= 2:
                                        t.wait_ge(S("cyv"), 2 * (tsg - 2) + dh + 1)
                                    h = None
                                    for fc in range(32):
                                        h = t.matmul(
                                            y_ps[tsg % 2][dh][:],
                                            h_sb[:, fc, 128 * ts:128 * ts + 128],
                                            w2b_r[:, fc, 512 * dh:512 * dh + 512],
                                            start=(fc == 0), stop=(fc == 31))
                                    h.then_inc(S("cym"), 1)

                    @blk.vector
                    def _(v):
                        for hq in range(NHALF + 1):
                            if hq < NHALF:
                                for gg in range(6):
                                    if gg >= 2:
                                        gi = 4 * hq + gg - 2
                                        for c in range(8):
                                            j = 8 * gi + c
                                            v.wait_ge(S("cxt"), j + 1)
                                            if hq >= 2 and gg == 2 and c == 0:
                                                v.wait_ge(S("chm"), 32 * (hq - 1))
                                            v.tensor_copy(
                                                xeT[hq % 2][:, c, 128 * (gg - 2):128 * (gg - 2) + 128],
                                                ht_ps[j % 2][:]).then_inc(S("cxv"), 1)
                                    if gg < 4:
                                        gi = 4 * hq + gg
                                        if gi >= 2:
                                            v.wait_ge(S("cxt"), 8 * (gi - 1))
                                        v.memset(xe_t[gi % 2][:], 0.0).then_inc(S("cme"), 1)
                            if hq >= 1:
                                for ts in range(4):
                                    tsg = 4 * (hq - 1) + ts
                                    for dh in range(2):
                                        f = 2 * tsg + dh
                                        v.wait_ge(S("cym"), f + 1)
                                        if tsg >= 4 and dh == 0:
                                            v.wait_ge(S(f"cys{tsg % 4}"),
                                                      16 * (tsg // 4))
                                        v.tensor_copy(
                                            ysnd_t[tsg % 4][:, 512 * dh:512 * dh + 512],
                                            y_ps[tsg % 2][dh][:]).then_inc(S("cyv"), 1)

                    @blk.gpsimd
                    def _(g):
                        with g.register("bc_src") as bc_src:
                            g.reg_mov(bc_src, NE * SA2A - 1)
                            g.wait_ge(S("post"), P_TOTAL)
                            for hq in range(NHALF + 1):
                                if hq < NHALF:
                                    for gg in range(4):
                                        gi = 4 * hq + gg
                                        g.wait_ge(S("cme"), gi + 1)
                                        g.wait_ge(S("cidx"), 8 * (gi + 1))
                                        g.indirect_dma_start(
                                            xe_t[gi % 2][:], None, rcv_d,
                                            IOA(srcq_sb[:, gi:gi + 1], 0),
                                            bounds_check=bc_src,
                                            oob_is_err=False).then_inc(S(f"cg{gi % 2}"), 16)
                                if hq >= 1:
                                    for ts in range(4):
                                        tsg = 4 * (hq - 1) + ts
                                        g.wait_ge(S("cyv"), 2 * tsg + 2)
                                        g.indirect_dma_start(
                                            ysnd_d, IOA(srcq_sb[:, tsg:tsg + 1], 0),
                                            ysnd_t[tsg % 4][:], None,
                                            bounds_check=bc_src,
                                            oob_is_err=False).then_inc(S(f"cys{tsg % 4}"), 16)
                            for pp in range(4):
                                g.wait_ge(S(f"cys{pp}"), 16 * 8)
                            with allow_ici_all_to_all():
                                g.collective_compute(
                                    "AllToAll", ALU.bypass,
                                    replica_groups=[list(range(8))],
                                    ins=[ysnd_d.opt()],
                                    outs=[yrcv_d[0:NE * SA2A, :].opt()]).then_inc(S("caa"), 1)

            with contextlib.ExitStack() as c2ctx:
                CM2, _c2off = _man_alloc(sweep_base)
                ot_t = [CM2(f"ot{i}", [128, D], F32) for i in range(4)]
                yg_t = [CM2(f"yg{i}", [128, D], BF16) for i in range(4)]
                fin_t = [CM2(f"fint{i}", [128, D], F32) for i in range(4)]
                assert _c2off[0] <= window_base, (_c2off[0], window_base)
                with nc.Block() as blk:
                    @blk.gpsimd
                    def _(g):
                        with g.register("bc_f") as bc_f:
                            g.reg_mov(bc_f, NE * SA2A + 127)
                            g.wait_ge(S("caa"), 1)
                            for t2 in range(NT):
                                if t2 >= 4:
                                    g.wait_ge(S("ffa"), t2 - 3)
                                g.indirect_dma_start(
                                    yg_t[t2 % 4][:], None, yrcv_d,
                                    IOA(fslot_sb[:, t2:t2 + 1], 0),
                                    bounds_check=bc_f,
                                    oob_is_err=False).then_inc(S(f"fyg{t2 % 4}"), 16)

                    @blk.sync
                    def _(sy):
                        for pp in range(2):
                            sy.wait_ge(S(f"cg{pp}"), 16 * 16)
                        for pp in range(4):
                            sy.wait_ge(S(f"cys{pp}"), 16 * 8)
                        for pp in range(4):
                            sy.wait_ge(S(f"x1st{pp}"), 16 * (NT // 4))
                        for t2 in range(NT):
                            if t2 >= 4:
                                sy.wait_ge(S("ffa"), t2 - 3)
                            sy.dma_start(ot_t[t2 % 4][:],
                                         otile[t2]).then_inc(S(f"fld{t2 % 4}"), 16)
                            if t2 >= 2:
                                t3 = t2 - 2
                                sy.wait_ge(S("ffa"), t3 + 1)
                                sy.dma_start(otile[t3],
                                             fin_t[t3 % 4][:]).then_inc(S(f"fo{t3 % 4}"), 16)
                        for t3 in (NT - 2, NT - 1):
                            sy.wait_ge(S("ffa"), t3 + 1)
                            sy.dma_start(otile[t3],
                                         fin_t[t3 % 4][:]).then_inc(S(f"fo{t3 % 4}"), 16)

                    @blk.vector
                    def _(v):
                        for pp in range(2):
                            v.wait_ge(S(f"cg{pp}"), 16 * 16)
                        for pp in range(4):
                            v.wait_ge(S(f"cys{pp}"), 16 * 8)
                        for t2 in range(NT):
                            v.wait_ge(S(f"fld{t2 % 4}"), 16 * (t2 // 4 + 1))
                            v.wait_ge(S(f"fyg{t2 % 4}"), 16 * (t2 // 4 + 1))
                            if t2 >= 4:
                                v.wait_ge(S(f"fo{t2 % 4}"), 16 * (t2 // 4))
                            v.tensor_tensor(fin_t[t2 % 4][:], ot_t[t2 % 4][:],
                                            yg_t[t2 % 4][:],
                                            ALU.add).then_inc(S("ffa"), 1)

        with nc.Block() as blk:
            @blk.sync
            def _(sy):
                for pp in range(4):
                    sy.wait_ge(S(f"x1st{pp}"), 16 * (NT // 4))
                for pp in range(8):
                    sy.wait_ge(S(f"poolsc{pp}"), 16 * (NT // 8))
                if do_c:
                    for pp in range(4):
                        sy.wait_ge(S(f"fo{pp}"), 16 * (NT // 4))
                if debug:
                    sy.dma_start(dbg_sml[:, 0:8], bases_bc[:]).then_inc(S("fin"), 16)
                    sy.dma_start(dbg_sml[0:8, 8:16], cnt_all[:]).then_inc(S("fin"), 16)
                    sy.dma_start(dbg_sml[0:1, 16:24], tot_sb[:]).then_inc(S("fin"), 16)
                    sy.dma_start(dbg_sml[0:1, 24:32],
                                 brun[0:1, 0:8]).then_inc(S("fin"), 16)
                    sy.dma_start(dbg_win, winfull[:]).then_inc(S("fin"), 16)
                    sy.dma_start(dbg2, dbgt[:]).then_inc(S("fin"), 16)
                    sy.wait_ge(S("fin"), 96)

    return nc


_NCACHE = {}


def _consts():
    idbf = np.eye(128, dtype=ml_dtypes.bfloat16)
    idf32 = np.eye(128, dtype=np.float32)
    lt = (np.arange(128)[:, None] < np.arange(128)[None, :]).astype(np.float32)
    io8 = np.broadcast_to(np.arange(8, dtype=np.float32), (128, 8)).copy()
    on1 = np.ones((1, 128), dtype=np.float32)
    s = np.arange(32)[:, None, None]
    j = np.arange(32)[None, :, None]
    k = np.arange(32)[None, None, :]
    wm = np.where((j <= s) & (k <= s), 0.0, -1.0e30).astype(np.float32).reshape(32, 1024)
    rrep = (np.arange(32)[:, None] == (np.arange(128)[None, :] % 32)).astype(np.float32)
    rjs = np.zeros((32, 8, 128), dtype=np.float32)
    for c in range(8):
        rjs[:, c, :] = (np.arange(32)[:, None] == (4 * c + np.arange(128)[None, :] // 32))
    return dict(idbf=idbf, idf32=idf32, ltm=lt, iota8=io8,
                iom9c=(io8 - 9.0).astype(np.float32),
                io8p16c=(io8 + 16.0).astype(np.float32), ones1=on1, wmask=wm,
                rrep=rrep, rj=rjs.reshape(32, 1024).astype(np.float32))


def kernel(L0_attn_x, L0_attn_W_Q, L0_attn_W_K, L0_attn_W_V, L0_attn_W_Kp,
           L0_attn_W_O, L0_attn_ln_gamma, L0_attn_ln_beta, L0_ffn_Wr,
           L0_ffn_W1, L0_ffn_W2, _debug=False, _do_c=True, _c_stage=3):
    x = np.ascontiguousarray(np.asarray(L0_attn_x, dtype=np.float32))
    B, Sq, _ = x.shape
    xf = x.reshape(B * Sq, D)
    shared = dict(
        wq=np.ascontiguousarray(np.asarray(L0_attn_W_Q, np.float32)),
        wk=np.ascontiguousarray(np.asarray(L0_attn_W_K, np.float32)),
        wkp=np.ascontiguousarray(np.asarray(L0_attn_W_Kp, np.float32)),
        wv=np.ascontiguousarray(np.asarray(L0_attn_W_V, np.float32)),
        wo=np.ascontiguousarray(np.asarray(L0_attn_W_O, np.float32)),
        gam=np.ascontiguousarray(np.asarray(L0_attn_ln_gamma, np.float32)).reshape(1, D),
        bet=np.ascontiguousarray(np.asarray(L0_attn_ln_beta, np.float32)).reshape(1, D),
        wr=np.ascontiguousarray(np.asarray(L0_ffn_Wr, np.float32)),
        **_consts(),
    )
    w1a = np.asarray(L0_ffn_W1, np.float32)
    w2a = np.asarray(L0_ffn_W2, np.float32)
    in_maps = []
    for c in range(8):
        winap = np.zeros((128, 1), np.float32)
        if c % 2 == 1:
            winap[96:128] = 1.0
        m = dict(shared)
        m["xsh"] = np.ascontiguousarray(xf[4096 * c:4096 * (c + 1)])
        m["winx"] = np.ascontiguousarray(x[c // 2, Sq - 32:Sq, :])
        m["winap"] = winap
        m["coresel"] = (np.arange(8) < c).astype(np.float32).reshape(8, 1)
        m["w1my"] = np.ascontiguousarray(w1a[c])
        m["w2my"] = np.ascontiguousarray(w2a[c])
        m["myexp"] = np.broadcast_to(
            (np.arange(8) == c).astype(np.float32), (8, 8)).copy()
        in_maps.append(m)

    key = (_debug, _do_c, _c_stage)
    if key not in _NCACHE:
        _NCACHE[key] = build_nc(debug=_debug, do_c=_do_c, c_stage=_c_stage)
    nc = _NCACHE[key]
    res = run_bass_kernel_spmd(nc, in_maps, core_ids=list(range(8)))
    kernel._last_results = res.results
    outs = np.concatenate([np.asarray(r["out"]) for r in res.results], axis=0)
    return outs.reshape(B, Sq, D).astype(np.float32)


# version 18
# speedup vs baseline: 1.0004x; 1.0004x over previous
import contextlib
import numpy as np
import ml_dtypes
import concourse.bass as bass
import concourse.mybir as mybir
from concourse.bass_utils import run_bass_kernel_spmd
from concourse.aot_env import allow_ici_all_to_all

F32 = mybir.dt.float32
BF16 = mybir.dt.bfloat16
U32 = mybir.dt.uint32
AF = mybir.ActivationFunctionType
ALU = mybir.AluOpType
AX = mybir.AxisListType
IOA = bass.IndirectOffsetOnAxis

NT = 32
D = 1024
NE = 8
SA2A = 1152
CAP = 4096
EPS_LN = 1e-5
SENT = float(2 ** 16)


class Script:

    def __init__(self, sem, dsem=None):
        self.sem = sem
        self.dsem = dsem
        self.ops = []

    def add(self, eng, fn, inc=1):
        self.ops.append((eng, fn, inc))

    def total(self):
        return sum(i for (_, _, i) in self.ops if i == 1)

    def emit(self, name, eng):
        self.emit_range(name, eng, 0, len(self.ops))

    def emit_range(self, name, eng, lo, hi):
        cum_c = 0
        cum_d = 0
        for idx, (en, fn, inc) in enumerate(self.ops):
            if en == name and lo <= idx < hi:
                if cum_c:
                    eng.wait_ge(self.sem, cum_c)
                if cum_d:
                    eng.wait_ge(self.dsem, cum_d)
                h = fn(eng)
                assert h is not None, f"script op on {name} returned no handle"
                h.then_inc(self.sem if inc == 1 else self.dsem, inc)
            if inc == 1:
                cum_c += inc
            else:
                cum_d += inc


def build_nc(debug=False, do_c=True, c_stage=3):
    nc = bass.Bass("TRN2", target_bir_lowering=False)

    xsh = nc.dram_tensor("xsh", [NT * 128, D], F32, kind="ExternalInput").ap()
    winx = nc.dram_tensor("winx", [32, D], F32, kind="ExternalInput").ap()
    winap = nc.dram_tensor("winap", [128, 1], F32, kind="ExternalInput").ap()
    coresel = nc.dram_tensor("coresel", [8, 1], F32, kind="ExternalInput").ap()
    wq = nc.dram_tensor("wq", [D, D], F32, kind="ExternalInput").ap()
    wk = nc.dram_tensor("wk", [D, D], F32, kind="ExternalInput").ap()
    wkp = nc.dram_tensor("wkp", [D, D], F32, kind="ExternalInput").ap()
    wv = nc.dram_tensor("wv", [D, D], F32, kind="ExternalInput").ap()
    wo = nc.dram_tensor("wo", [D, D], F32, kind="ExternalInput").ap()
    gam = nc.dram_tensor("gam", [1, D], F32, kind="ExternalInput").ap()
    bet = nc.dram_tensor("bet", [1, D], F32, kind="ExternalInput").ap()
    wr = nc.dram_tensor("wr", [D, 8], F32, kind="ExternalInput").ap()
    w1my = nc.dram_tensor("w1my", [D, 4096], F32, kind="ExternalInput").ap()
    w2my = nc.dram_tensor("w2my", [4096, D], F32, kind="ExternalInput").ap()
    myexp = nc.dram_tensor("myexp", [8, 8], F32, kind="ExternalInput").ap()
    idbf = nc.dram_tensor("idbf", [128, 128], BF16, kind="ExternalInput").ap()
    idf32 = nc.dram_tensor("idf32", [128, 128], F32, kind="ExternalInput").ap()
    ltm = nc.dram_tensor("ltm", [128, 128], F32, kind="ExternalInput").ap()
    iota8 = nc.dram_tensor("iota8", [128, 8], F32, kind="ExternalInput").ap()
    iom9c = nc.dram_tensor("iom9c", [128, 8], F32, kind="ExternalInput").ap()
    io8p16c = nc.dram_tensor("io8p16c", [128, 8], F32, kind="ExternalInput").ap()
    ones1 = nc.dram_tensor("ones1", [1, 128], F32, kind="ExternalInput").ap()
    wmask = nc.dram_tensor("wmask", [32, 1024], F32, kind="ExternalInput").ap()
    rrep = nc.dram_tensor("rrep", [32, 128], F32, kind="ExternalInput").ap()
    rj = nc.dram_tensor("rj", [32, 1024], F32, kind="ExternalInput").ap()

    out = nc.dram_tensor("out", [NT * 128, D], F32, kind="ExternalOutput").ap()
    dbg = dict(kind="ExternalOutput") if debug else {}
    snd_d = nc.dram_tensor("snd_d", [NE * SA2A, D], BF16, **dbg).ap()
    rcv_d = nc.dram_tensor("rcv_d", [NE * SA2A, D], BF16).ap()
    ysnd_d = nc.dram_tensor("ysnd_d", [NE * SA2A, D], BF16).ap()
    yrcv_d = nc.dram_tensor("yrcv_d", [NE * SA2A + 128, D], BF16, **dbg).ap()
    cc_in = nc.dram_tensor("cc_in", [1, 128], F32)
    cc_out = nc.dram_tensor("cc_out", [8, 128], F32)
    if debug:
        dbg_sml = nc.dram_tensor("dbg_sml", [128, 32], F32, kind="ExternalOutput").ap()
        dbg2 = nc.dram_tensor("dbg2", [128, 8], F32, kind="ExternalOutput").ap()
        dbg_win = nc.dram_tensor("dbg_win", [128, D], F32, kind="ExternalOutput").ap()

    with contextlib.ExitStack() as ctx:
        E = ctx.enter_context

        id_bf = E(nc.sbuf_tensor("id_bf", [128, 128], BF16))
        id_f = E(nc.sbuf_tensor("id_f", [128, 128], F32))
        lt_sb = E(nc.sbuf_tensor("lt_sb", [128, 128], F32))
        io8_sb = E(nc.sbuf_tensor("io8_sb", [128, 8], F32))
        iom9 = E(nc.sbuf_tensor("iom9", [128, 8], F32))
        io8p16 = E(nc.sbuf_tensor("io8p16", [128, 8], F32))
        on1_sb = E(nc.sbuf_tensor("on1_sb", [1, 128], F32))
        onesc = E(nc.sbuf_tensor("onesc", [128, 1], F32))
        pidx = E(nc.sbuf_tensor("pidx", [128, 1], F32))
        eps_t = E(nc.sbuf_tensor("eps_t", [128, 1], F32))
        tiny_t = E(nc.sbuf_tensor("tiny_t", [128, 1], F32))
        gam_bc = E(nc.sbuf_tensor("gam_bc", [128, D], F32))
        bet_bc = E(nc.sbuf_tensor("bet_bc", [128, D], F32))
        wr_sb = E(nc.sbuf_tensor("wr_sb", [128, 8, 8], F32))
        wap_sb = E(nc.sbuf_tensor("wap_sb", [128, 1], F32))
        csel_sb = E(nc.sbuf_tensor("csel_sb", [8, 1], F32))
        winfull = E(nc.sbuf_tensor("winfull", [128, D], F32))
        bases_bc = E(nc.sbuf_tensor("bases_bc", [128, 8], F32))
        cnt_all = E(nc.sbuf_tensor("cnt_all", [8, 8], F32))
        basel = E(nc.sbuf_tensor("basel", [8, 8], F32))
        bcol = E(nc.sbuf_tensor("bcol", [8, 1], F32))
        brow = E(nc.sbuf_tensor("brow", [1, 8], F32))
        myx_sb = E(nc.sbuf_tensor("myx_sb", [8, 8], F32))
        tot_sb = E(nc.sbuf_tensor("tot_sb", [1, 8], F32))
        brun = E(nc.sbuf_tensor("brun", [1, 8], F32))
        tsl = E(nc.sbuf_tensor("tsl", [1, 8], F32))
        zrow = E(nc.sbuf_tensor("zrow", [1, 128], F32))
        zbig = E(nc.sbuf_tensor("zbig", [128, 1024], BF16))
        slot_sb = E(nc.sbuf_tensor("slot_sb", [128, NT], U32))
        fslot_sb = E(nc.sbuf_tensor("fslot_sb", [128, NT], U32))
        srcq_sb = E(nc.sbuf_tensor("srcq_sb", [128, 32], U32))
        dbgt = E(nc.sbuf_tensor("dbgt", [128, 8], F32))

        sem_names = [
            "w", "post", "sX", "sGd",
            "xld0", "xld1", "xld2", "xld3", "dve1", "act1", "dvexn", "w31",
            "acta", "act2", "dve2", "dvean", "dvgb", "x1st0", "x1st1",
            "x1st2", "x1st3", "act3a", "dvev3", "act3", "dvexn2", "petr",
            "aev", "pelg", "dvemx", "actsm", "dvpr", "pecnt", "petot",
            "dvbr", "dverk", "dvxb", "x31",
            "poolsc0", "poolsc1", "poolsc2", "poolsc3",
            "poolsc4", "poolsc5", "poolsc6", "poolsc7",
            "zsnd", "wdm", "wdm2", "cidx", "cg0", "cg1", "cme", "cxt", "cxv",
            "chm", "cha", "cym", "cyv", "cys0", "cys1", "cys2", "cys3", "caa",
            "fld0", "fld1", "fld2", "fld3", "fms", "fyg0", "fyg1", "fyg2",
            "fyg3", "ffa", "fo0", "fo1", "fo2", "fo3",
            "fin", "sGp", "postd", "dbgs", "wdx0", "wdx1", "wdx2",
        ]
        sems = {n: E(nc.semaphore(n)) for n in sem_names}

        def S(n):
            return sems[n]

        n_sX = [0]
        n_sGd = [0]

        def _man_alloc(base):
            off = [base]

            def alloc(name, shape, dtype):
                nbytes = 1
                for s in shape[1:]:
                    nbytes *= s
                nbytes *= 2 if dtype == BF16 else 4
                off[0] = (off[0] + 31) // 32 * 32
                h = nc.alloc_sbuf_tensor_at(name, shape, dtype, offset=off[0])
                off[0] += nbytes
                return h.ap()
            return alloc, off

        sweep_base = nc.sbuf_base
        sctx = E(contextlib.ExitStack())
        Se = sctx.enter_context
        NB = 2
        xin = [Se(nc.sbuf_tensor(f"xin{i}", [128, D], F32)) for i in range(NB)]
        xn_t = [Se(nc.sbuf_tensor(f"xn{i}", [128, D], F32)) for i in range(NB)]
        pre31 = Se(nc.sbuf_tensor("pre31", [128, D], F32))
        x1_t = [Se(nc.sbuf_tensor(f"x1_{i}", [128, D], F32)) for i in range(NB)]
        xn2_t = [Se(nc.sbuf_tensor(f"xn2_{i}", [128, D], F32)) for i in range(NB)]
        xn2f = [Se(nc.sbuf_tensor(f"xn2f{i}", [128, 8, 128], F32)) for i in range(NB)]
        NXB = 6
        xb_t = [Se(nc.sbuf_tensor(f"xb_{i}", [128, D], BF16)) for i in range(NXB)]
        sq_scr = Se(nc.sbuf_tensor("sq_scr", [128, D], F32))
        sv = [Se(nc.sbuf_tensor(f"sv{i}", [128, 16], F32)) for i in range(NB)]
        rmur2 = [Se(nc.sbuf_tensor(f"rmur2_{i}", [128, 4], F32)) for i in range(NB)]
        e8_t = [Se(nc.sbuf_tensor(f"e8_{i}", [128, 8], F32)) for i in range(NB)]
        oh_t = [Se(nc.sbuf_tensor(f"oh_{i}", [128, 8], F32)) for i in range(NB)]
        cnd_t = [Se(nc.sbuf_tensor(f"cnd{i}", [128, 8], F32)) for i in range(NB)]
        rkm_t = [Se(nc.sbuf_tensor(f"rkm{i}", [128, 8], F32)) for i in range(NB)]
        sl_t = [Se(nc.sbuf_tensor(f"sl{i}", [128, 4], F32)) for i in range(NB)]
        pctx = contextlib.ExitStack()
        tr8 = pctx.enter_context(nc.psum_tensor("tr8", [128, 8, 128], F32))
        lgb = pctx.enter_context(nc.psum_tensor("lgb", [128, 16], F32))
        cntt = pctx.enter_context(nc.psum_tensor("cntt", [128, 512], F32))
        window_base = nc.sbuf_base

        with contextlib.ExitStack() as wctx:
            We = wctx.enter_context
            wqb = We(nc.sbuf_tensor("wqb", [128, 8, 1024], BF16))
            wkb = We(nc.sbuf_tensor("wkb", [128, 8, 1024], BF16))
            wkpb = wqb
            wvb = wkb
            wob = wqb
            gam_r = We(nc.sbuf_tensor("gam_r", [1, D], F32))
            bet_r = We(nc.sbuf_tensor("bet_r", [1, D], F32))
            winx_sb = We(nc.sbuf_tensor("winx_sb", [32, D], F32))
            wn = We(nc.sbuf_tensor("wn", [32, D], F32))
            wnT = We(nc.sbuf_tensor("wnT", [128, 8, 32], BF16))
            q_sb = We(nc.sbuf_tensor("q_sb", [32, D], F32))
            k_sb = We(nc.sbuf_tensor("k_sb", [32, D], F32))
            kp_sb = We(nc.sbuf_tensor("kp_sb", [32, D], F32))
            v_sb = We(nc.sbuf_tensor("v_sb", [32, D], F32))
            qt_bf = We(nc.sbuf_tensor("qt_bf", [128, 8, 32], BF16))
            kt_f = We(nc.sbuf_tensor("kt_f", [128, 8, 32], F32))
            kpt_f = We(nc.sbuf_tensor("kpt_f", [128, 8, 32], F32))
            rrep_sb = We(nc.sbuf_tensor("rrep_sb", [32, 128], F32))
            rj_sb = We(nc.sbuf_tensor("rj_sb", [32, 1024], F32))
            vp4 = We(nc.sbuf_tensor("vp4", [128, D], F32))
            vj_s = We(nc.sbuf_tensor("vj_s", [128, D], F32))
            vv_b = We(nc.sbuf_tensor("vv_b", [128, 8, 1024], BF16))
            kkp_b = We(nc.sbuf_tensor("kkp_b", [128, 1024], BF16))
            wms = We(nc.sbuf_tensor("wms", [32, 1024], F32))
            a_sb = We(nc.sbuf_tensor("a_sb", [32, 1024], F32))
            p_bf = We(nc.sbuf_tensor("p_bf", [32, 1024], BF16))
            pt_bf = We(nc.sbuf_tensor("pt_bf", [128, 8, 32], BF16))
            z_sb = We(nc.sbuf_tensor("z_sb", [32, 1024], F32))
            zb_sb = We(nc.sbuf_tensor("zb_sb", [32, 1024], BF16))
            zt_bf = We(nc.sbuf_tensor("zt_bf", [128, 8, 128], BF16))
            wsc1 = We(nc.sbuf_tensor("wsc1", [32, 1024], F32))
            wsc2 = We(nc.sbuf_tensor("wsc2", [32, 16], F32))
            wsm = We(nc.sbuf_tensor("wsm", [32, 1], F32))
            wsm2 = We(nc.sbuf_tensor("wsm2", [32, 1], F32))
            wsm3 = We(nc.sbuf_tensor("wsm3", [32, 1], F32))
            wps = [We(nc.psum_tensor(f"wps{i}", [128, 512], F32)) for i in range(3)]
            wpb = [We(nc.psum_tensor("wpb0", [128, 8, 32], BF16))]
            

            with nc.Block() as blk:
                @blk.sync
                def _(sy):
                    loads = [
                        (id_bf[:], idbf), (id_f[:], idf32), (lt_sb[:], ltm),
                        (io8_sb[:], iota8), (iom9[:], iom9c), (io8p16[:], io8p16c),
                        (on1_sb[:], ones1),
                        (gam_r[:], gam), (bet_r[:], bet),
                        (wr_sb[:], wr.rearrange("(c p) e -> p c e", p=128)),
                        (winx_sb[:], winx), (wap_sb[:], winap), (myx_sb[:], myexp),
                        (csel_sb[:], coresel), (wms[:], wmask),
                        (rrep_sb[:], rrep), (rj_sb[:], rj),
                    ]
                    for dst, src in loads:
                        sy.dma_start(dst, src).then_inc(S("sX"), 16)
                        n_sX[0] += 16

                @blk.gpsimd
                def _(g):
                    g.iota(pidx[:], [[0, 1]], base=0, channel_multiplier=1,
                           allow_small_or_imprecise_dtypes=True)
                    g.memset(zbig[:], 0.0).then_inc(S("sGp"), 1)
                    g.memset(zrow[:], 0.0).then_inc(S("sGp"), 1)
                    g.memset(brun[:], 0.0)
                    g.memset(eps_t[:], EPS_LN)
                    g.memset(tiny_t[:], 1e-38)
                    g.memset(onesc[:], 1.0)
                    g.memset(winfull[:], 0.0)
                    g.memset(zt_bf[:], 0.0)
                    if debug:
                        g.memset(dbgt[:], 0.0)
                    g.wait_ge(S("sGp"), 2)
                    g.dma_start(cc_in.ap(), zrow[:]).then_inc(S("sGd"), 16)
                    for wsrc, wdst in [(wq, wqb), (wk, wkb)]:
                        g.dma_start(wdst[:], wsrc.rearrange("(c p) f -> p c f", p=128)
                                    ).then_inc(S("sGd"), 16)
                    n_sGd[0] += 3 * 16

            W = Script(S("w"), S("wdx0"))

            def mm(t, ps, lhsT, rhs, start=True, stop=True, sg=False):
                return t.matmul(ps, lhsT, rhs, start=start, stop=stop,
                                skip_group_check=sg)

            for half in range(2):
                W.add("tensor", lambda t, h=half: mm(t, wps[0][:, 0:512], on1_sb[:],
                                                     gam_r[0:1, 512 * h:512 * h + 512]))
                W.add("vector", lambda v, h=half: v.tensor_copy(
                    gam_bc[:, 512 * h:512 * h + 512], wps[0][:, 0:512]))
                W.add("tensor", lambda t, h=half: mm(t, wps[1][:, 0:512], on1_sb[:],
                                                     bet_r[0:1, 512 * h:512 * h + 512]))
                W.add("vector", lambda v, h=half: v.tensor_copy(
                    bet_bc[:, 512 * h:512 * h + 512], wps[1][:, 0:512]))

            W.add("scalar", lambda a: a.activation(wsc1[:], winx_sb[:], AF.Square,
                                                   accum_out=wsm[:]))
            W.add("scalar", lambda a: a.activation(wsc1[:], winx_sb[:], AF.Identity,
                                                   accum_out=wsm2[:]))
            W.add("vector", lambda v: v.tensor_scalar(wsm2[:], wsm2[:], 1.0 / D,
                                                      None, ALU.mult))
            W.add("vector", lambda v: v.tensor_scalar(wsm[:], wsm[:], 1.0 / D,
                                                      None, ALU.mult))
            W.add("vector", lambda v: v.scalar_tensor_tensor(
                wsm3[:], wsm2[:], wsm2[0:32, :], wsm[:], ALU.mult, ALU.subtract))
            W.add("scalar", lambda a: a.activation(wsm[:], wsm3[:], AF.Ln,
                                                   bias=eps_t[0:32, :], scale=-1.0))
            W.add("scalar", lambda a: a.activation(wsm[:], wsm[:], AF.Exp,
                                                   bias=0.0, scale=-0.5))
            W.add("vector", lambda v: v.tensor_scalar(wsm2[:], wsm2[:], wsm[0:32, :],
                                                      -1.0, ALU.mult, ALU.mult))
            W.add("vector", lambda v: v.tensor_scalar(wn[:], winx_sb[:], wsm[0:32, :],
                                                      wsm2[0:32, :], ALU.mult, ALU.add))

            for c in range(8):
                W.add("tensor", lambda t, c=c: t.transpose(
                    wps[2][:, 32 * c:32 * c + 32], wn[0:32, 128 * c:128 * c + 128],
                    id_f[0:32, 0:32]))
            W.add("vector", lambda v: v.tensor_copy(wnT[:], wps[2][:, 0:256]))

            def wait_wdma(t):
                t.wait_ge(S("sGd"), n_sGd[0])
                return t.nop()
            W.add("tensor", wait_wdma)

            def _proj(wbuf, dest):
                for fh in range(2):
                    for dc in range(8):
                        W.add("tensor", lambda t, fh=fh, dc=dc, wbuf=wbuf: mm(
                            t, wps[2][0:32, 0:512], wnT[:, dc, :],
                            wbuf[:, dc, 512 * fh:512 * fh + 512],
                            start=(dc == 0), stop=(dc == 7)))
                    W.add("vector", lambda v, fh=fh, dest=dest: v.tensor_copy(
                        dest[:, 512 * fh:512 * fh + 512], wps[2][0:32, 0:512]))

            wmarks = []
            _proj(wqb, q_sb)
            wmarks.append(W.total())
            _proj(wkb, k_sb)
            wmarks.append(W.total())
            W.add("tensor", lambda t: (t.wait_ge(S("wdx0"), 16), t.nop())[1])
            _proj(wkpb, kp_sb)
            wmarks.append(W.total())
            W.add("tensor", lambda t: (t.wait_ge(S("wdx1"), 16), t.nop())[1])
            _proj(wvb, v_sb)

            for dest in (q_sb, k_sb, kp_sb):
                W.add("vector", lambda v, d=dest: v.tensor_tensor(
                    wsc1[:], d[:], d[:], ALU.mult))
                W.add("vector", lambda v: v.tensor_reduce(
                    wsc2[:], wsc1[0:32, :].rearrange("p (h d) -> p h d", d=64),
                    AX.X, ALU.add))
                W.add("scalar", lambda a: a.activation(wsc2[:], wsc2[:], AF.Ln,
                                                       bias=tiny_t[0:32, :], scale=1.0))
                W.add("scalar", lambda a: a.activation(wsc2[:], wsc2[:], AF.Exp,
                                                       bias=0.0, scale=0.5))
                W.add("vector", lambda v: v.tensor_scalar(wsc2[:], wsc2[:], 1e-7,
                                                          None, ALU.add))
                W.add("vector", lambda v: v.reciprocal(wsc2[:], wsc2[:]))
                for h in range(16):
                    W.add("vector", lambda v, h=h, d=dest: v.tensor_scalar(
                        d[:, 64 * h:64 * h + 64], d[:, 64 * h:64 * h + 64],
                        wsc2[0:32, h:h + 1], None, ALU.mult))

            for c in range(8):
                W.add("tensor", lambda t, c=c: t.transpose(
                    wps[2][:, 32 * c:32 * c + 32], q_sb[0:32, 128 * c:128 * c + 128],
                    id_f[0:32, 0:32]))
            W.add("scalar", lambda a: a.activation(
                qt_bf[:], wps[2][:, 0:256], AF.Identity, bias=0.0, scale=0.125))
            for src, dst in [(k_sb, kt_f), (kp_sb, kpt_f)]:
                for c in range(8):
                    W.add("tensor", lambda t, c=c, src=src: t.transpose(
                        wps[2][:, 32 * c:32 * c + 32], src[0:32, 128 * c:128 * c + 128],
                        id_f[0:32, 0:32]))
                W.add("vector", lambda v, dst=dst: v.tensor_copy(
                    dst[:], wps[2][:, 0:256]))

            for fh in range(2):
                W.add("tensor", lambda t, fh=fh: mm(t, wps[2][:, 0:512], rrep_sb[:],
                                                    v_sb[0:32, 512 * fh:512 * fh + 512]))
                W.add("vector", lambda v, fh=fh: v.tensor_copy(
                    vp4[:, 512 * fh:512 * fh + 512], wps[2][:, 0:512]))
            for c in range(8):
                for fh in range(2):
                    W.add("tensor", lambda t, c=c, fh=fh: mm(
                        t, wps[2][:, 0:512], rj_sb[0:32, 128 * c:128 * c + 128],
                        v_sb[0:32, 512 * fh:512 * fh + 512]))
                    W.add("vector", lambda v, fh=fh: v.tensor_copy(
                        vj_s[:, 512 * fh:512 * fh + 512], wps[2][:, 0:512]))
                W.add("vector", lambda v, c=c: v.tensor_tensor(
                    vv_b[:, c, :], vj_s[:], vp4[:], ALU.mult))

            zps_banks = (wps[0], wps[1])
            for hc in range(8):
                W.add("vector", lambda v, hc=hc: v.tensor_tensor(
                    kkp_b[:].rearrange("p (j k) -> p j k", k=32),
                    bass.AP(kt_f, hc * 32, [[256, 128], [1, 32], [0, 32]]),
                    bass.AP(kpt_f, hc * 32, [[256, 128], [0, 32], [1, 32]]),
                    ALU.mult))
                for u in range(2):
                    h = 2 * hc + u
                    for fh in range(2):
                        W.add("tensor", lambda t, u=u, hc=hc, fh=fh: mm(
                            t, wps[2][0:32, 0:512],
                            qt_bf[64 * u:64 * u + 64, hc, :],
                            kkp_b[64 * u:64 * u + 64, 512 * fh:512 * fh + 512]))
                        W.add("vector", lambda v, fh=fh: v.tensor_tensor(
                            a_sb[:, 512 * fh:512 * fh + 512], wps[2][0:32, 0:512],
                            wms[:, 512 * fh:512 * fh + 512], ALU.add))
                    W.add("vector", lambda v: v.reduce_max(wsm[:], a_sb[:], AX.X))
                    W.add("vector", lambda v: v.tensor_scalar(wsm[:], wsm[:], -1.0,
                                                              None, ALU.mult))
                    W.add("scalar", lambda a: a.activation(wsc1[:], a_sb[:], AF.Exp,
                                                           bias=wsm[0:32, :], scale=1.0,
                                                           accum_out=wsm2[:]))
                    W.add("vector", lambda v: v.reciprocal(wsm2[:], wsm2[:]))
                    W.add("vector", lambda v: v.tensor_scalar(p_bf[:], wsc1[:],
                                                              wsm2[0:32, :], None,
                                                              ALU.mult))
                    for c in range(8):
                        W.add("tensor", lambda t, c=c: t.transpose(
                            wpb[0][:, c, :], p_bf[0:32, 128 * c:128 * c + 128],
                            id_bf[0:32, 0:32]))
                    W.add("vector", lambda v: v.tensor_copy(pt_bf[:], wpb[0][:]))
                    for c in range(8):
                        W.add("tensor", lambda t, c=c, h=h: mm(
                            t, zps_banks[h // 8][0:32, 64 * (h % 8):64 * (h % 8) + 64],
                            pt_bf[:, c, :], vv_b[:, c, 64 * h:64 * h + 64],
                            start=(c == 0), stop=(c == 7), sg=True))

            for b2 in range(2):
                W.add("vector", lambda v, b2=b2: v.tensor_copy(
                    z_sb[:, 512 * b2:512 * b2 + 512], zps_banks[b2][0:32, 0:512]))
            W.add("scalar", lambda a: a.activation(zb_sb[:], z_sb[:], AF.Identity,
                                                   bias=0.0, scale=1.0))
            for c in range(8):
                W.add("tensor", lambda t, c=c: t.transpose(
                    wpb[0][:, c, :], zb_sb[0:32, 128 * c:128 * c + 128],
                    id_bf[0:32, 0:32]))
            W.add("vector", lambda v: v.tensor_copy(zt_bf[:, :, 96:128], wpb[0][:]))
            W.add("tensor", lambda t: (t.wait_ge(S("wdx2"), 16), t.nop())[1])
            for fh in range(2):
                for dc in range(8):
                    W.add("tensor", lambda t, fh=fh, dc=dc: mm(
                        t, wps[2][:, 0:512], zt_bf[:, dc, :],
                        wob[:, dc, 512 * fh:512 * fh + 512],
                        start=(dc == 0), stop=(dc == 7)))
                W.add("vector", lambda v, fh=fh: v.tensor_copy(
                    winfull[:, 512 * fh:512 * fh + 512], wps[2][:, 0:512]))

            W_TOTAL = W.total()
            W_LEN = len(W.ops)

            def SEG(k):
                return min(W_LEN, (W_LEN * (k + 1) + 61) // 62)

            assert SEG(2 * (NXB + 2)) >= wmarks[2], (wmarks, W_LEN)

            wptr = {"scalar": 0, "vector": 0, "tensor": 0}

            def wseg(name, eng, hi):
                hi = min(hi, W_LEN)
                if wptr[name] < hi:
                    W.emit_range(name, eng, wptr[name], hi)
                    wptr[name] = hi

        w1b_r = E(nc.sbuf_tensor("w1b_r", [128, 8, 4096], BF16))
        w2b_r = E(nc.sbuf_tensor("w2b_r", [128, 32, D], BF16))

        if True:
            xtile = xsh.rearrange("(t p) f -> t p f", p=128)
            otile = out.rearrange("(t p) f -> t p f", p=128)

            with nc.Block() as blk:
                @blk.sync
                def _(sy):
                    for t in range(NT):
                        if t >= NB:
                            sy.wait_ge(S("dvgb"), t - NB + 1)
                        sy.dma_start(xin[t % NB][:], xtile[t]).then_inc(S(f"xld{t % 4}"), 16)

                @blk.scalar
                def _(a):
                    def rstd_smalls(b, scol, qcol, t1col, ocol, sem, base):
                        a.wait_ge(S(sem), base + 2)
                        a.activation(sv[b][:, t1col:t1col + 1], sv[b][:, scol:scol + 1],
                                     AF.Square, scale=1.0 / D).then_inc(S(sem), 1)
                        a.wait_ge(S(sem), base + 3)
                        a.activation(sv[b][:, t1col:t1col + 1], sv[b][:, t1col:t1col + 1],
                                     AF.Identity, bias=eps_t[:],
                                     scale=-1.0).then_inc(S(sem), 1)
                        a.wait_ge(S(sem), base + 4)
                        a.activation(sv[b][:, ocol:ocol + 1], sv[b][:, qcol:qcol + 1],
                                     AF.Ln, bias=sv[b][:, t1col:t1col + 1],
                                     scale=1.0 / D).then_inc(S(sem), 1)
                        a.wait_ge(S(sem), base + 5)
                        a.activation(sv[b][:, ocol:ocol + 1], sv[b][:, ocol:ocol + 1],
                                     AF.Exp, bias=0.0, scale=-0.5).then_inc(S(sem), 1)

                    def stage_a1(t):
                        b = t % NB
                        a.wait_ge(S(f"xld{t % 4}"), 16 * (t // 4 + 1))
                        if t >= NB:
                            a.wait_ge(S("dvexn"), 2 * (t - NB) + 2)
                        a.activation(sq_scr[:], xin[b][:], AF.Square,
                                     accum_out=sv[b][:, 1:2]).then_inc(S("act1"), 1)
                        a.wait_ge(S("act1"), 6 * t + 1)
                        a.activation(sq_scr[:], xin[b][:], AF.Identity,
                                     accum_out=sv[b][:, 0:1]).then_inc(S("act1"), 1)
                        rstd_smalls(b, 0, 1, 2, 3, "act1", 6 * t)

                    def stage_a2(t):
                        b = t % NB
                        pre = pre31 if t == NT - 1 else xn_t[b]
                        a.wait_ge(S("dvexn"), 2 * t + 2)
                        if t == NT - 1:
                            a.wait_ge(S("w31"), 1)
                        if t >= NB:
                            a.wait_ge(S("dvean"), 3 * (t - NB) + 3)
                        a.activation(sq_scr[:], pre[:], AF.Square,
                                     accum_out=rmur2[b][:, 3:4]).then_inc(S("acta"), 1)
                        if t == NT - 1:
                            a.wait_ge(S("acta"), 3 * t + 1)
                            a.activation(sq_scr[:], pre[:], AF.Identity,
                                         accum_out=rmur2[b][:, 2:3]).then_inc(S("x31"), 1)
                            a.wait_ge(S("x31"), 1)
                            a.activation(rmur2[b][:, 1:2], rmur2[b][:, 2:3],
                                         AF.Square, scale=1.0 / D).then_inc(S("x31"), 1)
                            a.wait_ge(S("x31"), 2)
                            a.activation(rmur2[b][:, 1:2], rmur2[b][:, 1:2],
                                         AF.Identity, bias=eps_t[:],
                                         scale=-1.0).then_inc(S("x31"), 1)
                            a.wait_ge(S("x31"), 3)
                            a.wait_ge(S("acta"), 3 * t + 1)
                            a.activation(rmur2[b][:, 0:1], rmur2[b][:, 3:4], AF.Ln,
                                         bias=rmur2[b][:, 1:2],
                                         scale=1.0 / D).then_inc(S("acta"), 1)
                        else:
                            a.wait_ge(S("acta"), 3 * t + 1)
                            a.activation(rmur2[b][:, 0:1], rmur2[b][:, 3:4], AF.Ln,
                                         bias=eps_t[:], scale=1.0 / D).then_inc(S("acta"), 1)
                        a.wait_ge(S("acta"), 3 * t + 2)
                        a.activation(rmur2[b][:, 0:1], rmur2[b][:, 0:1], AF.Exp,
                                     bias=0.0, scale=-0.5).then_inc(S("acta"), 1)

                    def stage_b1(u):
                        bu = u % NB
                        a.wait_ge(S("dvgb"), u + 1)
                        a.dma_start(otile[u], x1_t[bu][:]).then_inc(S(f"x1st{u % 4}"), 16)
                        if u >= NB:
                            a.wait_ge(S("dvexn2"), 2 * (u - NB) + 2)
                        a.activation(sq_scr[:], x1_t[bu][:], AF.Square,
                                     accum_out=sv[bu][:, 6:7]).then_inc(S("act3"), 1)
                        a.wait_ge(S("act3"), 6 * u + 1)
                        a.activation(sq_scr[:], x1_t[bu][:], AF.Identity,
                                     accum_out=sv[bu][:, 5:6]).then_inc(S("act3"), 1)
                        rstd_smalls(bu, 5, 6, 7, 8, "act3", 6 * u)

                    def stage_b2a(u):
                        bu = u % NB
                        a.wait_ge(S("petr"), 8 * u + 8)
                        if u >= NB:
                            a.wait_ge(S("pelg"), u - 1)
                        a.copy(xn2f[bu][:], tr8[:]).then_inc(S("aev"), 1)

                    def stage_b2b(u):
                        bu = u % NB
                        a.wait_ge(S("dvemx"), 2 * u + 2)
                        if u >= NB:
                            a.wait_ge(S("dvpr"), 8 * (u - NB) + 5)
                        a.activation(e8_t[bu][:], lgb[:, 8 * (u % 2):8 * (u % 2) + 8], AF.Exp,
                                     bias=sv[bu][:, 10:11], scale=1.0).then_inc(S("actsm"), 1)

                    a.wait_ge(S("sX"), n_sX[0])
                    a.wait_ge(S("sGp"), 1)
                    zsrc = bass.AP(zbig, 0, [[1024, 128], [0, 4], [1, 1024]])
                    for i in range(NE * SA2A // 512):
                        for dst in (snd_d, ysnd_d):
                            a.dma_start(
                                dst.rearrange("(n a p) f -> n p a f", a=4, p=128)[i],
                                zsrc).then_inc(S("zsnd"), 16)
                    a.dma_start(yrcv_d[NE * SA2A:NE * SA2A + 128, :],
                                zbig[:]).then_inc(S("zsnd"), 16)
                    for t in range(NT + 2):
                        wseg("scalar", a, SEG(2 * t))
                        if t < NT:
                            stage_a1(t)
                        if 1 <= t <= NT:
                            stage_b1(t - 1)
                        if t < NT:
                            stage_a2(t)
                        if t >= 2:
                            stage_b2a(t - 2)
                            stage_b2b(t - 2)
                        wseg("scalar", a, SEG(2 * t + 1))

                @blk.vector
                def _(v):
                    v.wait_ge(S("sX"), n_sX[0])
                    def stage_a(t):
                        b = t % NB
                        v.wait_ge(S("act1"), 6 * t + 6)
                        v.tensor_scalar(sv[b][:, 4:5], sv[b][:, 0:1], sv[b][:, 3:4],
                                        -1.0 / D, ALU.mult, ALU.mult).then_inc(S("dvexn"), 1)
                        v.wait_ge(S("dvexn"), 2 * t + 1)
                        if t >= NB:
                            v.wait_ge(S("acta"), 3 * (t - NB) + 3)
                        v.tensor_scalar(xn_t[b][:], xin[b][:], sv[b][:, 3:4],
                                        sv[b][:, 4:5], ALU.mult,
                                        ALU.add).then_inc(S("dvexn"), 1)
                        if t == NT - 1:
                            v.wait_ge(S("dvexn"), 2 * t + 2)
                            v.wait_ge(S("w"), W_TOTAL)
                            v.scalar_tensor_tensor(pre31[:], winfull[:], wap_sb[:],
                                                   xn_t[b][:], ALU.mult,
                                                   ALU.add).then_inc(S("w31"), 1)
                        v.wait_ge(S("acta"), 3 * t + 3)
                        v.wait_ge(S("w"), 8)
                        if t == NT - 1:
                            v.wait_ge(S("x31"), 1)
                            v.tensor_scalar(rmur2[b][:, 1:2], rmur2[b][:, 2:3],
                                            rmur2[b][:, 0:1], -1.0 / D, ALU.mult,
                                            ALU.mult).then_inc(S("w31"), 1)
                            v.wait_ge(S("w31"), 2)
                            v.tensor_scalar(xn_t[b][:], pre31[:], rmur2[b][:, 0:1],
                                            rmur2[b][:, 1:2], ALU.mult,
                                            ALU.add).then_inc(S("dvean"), 1)
                        else:
                            v.tensor_scalar(xn_t[b][:], xn_t[b][:], rmur2[b][:, 0:1],
                                            None, ALU.mult).then_inc(S("dvean"), 1)
                        v.wait_ge(S("dvean"), 3 * t + 1)
                        v.tensor_tensor(xn_t[b][:], xn_t[b][:], gam_bc[:],
                                        ALU.mult).then_inc(S("dvean"), 1)
                        v.wait_ge(S("dvean"), 3 * t + 2)
                        v.tensor_tensor(xn_t[b][:], xn_t[b][:], bet_bc[:],
                                        ALU.add).then_inc(S("dvean"), 1)
                        v.wait_ge(S("dvean"), 3 * t + 3)
                        if t >= NB:
                            v.wait_ge(S("act3"), 6 * (t - NB) + 2)
                            v.wait_ge(S(f"x1st{(t - NB) % 4}"),
                                      16 * ((t - NB) // 4 + 1))
                        v.tensor_tensor(x1_t[b][:], xin[b][:], xn_t[b][:],
                                        ALU.add).then_inc(S("dvgb"), 1)

                    def stage_b1(u):
                        bu = u % NB
                        v.wait_ge(S("act3"), 6 * u + 6)
                        v.tensor_scalar(sv[bu][:, 9:10], sv[bu][:, 5:6], sv[bu][:, 8:9],
                                        -1.0 / D, ALU.mult, ALU.mult).then_inc(S("dvexn2"), 1)
                        v.wait_ge(S("dvexn2"), 2 * u + 1)
                        if u >= NB:
                            v.wait_ge(S("dvxb"), u - NB + 1)
                        v.tensor_scalar(xn2_t[bu][:], x1_t[bu][:], sv[bu][:, 8:9],
                                        sv[bu][:, 9:10], ALU.mult,
                                        ALU.add).then_inc(S("dvexn2"), 1)

                    def stage_b2(t):
                        b = t % NB
                        v.wait_ge(S("pelg"), t + 1)
                        if t >= NB:
                            v.wait_ge(S("actsm"), t - NB + 1)
                        v.reduce_max(sv[b][:, 10:11], lgb[:, 8 * (t % 2):8 * (t % 2) + 8], AX.X).then_inc(S("dvemx"), 1)
                        v.wait_ge(S("dvemx"), 2 * t + 1)
                        v.tensor_scalar(sv[b][:, 10:11], sv[b][:, 10:11], -1.0,
                                        None, ALU.mult).then_inc(S("dvemx"), 1)
                        v.wait_ge(S("actsm"), t + 1)
                        v.tensor_reduce(sv[b][:, 11:12], e8_t[b][:], AX.X,
                                        ALU.add).then_inc(S("dvpr"), 1)
                        v.wait_ge(S("dvpr"), 8 * t + 1)
                        v.reciprocal(sv[b][:, 12:13], sv[b][:, 11:12]).then_inc(S("dvpr"), 1)
                        v.wait_ge(S("dvpr"), 8 * t + 2)
                        v.reduce_max(sv[b][:, 15:16], e8_t[b][:], AX.X).then_inc(S("dvpr"), 1)
                        v.wait_ge(S("dvpr"), 8 * t + 3)
                        v.tensor_scalar(sv[b][:, 13:14], sv[b][:, 15:16], sv[b][:, 12:13],
                                        None, ALU.mult).then_inc(S("dvpr"), 1)
                        v.tensor_scalar(cnd_t[b][:], e8_t[b][:], sv[b][:, 15:16],
                                        None, ALU.is_ge).then_inc(S("dvpr"), 1)
                        v.wait_ge(S("dvpr"), 8 * t + 5)
                        v.scalar_tensor_tensor(cnd_t[b][:], cnd_t[b][:], -16.0,
                                               io8p16[:], ALU.mult,
                                               ALU.add).then_inc(S("dvpr"), 1)
                        v.wait_ge(S("dvpr"), 8 * t + 6)
                        v.tensor_reduce(sv[b][:, 14:15], cnd_t[b][:], AX.X,
                                        ALU.min).then_inc(S("dvpr"), 1)
                        v.wait_ge(S("dvpr"), 8 * t + 7)
                        if t >= NB:
                            v.wait_ge(S("petot"), t - NB + 1)
                            v.wait_ge(S("pecnt"), t - NB + 1)
                        v.tensor_scalar(oh_t[b][:], io8_sb[:], sv[b][:, 14:15],
                                        None, ALU.is_equal).then_inc(S("dvpr"), 1)
                        v.wait_ge(S("dvexn2"), 2 * t + 2)
                        if t >= NXB:
                            v.wait_ge(S(f"poolsc{(t - NXB) % 8}"),
                                      16 * ((t - NXB) // 8 + 1))
                        v.tensor_scalar(xb_t[t % NXB][:], xn2_t[b][:], sv[b][:, 13:14],
                                        None, ALU.mult).then_inc(S("dvxb"), 1)

                    def stage_b3(t):
                        b = t % NB
                        v.wait_ge(S("pecnt"), t + 1)
                        v.wait_ge(S("petot"), t + 1)
                        v.tensor_tensor(rkm_t[b][:], cntt[:, 8 * (t % 2):8 * (t % 2) + 8], oh_t[b][:],
                                        ALU.mult).then_inc(S("dverk"), 1)
                        v.wait_ge(S("dverk"), 8 * t + 1)
                        v.tensor_reduce(sl_t[b][:, 0:1], rkm_t[b][:], AX.X,
                                        ALU.add).then_inc(S("dverk"), 1)
                        v.wait_ge(S("dverk"), 8 * t + 2)
                        v.scalar_tensor_tensor(sl_t[b][:, 1:2], sv[b][:, 14:15],
                                               float(SA2A), sl_t[b][:, 0:1],
                                               ALU.mult, ALU.add).then_inc(S("dverk"), 1)
                        v.wait_ge(S("dverk"), 8 * t + 3)
                        v.tensor_scalar(sl_t[b][:, 2:3], sl_t[b][:, 0:1], float(SA2A),
                                        None, ALU.is_ge).then_inc(S("dverk"), 1)
                        v.wait_ge(S("dverk"), 8 * t + 4)
                        v.scalar_tensor_tensor(sl_t[b][:, 3:4], sl_t[b][:, 2:3], SENT,
                                               sl_t[b][:, 1:2], ALU.mult,
                                               ALU.add).then_inc(S("dverk"), 1)
                        v.wait_ge(S("dverk"), 8 * t + 5)
                        v.tensor_copy(slot_sb[:, t:t + 1], sl_t[b][:, 3:4]).then_inc(S("dverk"), 1)
                        v.tensor_scalar(sl_t[b][:, 2:3], sl_t[b][:, 3:4],
                                        float(NE * SA2A), None,
                                        ALU.min).then_inc(S("dverk"), 1)
                        v.wait_ge(S("dverk"), 8 * t + 7)
                        v.tensor_copy(fslot_sb[:, t:t + 1], sl_t[b][:, 2:3]).then_inc(S("dverk"), 1)
                        v.wait_ge(S("petot"), t + 1)
                        v.tensor_copy(tsl[:], cntt[0:1, 256 + 8 * t:256 + 8 * t + 8]).then_inc(S("dvbr"), 1)
                        v.wait_ge(S("dvbr"), 2 * t + 1)
                        v.wait_ge(S("pecnt"), t + 1)
                        v.tensor_tensor(brun[:], brun[:], tsl[:],
                                        ALU.add).then_inc(S("dvbr"), 1)

                    for t in range(NT + 3):
                        wseg("vector", v, SEG(2 * t))
                        if t < NT:
                            stage_a(t)
                        if 1 <= t <= NT:
                            stage_b1(t - 1)
                        if 2 <= t <= NT + 1:
                            stage_b2(t - 2)
                        wseg("vector", v, SEG(2 * t + 1))
                        if t >= 3:
                            stage_b3(t - 3)

                @blk.tensor
                def _(t_eng):
                    t_eng.wait_ge(S("sX"), n_sX[0])
                    for it in range(NT + 2):
                        wseg("tensor", t_eng, SEG(2 * it + 2))
                        if 1 <= it <= NT:
                            v2 = it - 1
                            bv = v2 % NB
                            t_eng.wait_ge(S("aev"), v2 + 1)
                            if v2 >= 2:
                                t_eng.wait_ge(S("actsm"), v2 - 1)
                                t_eng.wait_ge(S("dvemx"), 2 * (v2 - 1))
                            h = None
                            for c in range(8):
                                h = t_eng.matmul(lgb[:, 8 * (v2 % 2):8 * (v2 % 2) + 8], xn2f[bv][:, c, :],
                                                 wr_sb[:, c, :],
                                                 start=(c == 0), stop=(c == 7),
                                                 skip_group_check=True)
                            h.then_inc(S("pelg"), 1)
                        if it < NT:
                            t = it
                            b = t % NB
                            for c in range(8):
                                if c == 0:
                                    t_eng.wait_ge(S("dvexn2"), 2 * t + 2)
                                    if t >= 1:
                                        t_eng.wait_ge(S("aev"), t)
                                t_eng.transpose(tr8[:, c, :],
                                                xn2_t[b][:, 128 * c:128 * c + 128],
                                                id_f[:]).then_inc(S("petr"), 1)
                        if it >= 2:
                            u = it - 2
                            bu = u % NB
                            t_eng.wait_ge(S("dvpr"), 8 * u + 8)
                            t_eng.wait_ge(S("dvbr"), 2 * u)
                            if u >= 2:
                                t_eng.wait_ge(S("dverk"), 8 * (u - 2) + 1)
                            t_eng.matmul(cntt[:, 8 * (u % 2):8 * (u % 2) + 8], lt_sb[:], oh_t[bu][:],
                                         start=True, stop=False,
                                         skip_group_check=True)
                            t_eng.matmul(cntt[:, 8 * (u % 2):8 * (u % 2) + 8], on1_sb[:], brun[:],
                                         start=False, stop=True,
                                         skip_group_check=True).then_inc(S("pecnt"), 1)
                            t_eng.matmul(cntt[0:1, 256 + 8 * u:256 + 8 * u + 8], onesc[:], oh_t[bu][:],
                                         start=True, stop=True,
                                         skip_group_check=True).then_inc(S("petot"), 1)

                @blk.gpsimd
                def _(g):
                    with g.register("bc_disp") as bc_disp:
                        g.reg_mov(bc_disp, NE * SA2A - 1)
                        g.wait_ge(S("w"), wmarks[0])
                        g.dma_start(wkpb[:],
                                    wkp.rearrange("(c p) f -> p c f", p=128)
                                    ).then_inc(S("wdx0"), 16)
                        g.wait_ge(S("w"), wmarks[1])
                        g.dma_start(wvb[:],
                                    wv.rearrange("(c p) f -> p c f", p=128)
                                    ).then_inc(S("wdx1"), 16)
                        g.wait_ge(S("w"), wmarks[2])
                        g.dma_start(wob[:],
                                    wo.rearrange("(c p) f -> p c f", p=128)
                                    ).then_inc(S("wdx2"), 16)
                        g.wait_ge(S("sGd"), n_sGd[0])
                        g.wait_ge(S("zsnd"), 16 * (2 * (NE * SA2A // 512) + 1))
                        for t in range(NT):
                            g.wait_ge(S("dverk"), 8 * t + 6)
                            g.wait_ge(S("dvxb"), t + 1)
                            g.indirect_dma_start(snd_d, IOA(slot_sb[:, t:t + 1], 0),
                                                 xb_t[t % NXB][:], None,
                                                 bounds_check=bc_disp,
                                                 oob_is_err=False).then_inc(S(f"poolsc{t % 8}"), 16)
                        g.wait_ge(S("w"), W_TOTAL)
                        for ss in range(8):
                            g.dma_start(w1b_r[:, :, 512 * ss:512 * ss + 512],
                                        w1my[:, 512 * ss:512 * ss + 512]
                                        .rearrange("(c p) f -> p c f", p=128)
                                        ).then_inc(S("wdm"), 16)
                        for qq in range(4):
                            g.dma_start(w2b_r[:, 8 * qq:8 * qq + 8, :],
                                        w2my[1024 * qq:1024 * qq + 1024, :]
                                        .rearrange("(c p) f -> p c f", p=128)
                                        ).then_inc(S("wdm2"), 16)

            P = Script(S("post"), S("postd"))
            P.add("vector", lambda v: v.tensor_copy(tot_sb[:], brun[:]))
            P.add("gpsimd", lambda g: g.dma_start(cc_in.ap()[0:1, 0:8], tot_sb[:]),
                  inc=16)
            P.add("gpsimd", lambda g: g.collective_compute(
                "AllGather", ALU.bypass, replica_groups=[list(range(8))],
                ins=[cc_in.ap().opt()], outs=[cc_out.ap().opt()]))
            P.add("gpsimd", lambda g: g.dma_start(cnt_all[:], cc_out.ap()[0:8, 0:8]),
                  inc=16)

            def _a2a1(g):
                with allow_ici_all_to_all():
                    return g.collective_compute(
                        "AllToAll", ALU.bypass, replica_groups=[list(range(8))],
                        ins=[snd_d.opt()], outs=[rcv_d.opt()])
            P.add("gpsimd", _a2a1)
            P.add("tensor", lambda t: t.matmul(cntt[0:8, 0:8], lt_sb[0:8, 0:8],
                                               cnt_all[:], start=True, stop=True,
                                               skip_group_check=True))
            P.add("vector", lambda v: v.tensor_copy(basel[:], cntt[0:8, 0:8]))
            P.add("vector", lambda v: v.tensor_tensor(basel[:], basel[:], myx_sb[:],
                                                      ALU.mult))
            P.add("vector", lambda v: v.tensor_reduce(bcol[:], basel[:], AX.X,
                                                      ALU.add))
            P.add("tensor", lambda t: t.transpose(cntt[0:1, 8:16], bcol[:],
                                                  id_f[0:8, 0:8]))
            P.add("vector", lambda v: v.tensor_copy(brow[:], cntt[0:1, 8:16]))
            P.add("tensor", lambda t: t.matmul(cntt[:, 0:8], on1_sb[:], brow[:],
                                               start=True, stop=True,
                                               skip_group_check=True))
            P.add("vector", lambda v: v.tensor_copy(bases_bc[:], cntt[:, 0:8]))
            P_TOTAL = P.total()

            with nc.Block() as blk:
                @blk.vector
                def _(v):
                    v.wait_ge(S("dvbr"), 2 * NT)
                    P.emit("vector", v)

                @blk.tensor
                def _(t):
                    P.emit("tensor", t)

                @blk.gpsimd
                def _(g):
                    g.wait_ge(S("sGd"), n_sGd[0])
                    for pp in range(8):
                        g.wait_ge(S(f"poolsc{pp}"), 16 * (NT // 8))
                    P.emit("gpsimd", g)

            pctx.close()

        if do_c:
            with contextlib.ExitStack() as cctx:
                Ce = cctx.enter_context
                CM, _coff = _man_alloc(sweep_base)
                xe_t = [CM(f"xe{i}", [128, D], BF16) for i in range(2)]
                xeT = [CM(f"xeT{i}", [128, 8, 512], BF16) for i in range(2)]
                h_sb = CM("h_sb", [128, 32, 512], BF16)
                ysnd_t = [CM(f"ysnd{i}", [128, D], BF16) for i in range(4)]
                cj_t = CM("cj_t", [128, 8], F32)
                cmp8_t = CM("cmp8_t", [128, 8], F32)
                bm8_t = CM("bm8_t", [128, 8], F32)
                assert _coff[0] <= window_base, (_coff[0], window_base)
                ht_ps = [Ce(nc.psum_tensor(f"ht_ps{i}", [128, 128], BF16)) for i in range(2)]
                h_ps = [Ce(nc.psum_tensor(f"h_ps{i}", [128, 512], F32)) for i in range(2)]
                y_ps = [[Ce(nc.psum_tensor(f"y_ps{i}_{j}", [128, 512], F32))
                         for j in range(2)] for i in range(2)]

                NHALF = CAP // 512
                NGRP = CAP // 128

                def _sweep_dma_gate(eng):
                    for pp in range(4):
                        eng.wait_ge(S(f"x1st{pp}"), 16 * (NT // 4))
                        eng.wait_ge(S(f"xld{pp}"), 16 * (NT // 4))
                    for pp in range(8):
                        eng.wait_ge(S(f"poolsc{pp}"), 16 * (NT // 8))

                with nc.Block() as blk:
                    @blk.vector
                    def _(v):
                        _sweep_dma_gate(v)
                        v.wait_ge(S("post"), P_TOTAL)
                        for g in range(NGRP):
                            b8 = 8 * g
                            v.tensor_scalar(cj_t[:, 0:1], pidx[:], float(128 * g),
                                            None, ALU.add).then_inc(S("cidx"), 1)
                            v.wait_ge(S("cidx"), b8 + 1)
                            v.tensor_scalar(cmp8_t[:], bases_bc[:], cj_t[:, 0:1],
                                            None, ALU.is_le).then_inc(S("cidx"), 1)
                            v.wait_ge(S("cidx"), b8 + 2)
                            v.tensor_reduce(cj_t[:, 1:2], cmp8_t[:], AX.X,
                                            ALU.add).then_inc(S("cidx"), 1)
                            v.wait_ge(S("cidx"), b8 + 3)
                            v.tensor_tensor(bm8_t[:], bases_bc[:], cmp8_t[:],
                                            ALU.mult).then_inc(S("cidx"), 1)
                            v.wait_ge(S("cidx"), b8 + 4)
                            v.tensor_reduce(cj_t[:, 2:3], bm8_t[:], AX.X,
                                            ALU.max).then_inc(S("cidx"), 1)
                            v.wait_ge(S("cidx"), b8 + 5)
                            v.scalar_tensor_tensor(cj_t[:, 3:4], cj_t[:, 1:2],
                                                   float(SA2A), cj_t[:, 2:3],
                                                   ALU.mult,
                                                   ALU.subtract).then_inc(S("cidx"), 1)
                            v.wait_ge(S("cidx"), b8 + 6)
                            v.tensor_scalar(cj_t[:, 3:4], cj_t[:, 3:4], cj_t[:, 0:1],
                                            -float(SA2A), ALU.add,
                                            ALU.add).then_inc(S("cidx"), 1)
                            v.wait_ge(S("cidx"), b8 + 7)
                            v.tensor_copy(srcq_sb[:, g:g + 1],
                                          cj_t[:, 3:4]).then_inc(S("cidx"), 1)

                    @blk.scalar
                    def _(a):
                        _sweep_dma_gate(a)
                        for hq in range(NHALF):
                            for k in range(32):
                                kk = 32 * hq + k
                                a.wait_ge(S("chm"), kk + 1)
                                s_, fc4 = divmod(k, 4)
                                fc = 4 * s_ + fc4
                                a.activation(h_sb[:, fc, :], h_ps[kk % 2][:],
                                             AF.Relu, bias=0.0,
                                             scale=1.0).then_inc(S("cha"), 1)

                    @blk.tensor
                    def _(t):
                        def emit_tr(hq):
                            if hq >= 2:
                                t.wait_ge(S("chm"), 32 * (hq - 1))
                            for gg in range(4):
                                gi = 4 * hq + gg
                                t.wait_ge(S(f"cg{gi % 2}"), 16 * (gi // 2 + 1))
                                for c in range(8):
                                    j = 8 * gi + c
                                    if j >= 2:
                                        t.wait_ge(S("cxv"), j - 1)
                                    t.transpose(ht_ps[j % 2][:],
                                                xe_t[gi % 2][:, 128 * c:128 * c + 128],
                                                id_bf[:]).then_inc(S("cxt"), 1)
                        _sweep_dma_gate(t)
                        t.wait_ge(S("wdm"), 16 * 8)
                        emit_tr(0)
                        for hq in range(NHALF):
                            if hq == 0:
                                pass
                            t.wait_ge(S("cxv"), 32 * (hq + 1))
                            for s_ in range(8):
                                for fc4 in range(4):
                                    k = 4 * s_ + fc4
                                    kk = 32 * hq + k
                                    if kk >= 2:
                                        t.wait_ge(S("cha"), kk - 1)
                                    h = None
                                    for dc in range(8):
                                        h = t.matmul(
                                            h_ps[kk % 2][:],
                                            w1b_r[:, dc, 512 * s_ + 128 * fc4:
                                                  512 * s_ + 128 * fc4 + 128],
                                            xeT[hq % 2][:, dc, :],
                                            start=(dc == 0), stop=(dc == 7))
                                    h.then_inc(S("chm"), 1)
                            if hq + 1 < NHALF:
                                emit_tr(hq + 1)
                            if hq == 0:
                                t.wait_ge(S("wdm2"), 16 * 4)
                            t.wait_ge(S("cha"), 32 * (hq + 1))
                            for ts in range(4):
                                tsg = 4 * hq + ts
                                for dh in range(2):
                                    if tsg >= 2:
                                        t.wait_ge(S("cyv"), 2 * (tsg - 2) + dh + 1)
                                    h = None
                                    for fc in range(32):
                                        h = t.matmul(
                                            y_ps[tsg % 2][dh][:],
                                            h_sb[:, fc, 128 * ts:128 * ts + 128],
                                            w2b_r[:, fc, 512 * dh:512 * dh + 512],
                                            start=(fc == 0), stop=(fc == 31))
                                    h.then_inc(S("cym"), 1)

                    @blk.vector
                    def _(v):
                        for hq in range(NHALF + 1):
                            if hq < NHALF:
                                for gg in range(6):
                                    if gg >= 2:
                                        gi = 4 * hq + gg - 2
                                        for c in range(8):
                                            j = 8 * gi + c
                                            v.wait_ge(S("cxt"), j + 1)
                                            if hq >= 2 and gg == 2 and c == 0:
                                                v.wait_ge(S("chm"), 32 * (hq - 1))
                                            v.tensor_copy(
                                                xeT[hq % 2][:, c, 128 * (gg - 2):128 * (gg - 2) + 128],
                                                ht_ps[j % 2][:]).then_inc(S("cxv"), 1)
                                    if gg < 4:
                                        gi = 4 * hq + gg
                                        if gi >= 2:
                                            v.wait_ge(S("cxt"), 8 * (gi - 1))
                                        v.memset(xe_t[gi % 2][:], 0.0).then_inc(S("cme"), 1)
                            if hq >= 1:
                                for ts in range(4):
                                    tsg = 4 * (hq - 1) + ts
                                    for dh in range(2):
                                        f = 2 * tsg + dh
                                        v.wait_ge(S("cym"), f + 1)
                                        if tsg >= 4 and dh == 0:
                                            v.wait_ge(S(f"cys{tsg % 4}"),
                                                      16 * (tsg // 4))
                                        v.tensor_copy(
                                            ysnd_t[tsg % 4][:, 512 * dh:512 * dh + 512],
                                            y_ps[tsg % 2][dh][:]).then_inc(S("cyv"), 1)

                    @blk.gpsimd
                    def _(g):
                        with g.register("bc_src") as bc_src:
                            g.reg_mov(bc_src, NE * SA2A - 1)
                            g.wait_ge(S("post"), P_TOTAL)
                            for hq in range(NHALF + 1):
                                if hq < NHALF:
                                    for gg in range(4):
                                        gi = 4 * hq + gg
                                        g.wait_ge(S("cme"), gi + 1)
                                        g.wait_ge(S("cidx"), 8 * (gi + 1))
                                        g.indirect_dma_start(
                                            xe_t[gi % 2][:], None, rcv_d,
                                            IOA(srcq_sb[:, gi:gi + 1], 0),
                                            bounds_check=bc_src,
                                            oob_is_err=False).then_inc(S(f"cg{gi % 2}"), 16)
                                if hq >= 1:
                                    for ts in range(4):
                                        tsg = 4 * (hq - 1) + ts
                                        g.wait_ge(S("cyv"), 2 * tsg + 2)
                                        g.indirect_dma_start(
                                            ysnd_d, IOA(srcq_sb[:, tsg:tsg + 1], 0),
                                            ysnd_t[tsg % 4][:], None,
                                            bounds_check=bc_src,
                                            oob_is_err=False).then_inc(S(f"cys{tsg % 4}"), 16)
                            for pp in range(4):
                                g.wait_ge(S(f"cys{pp}"), 16 * 8)
                            with allow_ici_all_to_all():
                                g.collective_compute(
                                    "AllToAll", ALU.bypass,
                                    replica_groups=[list(range(8))],
                                    ins=[ysnd_d.opt()],
                                    outs=[yrcv_d[0:NE * SA2A, :].opt()]).then_inc(S("caa"), 1)

            with contextlib.ExitStack() as c2ctx:
                CM2, _c2off = _man_alloc(sweep_base)
                ot_t = [CM2(f"ot{i}", [128, D], F32) for i in range(4)]
                yg_t = [CM2(f"yg{i}", [128, D], BF16) for i in range(4)]
                fin_t = [CM2(f"fint{i}", [128, D], F32) for i in range(4)]
                assert _c2off[0] <= window_base, (_c2off[0], window_base)
                with nc.Block() as blk:
                    @blk.gpsimd
                    def _(g):
                        with g.register("bc_f") as bc_f:
                            g.reg_mov(bc_f, NE * SA2A + 127)
                            g.wait_ge(S("caa"), 1)
                            for t2 in range(NT):
                                if t2 >= 4:
                                    g.wait_ge(S("ffa"), t2 - 3)
                                g.indirect_dma_start(
                                    yg_t[t2 % 4][:], None, yrcv_d,
                                    IOA(fslot_sb[:, t2:t2 + 1], 0),
                                    bounds_check=bc_f,
                                    oob_is_err=False).then_inc(S(f"fyg{t2 % 4}"), 16)

                    @blk.sync
                    def _(sy):
                        for pp in range(2):
                            sy.wait_ge(S(f"cg{pp}"), 16 * 16)
                        for pp in range(4):
                            sy.wait_ge(S(f"cys{pp}"), 16 * 8)
                        for pp in range(4):
                            sy.wait_ge(S(f"x1st{pp}"), 16 * (NT // 4))
                        for t2 in range(NT):
                            if t2 >= 4:
                                sy.wait_ge(S("ffa"), t2 - 3)
                            sy.dma_start(ot_t[t2 % 4][:],
                                         otile[t2]).then_inc(S(f"fld{t2 % 4}"), 16)
                            if t2 >= 2:
                                t3 = t2 - 2
                                sy.wait_ge(S("ffa"), t3 + 1)
                                sy.dma_start(otile[t3],
                                             fin_t[t3 % 4][:]).then_inc(S(f"fo{t3 % 4}"), 16)
                        for t3 in (NT - 2, NT - 1):
                            sy.wait_ge(S("ffa"), t3 + 1)
                            sy.dma_start(otile[t3],
                                         fin_t[t3 % 4][:]).then_inc(S(f"fo{t3 % 4}"), 16)

                    @blk.vector
                    def _(v):
                        for pp in range(2):
                            v.wait_ge(S(f"cg{pp}"), 16 * 16)
                        for pp in range(4):
                            v.wait_ge(S(f"cys{pp}"), 16 * 8)
                        for t2 in range(NT):
                            v.wait_ge(S(f"fld{t2 % 4}"), 16 * (t2 // 4 + 1))
                            v.wait_ge(S(f"fyg{t2 % 4}"), 16 * (t2 // 4 + 1))
                            if t2 >= 4:
                                v.wait_ge(S(f"fo{t2 % 4}"), 16 * (t2 // 4))
                            v.tensor_tensor(fin_t[t2 % 4][:], ot_t[t2 % 4][:],
                                            yg_t[t2 % 4][:],
                                            ALU.add).then_inc(S("ffa"), 1)

        with nc.Block() as blk:
            @blk.sync
            def _(sy):
                for pp in range(4):
                    sy.wait_ge(S(f"x1st{pp}"), 16 * (NT // 4))
                for pp in range(8):
                    sy.wait_ge(S(f"poolsc{pp}"), 16 * (NT // 8))
                if do_c:
                    for pp in range(4):
                        sy.wait_ge(S(f"fo{pp}"), 16 * (NT // 4))
                if debug:
                    sy.dma_start(dbg_sml[:, 0:8], bases_bc[:]).then_inc(S("fin"), 16)
                    sy.dma_start(dbg_sml[0:8, 8:16], cnt_all[:]).then_inc(S("fin"), 16)
                    sy.dma_start(dbg_sml[0:1, 16:24], tot_sb[:]).then_inc(S("fin"), 16)
                    sy.dma_start(dbg_sml[0:1, 24:32],
                                 brun[0:1, 0:8]).then_inc(S("fin"), 16)
                    sy.dma_start(dbg_win, winfull[:]).then_inc(S("fin"), 16)
                    sy.dma_start(dbg2, dbgt[:]).then_inc(S("fin"), 16)
                    sy.wait_ge(S("fin"), 96)

    return nc


_NCACHE = {}


def _consts():
    idbf = np.eye(128, dtype=ml_dtypes.bfloat16)
    idf32 = np.eye(128, dtype=np.float32)
    lt = (np.arange(128)[:, None] < np.arange(128)[None, :]).astype(np.float32)
    io8 = np.broadcast_to(np.arange(8, dtype=np.float32), (128, 8)).copy()
    on1 = np.ones((1, 128), dtype=np.float32)
    s = np.arange(32)[:, None, None]
    j = np.arange(32)[None, :, None]
    k = np.arange(32)[None, None, :]
    wm = np.where((j <= s) & (k <= s), 0.0, -1.0e30).astype(np.float32).reshape(32, 1024)
    rrep = (np.arange(32)[:, None] == (np.arange(128)[None, :] % 32)).astype(np.float32)
    rjs = np.zeros((32, 8, 128), dtype=np.float32)
    for c in range(8):
        rjs[:, c, :] = (np.arange(32)[:, None] == (4 * c + np.arange(128)[None, :] // 32))
    return dict(idbf=idbf, idf32=idf32, ltm=lt, iota8=io8,
                iom9c=(io8 - 9.0).astype(np.float32),
                io8p16c=(io8 + 16.0).astype(np.float32), ones1=on1, wmask=wm,
                rrep=rrep, rj=rjs.reshape(32, 1024).astype(np.float32))


def kernel(L0_attn_x, L0_attn_W_Q, L0_attn_W_K, L0_attn_W_V, L0_attn_W_Kp,
           L0_attn_W_O, L0_attn_ln_gamma, L0_attn_ln_beta, L0_ffn_Wr,
           L0_ffn_W1, L0_ffn_W2, _debug=False, _do_c=True, _c_stage=3):
    x = np.ascontiguousarray(np.asarray(L0_attn_x, dtype=np.float32))
    B, Sq, _ = x.shape
    xf = x.reshape(B * Sq, D)
    shared = dict(
        wq=np.ascontiguousarray(np.asarray(L0_attn_W_Q, np.float32)),
        wk=np.ascontiguousarray(np.asarray(L0_attn_W_K, np.float32)),
        wkp=np.ascontiguousarray(np.asarray(L0_attn_W_Kp, np.float32)),
        wv=np.ascontiguousarray(np.asarray(L0_attn_W_V, np.float32)),
        wo=np.ascontiguousarray(np.asarray(L0_attn_W_O, np.float32)),
        gam=np.ascontiguousarray(np.asarray(L0_attn_ln_gamma, np.float32)).reshape(1, D),
        bet=np.ascontiguousarray(np.asarray(L0_attn_ln_beta, np.float32)).reshape(1, D),
        wr=np.ascontiguousarray(np.asarray(L0_ffn_Wr, np.float32)),
        **_consts(),
    )
    w1a = np.asarray(L0_ffn_W1, np.float32)
    w2a = np.asarray(L0_ffn_W2, np.float32)
    in_maps = []
    for c in range(8):
        winap = np.zeros((128, 1), np.float32)
        if c % 2 == 1:
            winap[96:128] = 1.0
        m = dict(shared)
        m["xsh"] = np.ascontiguousarray(xf[4096 * c:4096 * (c + 1)])
        m["winx"] = np.ascontiguousarray(x[c // 2, Sq - 32:Sq, :])
        m["winap"] = winap
        m["coresel"] = (np.arange(8) < c).astype(np.float32).reshape(8, 1)
        m["w1my"] = np.ascontiguousarray(w1a[c])
        m["w2my"] = np.ascontiguousarray(w2a[c])
        m["myexp"] = np.broadcast_to(
            (np.arange(8) == c).astype(np.float32), (8, 8)).copy()
        in_maps.append(m)

    key = (_debug, _do_c, _c_stage)
    if key not in _NCACHE:
        _NCACHE[key] = build_nc(debug=_debug, do_c=_do_c, c_stage=_c_stage)
    nc = _NCACHE[key]
    res = run_bass_kernel_spmd(nc, in_maps, core_ids=list(range(8)))
    kernel._last_results = res.results
    outs = np.concatenate([np.asarray(r["out"]) for r in res.results], axis=0)
    return outs.reshape(B, Sq, D).astype(np.float32)


# version 19 (final)
# speedup vs baseline: 1.0100x; 1.0096x over previous
import contextlib
import numpy as np
import ml_dtypes
import concourse.bass as bass
import concourse.mybir as mybir
from concourse.bass_utils import run_bass_kernel_spmd
from concourse.aot_env import allow_ici_all_to_all

F32 = mybir.dt.float32
BF16 = mybir.dt.bfloat16
U32 = mybir.dt.uint32
AF = mybir.ActivationFunctionType
ALU = mybir.AluOpType
AX = mybir.AxisListType
IOA = bass.IndirectOffsetOnAxis

NT = 32
D = 1024
NE = 8
SA2A = 1152
CAP = 4096
EPS_LN = 1e-5
SENT = float(2 ** 16)


class Script:

    def __init__(self, sem, dsem=None):
        self.sem = sem
        self.dsem = dsem
        self.ops = []

    def add(self, eng, fn, inc=1):
        self.ops.append((eng, fn, inc))

    def total(self):
        return sum(i for (_, _, i) in self.ops if i == 1)

    def emit(self, name, eng):
        self.emit_range(name, eng, 0, len(self.ops))

    def emit_range(self, name, eng, lo, hi):
        cum_c = 0
        cum_d = 0
        for idx, (en, fn, inc) in enumerate(self.ops):
            if en == name and lo <= idx < hi:
                if cum_c:
                    eng.wait_ge(self.sem, cum_c)
                if cum_d:
                    eng.wait_ge(self.dsem, cum_d)
                h = fn(eng)
                assert h is not None, f"script op on {name} returned no handle"
                h.then_inc(self.sem if inc == 1 else self.dsem, inc)
            if inc == 1:
                cum_c += inc
            else:
                cum_d += inc


def build_nc(debug=False, do_c=True, c_stage=3):
    nc = bass.Bass("TRN2", target_bir_lowering=False)

    xsh = nc.dram_tensor("xsh", [NT * 128, D], F32, kind="ExternalInput").ap()
    winx = nc.dram_tensor("winx", [32, D], F32, kind="ExternalInput").ap()
    winap = nc.dram_tensor("winap", [128, 1], F32, kind="ExternalInput").ap()
    coresel = nc.dram_tensor("coresel", [8, 1], F32, kind="ExternalInput").ap()
    wq = nc.dram_tensor("wq", [D, D], F32, kind="ExternalInput").ap()
    wk = nc.dram_tensor("wk", [D, D], F32, kind="ExternalInput").ap()
    wkp = nc.dram_tensor("wkp", [D, D], F32, kind="ExternalInput").ap()
    wv = nc.dram_tensor("wv", [D, D], F32, kind="ExternalInput").ap()
    wo = nc.dram_tensor("wo", [D, D], F32, kind="ExternalInput").ap()
    gam = nc.dram_tensor("gam", [1, D], F32, kind="ExternalInput").ap()
    bet = nc.dram_tensor("bet", [1, D], F32, kind="ExternalInput").ap()
    wr = nc.dram_tensor("wr", [D, 8], F32, kind="ExternalInput").ap()
    w1my = nc.dram_tensor("w1my", [D, 4096], F32, kind="ExternalInput").ap()
    w2my = nc.dram_tensor("w2my", [4096, D], F32, kind="ExternalInput").ap()
    myexp = nc.dram_tensor("myexp", [8, 8], F32, kind="ExternalInput").ap()
    idbf = nc.dram_tensor("idbf", [128, 128], BF16, kind="ExternalInput").ap()
    idf32 = nc.dram_tensor("idf32", [128, 128], F32, kind="ExternalInput").ap()
    ltm = nc.dram_tensor("ltm", [128, 128], F32, kind="ExternalInput").ap()
    iota8 = nc.dram_tensor("iota8", [128, 8], F32, kind="ExternalInput").ap()
    iom9c = nc.dram_tensor("iom9c", [128, 8], F32, kind="ExternalInput").ap()
    io8p16c = nc.dram_tensor("io8p16c", [128, 8], F32, kind="ExternalInput").ap()
    ones1 = nc.dram_tensor("ones1", [1, 128], F32, kind="ExternalInput").ap()
    wmask = nc.dram_tensor("wmask", [32, 1024], F32, kind="ExternalInput").ap()
    rrep = nc.dram_tensor("rrep", [32, 128], F32, kind="ExternalInput").ap()
    rj = nc.dram_tensor("rj", [32, 1024], F32, kind="ExternalInput").ap()

    out = nc.dram_tensor("out", [NT * 128, D], F32, kind="ExternalOutput").ap()
    dbg = dict(kind="ExternalOutput") if debug else {}
    snd_d = nc.dram_tensor("snd_d", [NE * SA2A, D], BF16, **dbg).ap()
    rcv_d = nc.dram_tensor("rcv_d", [NE * SA2A, D], BF16).ap()
    ysnd_d = nc.dram_tensor("ysnd_d", [NE * SA2A, D], BF16).ap()
    yrcv_d = nc.dram_tensor("yrcv_d", [NE * SA2A + 128, D], BF16, **dbg).ap()
    cc_in = nc.dram_tensor("cc_in", [1, 128], F32)
    cc_out = nc.dram_tensor("cc_out", [8, 128], F32)
    if debug:
        dbg_sml = nc.dram_tensor("dbg_sml", [128, 32], F32, kind="ExternalOutput").ap()
        dbg2 = nc.dram_tensor("dbg2", [128, 8], F32, kind="ExternalOutput").ap()
        dbg_win = nc.dram_tensor("dbg_win", [128, D], F32, kind="ExternalOutput").ap()

    with contextlib.ExitStack() as ctx:
        E = ctx.enter_context

        id_bf = E(nc.sbuf_tensor("id_bf", [128, 128], BF16))
        id_f = E(nc.sbuf_tensor("id_f", [128, 128], F32))
        lt_sb = E(nc.sbuf_tensor("lt_sb", [128, 128], F32))
        io8_sb = E(nc.sbuf_tensor("io8_sb", [128, 8], F32))
        iom9 = E(nc.sbuf_tensor("iom9", [128, 8], F32))
        io8p16 = E(nc.sbuf_tensor("io8p16", [128, 8], F32))
        on1_sb = E(nc.sbuf_tensor("on1_sb", [1, 128], F32))
        onesc = E(nc.sbuf_tensor("onesc", [128, 1], F32))
        pidx = E(nc.sbuf_tensor("pidx", [128, 1], F32))
        eps_t = E(nc.sbuf_tensor("eps_t", [128, 1], F32))
        tiny_t = E(nc.sbuf_tensor("tiny_t", [128, 1], F32))
        gam_bc = E(nc.sbuf_tensor("gam_bc", [128, D], F32))
        bet_bc = E(nc.sbuf_tensor("bet_bc", [128, D], F32))
        wr_sb = E(nc.sbuf_tensor("wr_sb", [128, 8, 8], F32))
        wap_sb = E(nc.sbuf_tensor("wap_sb", [128, 1], F32))
        csel_sb = E(nc.sbuf_tensor("csel_sb", [8, 1], F32))
        winfull = E(nc.sbuf_tensor("winfull", [128, D], F32))
        bases_bc = E(nc.sbuf_tensor("bases_bc", [128, 8], F32))
        cnt_all = E(nc.sbuf_tensor("cnt_all", [8, 8], F32))
        basel = E(nc.sbuf_tensor("basel", [8, 8], F32))
        bcol = E(nc.sbuf_tensor("bcol", [8, 1], F32))
        brow = E(nc.sbuf_tensor("brow", [1, 8], F32))
        myx_sb = E(nc.sbuf_tensor("myx_sb", [8, 8], F32))
        tot_sb = E(nc.sbuf_tensor("tot_sb", [1, 8], F32))
        brun = E(nc.sbuf_tensor("brun", [1, 8], F32))
        tsl = E(nc.sbuf_tensor("tsl", [1, 8], F32))
        zrow = E(nc.sbuf_tensor("zrow", [1, 128], F32))
        zbig = E(nc.sbuf_tensor("zbig", [128, 1024], BF16))
        slot_sb = E(nc.sbuf_tensor("slot_sb", [128, NT], U32))
        fslot_sb = E(nc.sbuf_tensor("fslot_sb", [128, NT], U32))
        srcq_sb = E(nc.sbuf_tensor("srcq_sb", [128, 32], U32))
        dbgt = E(nc.sbuf_tensor("dbgt", [128, 8], F32))

        sem_names = [
            "w", "post", "sX", "sGd",
            "xld0", "xld1", "xld2", "xld3", "dve1", "act1", "dvexn", "w31",
            "acta", "act2", "dve2", "dvean", "dvgb", "x1st0", "x1st1",
            "x1st2", "x1st3", "act3a", "dvev3", "act3", "dvexn2", "petr",
            "aev", "pelg", "dvemx", "actsm", "dvpr", "pecnt", "petot",
            "dvbr", "dverk", "dvxb", "x31",
            "poolsc0", "poolsc1", "poolsc2", "poolsc3",
            "poolsc4", "poolsc5", "poolsc6", "poolsc7",
            "zsnd", "wdm", "wdm2", "cidx", "cg0", "cg1", "cme", "cxt", "cxv",
            "chm", "cha", "cym", "cyv", "cys0", "cys1", "cys2", "cys3", "caa",
            "fld0", "fld1", "fld2", "fld3", "fms", "fyg0", "fyg1", "fyg2",
            "fyg3", "ffa", "fo0", "fo1", "fo2", "fo3",
            "fin", "sGp", "postd", "dbgs", "wdx0", "wdx1", "wdx2",
        ]
        sems = {n: E(nc.semaphore(n)) for n in sem_names}

        def S(n):
            return sems[n]

        n_sX = [0]
        n_sGd = [0]

        def _man_alloc(base):
            off = [base]

            def alloc(name, shape, dtype):
                nbytes = 1
                for s in shape[1:]:
                    nbytes *= s
                nbytes *= 2 if dtype == BF16 else 4
                off[0] = (off[0] + 31) // 32 * 32
                h = nc.alloc_sbuf_tensor_at(name, shape, dtype, offset=off[0])
                off[0] += nbytes
                return h.ap()
            return alloc, off

        sweep_base = nc.sbuf_base
        sctx = E(contextlib.ExitStack())
        Se = sctx.enter_context
        NB = 2
        xin = [Se(nc.sbuf_tensor(f"xin{i}", [128, D], F32)) for i in range(NB)]
        xn_t = [Se(nc.sbuf_tensor(f"xn{i}", [128, D], F32)) for i in range(NB)]
        pre31 = Se(nc.sbuf_tensor("pre31", [128, D], F32))
        x1_t = [Se(nc.sbuf_tensor(f"x1_{i}", [128, D], F32)) for i in range(NB)]
        xn2_t = [Se(nc.sbuf_tensor(f"xn2_{i}", [128, D], F32)) for i in range(NB)]
        xn2f = [Se(nc.sbuf_tensor(f"xn2f{i}", [128, 8, 128], F32)) for i in range(NB)]
        NXB = 6
        xb_t = [Se(nc.sbuf_tensor(f"xb_{i}", [128, D], BF16)) for i in range(NXB)]
        sq_scr = Se(nc.sbuf_tensor("sq_scr", [128, D], F32))
        sv = [Se(nc.sbuf_tensor(f"sv{i}", [128, 16], F32)) for i in range(NB)]
        rmur2 = [Se(nc.sbuf_tensor(f"rmur2_{i}", [128, 4], F32)) for i in range(NB)]
        e8_t = [Se(nc.sbuf_tensor(f"e8_{i}", [128, 8], F32)) for i in range(NB)]
        oh_t = [Se(nc.sbuf_tensor(f"oh_{i}", [128, 8], F32)) for i in range(NB)]
        cnd_t = [Se(nc.sbuf_tensor(f"cnd{i}", [128, 8], F32)) for i in range(NB)]
        rkm_t = [Se(nc.sbuf_tensor(f"rkm{i}", [128, 8], F32)) for i in range(NB)]
        sl_t = [Se(nc.sbuf_tensor(f"sl{i}", [128, 4], F32)) for i in range(NB)]
        pctx = contextlib.ExitStack()
        tr8 = pctx.enter_context(nc.psum_tensor("tr8", [128, 8, 128], F32))
        lgb = pctx.enter_context(nc.psum_tensor("lgb", [128, 16], F32))
        cntt = pctx.enter_context(nc.psum_tensor("cntt", [128, 512], F32))
        window_base = nc.sbuf_base

        with contextlib.ExitStack() as wctx:
            We = wctx.enter_context
            wqb = We(nc.sbuf_tensor("wqb", [128, 8, 1024], BF16))
            wkb = We(nc.sbuf_tensor("wkb", [128, 8, 1024], BF16))
            wkpb = wqb
            wvb = wkb
            wob = wqb
            gam_r = We(nc.sbuf_tensor("gam_r", [1, D], F32))
            bet_r = We(nc.sbuf_tensor("bet_r", [1, D], F32))
            winx_sb = We(nc.sbuf_tensor("winx_sb", [32, D], F32))
            wn = We(nc.sbuf_tensor("wn", [32, D], F32))
            wnT = We(nc.sbuf_tensor("wnT", [128, 8, 32], BF16))
            q_sb = We(nc.sbuf_tensor("q_sb", [32, D], F32))
            k_sb = We(nc.sbuf_tensor("k_sb", [32, D], F32))
            kp_sb = We(nc.sbuf_tensor("kp_sb", [32, D], F32))
            v_sb = We(nc.sbuf_tensor("v_sb", [32, D], F32))
            qt_bf = We(nc.sbuf_tensor("qt_bf", [128, 8, 32], BF16))
            kt_f = We(nc.sbuf_tensor("kt_f", [128, 8, 32], F32))
            kpt_f = We(nc.sbuf_tensor("kpt_f", [128, 8, 32], F32))
            rrep_sb = We(nc.sbuf_tensor("rrep_sb", [32, 128], F32))
            rj_sb = We(nc.sbuf_tensor("rj_sb", [32, 1024], F32))
            vp4 = We(nc.sbuf_tensor("vp4", [128, D], F32))
            vj_s = We(nc.sbuf_tensor("vj_s", [128, D], F32))
            vv_b = We(nc.sbuf_tensor("vv_b", [128, 8, 1024], BF16))
            kkp_b = We(nc.sbuf_tensor("kkp_b", [128, 1024], BF16))
            wms = We(nc.sbuf_tensor("wms", [32, 1024], F32))
            a_sb = We(nc.sbuf_tensor("a_sb", [32, 1024], F32))
            p_bf = We(nc.sbuf_tensor("p_bf", [32, 1024], BF16))
            pt_bf = We(nc.sbuf_tensor("pt_bf", [128, 8, 32], BF16))
            z_sb = We(nc.sbuf_tensor("z_sb", [32, 1024], F32))
            zb_sb = We(nc.sbuf_tensor("zb_sb", [32, 1024], BF16))
            zt_bf = We(nc.sbuf_tensor("zt_bf", [128, 8, 128], BF16))
            wsc1 = We(nc.sbuf_tensor("wsc1", [32, 1024], F32))
            wsc2 = We(nc.sbuf_tensor("wsc2", [32, 16], F32))
            wsm = We(nc.sbuf_tensor("wsm", [32, 1], F32))
            wsm2 = We(nc.sbuf_tensor("wsm2", [32, 1], F32))
            wsm3 = We(nc.sbuf_tensor("wsm3", [32, 1], F32))
            wps = [We(nc.psum_tensor(f"wps{i}", [128, 512], F32)) for i in range(3)]
            wpb = [We(nc.psum_tensor("wpb0", [128, 8, 32], BF16))]
            

            with nc.Block() as blk:
                @blk.sync
                def _(sy):
                    loads = [
                        (id_bf[:], idbf), (id_f[:], idf32), (lt_sb[:], ltm),
                        (io8_sb[:], iota8), (iom9[:], iom9c), (io8p16[:], io8p16c),
                        (on1_sb[:], ones1),
                        (gam_r[:], gam), (bet_r[:], bet),
                        (wr_sb[:], wr.rearrange("(c p) e -> p c e", p=128)),
                        (winx_sb[:], winx), (wap_sb[:], winap), (myx_sb[:], myexp),
                        (csel_sb[:], coresel), (wms[:], wmask),
                        (rrep_sb[:], rrep), (rj_sb[:], rj),
                    ]
                    for dst, src in loads:
                        sy.dma_start(dst, src).then_inc(S("sX"), 16)
                        n_sX[0] += 16

                @blk.gpsimd
                def _(g):
                    g.iota(pidx[:], [[0, 1]], base=0, channel_multiplier=1,
                           allow_small_or_imprecise_dtypes=True)
                    g.memset(zbig[:], 0.0).then_inc(S("sGp"), 1)
                    g.memset(zrow[:], 0.0).then_inc(S("sGp"), 1)
                    g.memset(brun[:], 0.0)
                    g.memset(eps_t[:], EPS_LN)
                    g.memset(tiny_t[:], 1e-38)
                    g.memset(onesc[:], 1.0)
                    g.memset(winfull[:], 0.0)
                    g.memset(zt_bf[:], 0.0)
                    if debug:
                        g.memset(dbgt[:], 0.0)
                    g.wait_ge(S("sGp"), 2)
                    g.dma_start(cc_in.ap(), zrow[:]).then_inc(S("sGd"), 16)
                    for wsrc, wdst in [(wq, wqb), (wk, wkb)]:
                        g.dma_start(wdst[:], wsrc.rearrange("(c p) f -> p c f", p=128)
                                    ).then_inc(S("sGd"), 16)
                    n_sGd[0] += 3 * 16

            W = Script(S("w"), S("wdx0"))

            def mm(t, ps, lhsT, rhs, start=True, stop=True, sg=False):
                return t.matmul(ps, lhsT, rhs, start=start, stop=stop,
                                skip_group_check=sg)

            for half in range(2):
                W.add("tensor", lambda t, h=half: mm(t, wps[0][:, 0:512], on1_sb[:],
                                                     gam_r[0:1, 512 * h:512 * h + 512]))
                W.add("vector", lambda v, h=half: v.tensor_copy(
                    gam_bc[:, 512 * h:512 * h + 512], wps[0][:, 0:512]))
                W.add("tensor", lambda t, h=half: mm(t, wps[1][:, 0:512], on1_sb[:],
                                                     bet_r[0:1, 512 * h:512 * h + 512]))
                W.add("vector", lambda v, h=half: v.tensor_copy(
                    bet_bc[:, 512 * h:512 * h + 512], wps[1][:, 0:512]))

            W.add("scalar", lambda a: a.activation(wsc1[:], winx_sb[:], AF.Square,
                                                   accum_out=wsm[:]))
            W.add("scalar", lambda a: a.activation(wsc1[:], winx_sb[:], AF.Identity,
                                                   accum_out=wsm2[:]))
            W.add("vector", lambda v: v.tensor_scalar(wsm2[:], wsm2[:], 1.0 / D,
                                                      None, ALU.mult))
            W.add("vector", lambda v: v.tensor_scalar(wsm[:], wsm[:], 1.0 / D,
                                                      None, ALU.mult))
            W.add("vector", lambda v: v.scalar_tensor_tensor(
                wsm3[:], wsm2[:], wsm2[0:32, :], wsm[:], ALU.mult, ALU.subtract))
            W.add("scalar", lambda a: a.activation(wsm[:], wsm3[:], AF.Ln,
                                                   bias=eps_t[0:32, :], scale=-1.0))
            W.add("scalar", lambda a: a.activation(wsm[:], wsm[:], AF.Exp,
                                                   bias=0.0, scale=-0.5))
            W.add("vector", lambda v: v.tensor_scalar(wsm2[:], wsm2[:], wsm[0:32, :],
                                                      -1.0, ALU.mult, ALU.mult))
            W.add("vector", lambda v: v.tensor_scalar(wn[:], winx_sb[:], wsm[0:32, :],
                                                      wsm2[0:32, :], ALU.mult, ALU.add))

            for c in range(8):
                W.add("tensor", lambda t, c=c: t.transpose(
                    wps[2][:, 32 * c:32 * c + 32], wn[0:32, 128 * c:128 * c + 128],
                    id_f[0:32, 0:32]))
            W.add("vector", lambda v: v.tensor_copy(wnT[:], wps[2][:, 0:256]))

            def wait_wdma(t):
                t.wait_ge(S("sGd"), n_sGd[0])
                return t.nop()
            W.add("tensor", wait_wdma)

            def _proj(wbuf, dest):
                for fh in range(2):
                    for dc in range(8):
                        W.add("tensor", lambda t, fh=fh, dc=dc, wbuf=wbuf: mm(
                            t, wps[2][0:32, 0:512], wnT[:, dc, :],
                            wbuf[:, dc, 512 * fh:512 * fh + 512],
                            start=(dc == 0), stop=(dc == 7)))
                    W.add("vector", lambda v, fh=fh, dest=dest: v.tensor_copy(
                        dest[:, 512 * fh:512 * fh + 512], wps[2][0:32, 0:512]))

            wmarks = []
            _proj(wqb, q_sb)
            wmarks.append(W.total())
            _proj(wkb, k_sb)
            wmarks.append(W.total())
            W.add("tensor", lambda t: (t.wait_ge(S("wdx0"), 16), t.nop())[1])
            _proj(wkpb, kp_sb)
            wmarks.append(W.total())
            W.add("tensor", lambda t: (t.wait_ge(S("wdx1"), 16), t.nop())[1])
            _proj(wvb, v_sb)

            for dest in (q_sb, k_sb, kp_sb):
                W.add("vector", lambda v, d=dest: v.tensor_tensor(
                    wsc1[:], d[:], d[:], ALU.mult))
                W.add("vector", lambda v: v.tensor_reduce(
                    wsc2[:], wsc1[0:32, :].rearrange("p (h d) -> p h d", d=64),
                    AX.X, ALU.add))
                W.add("scalar", lambda a: a.activation(wsc2[:], wsc2[:], AF.Ln,
                                                       bias=tiny_t[0:32, :], scale=1.0))
                W.add("scalar", lambda a: a.activation(wsc2[:], wsc2[:], AF.Exp,
                                                       bias=0.0, scale=0.5))
                W.add("vector", lambda v: v.tensor_scalar(wsc2[:], wsc2[:], 1e-7,
                                                          None, ALU.add))
                W.add("vector", lambda v: v.reciprocal(wsc2[:], wsc2[:]))
                for h in range(16):
                    W.add("vector", lambda v, h=h, d=dest: v.tensor_scalar(
                        d[:, 64 * h:64 * h + 64], d[:, 64 * h:64 * h + 64],
                        wsc2[0:32, h:h + 1], None, ALU.mult))

            for c in range(8):
                W.add("tensor", lambda t, c=c: t.transpose(
                    wps[2][:, 32 * c:32 * c + 32], q_sb[0:32, 128 * c:128 * c + 128],
                    id_f[0:32, 0:32]))
            W.add("scalar", lambda a: a.activation(
                qt_bf[:], wps[2][:, 0:256], AF.Identity, bias=0.0, scale=0.125))
            for src, dst in [(k_sb, kt_f), (kp_sb, kpt_f)]:
                for c in range(8):
                    W.add("tensor", lambda t, c=c, src=src: t.transpose(
                        wps[2][:, 32 * c:32 * c + 32], src[0:32, 128 * c:128 * c + 128],
                        id_f[0:32, 0:32]))
                W.add("vector", lambda v, dst=dst: v.tensor_copy(
                    dst[:], wps[2][:, 0:256]))

            for fh in range(2):
                W.add("tensor", lambda t, fh=fh: mm(t, wps[2][:, 0:512], rrep_sb[:],
                                                    v_sb[0:32, 512 * fh:512 * fh + 512]))
                W.add("vector", lambda v, fh=fh: v.tensor_copy(
                    vp4[:, 512 * fh:512 * fh + 512], wps[2][:, 0:512]))
            for c in range(8):
                for fh in range(2):
                    W.add("tensor", lambda t, c=c, fh=fh: mm(
                        t, wps[2][:, 0:512], rj_sb[0:32, 128 * c:128 * c + 128],
                        v_sb[0:32, 512 * fh:512 * fh + 512]))
                    W.add("vector", lambda v, fh=fh: v.tensor_copy(
                        vj_s[:, 512 * fh:512 * fh + 512], wps[2][:, 0:512]))
                W.add("vector", lambda v, c=c: v.tensor_tensor(
                    vv_b[:, c, :], vj_s[:], vp4[:], ALU.mult))

            zps_banks = (wps[0], wps[1])
            for hc in range(8):
                W.add("vector", lambda v, hc=hc: v.tensor_tensor(
                    kkp_b[:].rearrange("p (j k) -> p j k", k=32),
                    bass.AP(kt_f, hc * 32, [[256, 128], [1, 32], [0, 32]]),
                    bass.AP(kpt_f, hc * 32, [[256, 128], [0, 32], [1, 32]]),
                    ALU.mult))
                for u in range(2):
                    h = 2 * hc + u
                    for fh in range(2):
                        W.add("tensor", lambda t, u=u, hc=hc, fh=fh: mm(
                            t, wps[2][0:32, 0:512],
                            qt_bf[64 * u:64 * u + 64, hc, :],
                            kkp_b[64 * u:64 * u + 64, 512 * fh:512 * fh + 512]))
                        W.add("vector", lambda v, fh=fh: v.tensor_tensor(
                            a_sb[:, 512 * fh:512 * fh + 512], wps[2][0:32, 0:512],
                            wms[:, 512 * fh:512 * fh + 512], ALU.add))
                    W.add("vector", lambda v: v.reduce_max(wsm[:], a_sb[:], AX.X))
                    W.add("vector", lambda v: v.tensor_scalar(wsm[:], wsm[:], -1.0,
                                                              None, ALU.mult))
                    W.add("scalar", lambda a: a.activation(wsc1[:], a_sb[:], AF.Exp,
                                                           bias=wsm[0:32, :], scale=1.0,
                                                           accum_out=wsm2[:]))
                    W.add("vector", lambda v: v.reciprocal(wsm2[:], wsm2[:]))
                    W.add("vector", lambda v: v.tensor_scalar(p_bf[:], wsc1[:],
                                                              wsm2[0:32, :], None,
                                                              ALU.mult))
                    for c in range(8):
                        W.add("tensor", lambda t, c=c: t.transpose(
                            wpb[0][:, c, :], p_bf[0:32, 128 * c:128 * c + 128],
                            id_bf[0:32, 0:32]))
                    W.add("vector", lambda v: v.tensor_copy(pt_bf[:], wpb[0][:]))
                    for c in range(8):
                        W.add("tensor", lambda t, c=c, h=h: mm(
                            t, zps_banks[h // 8][0:32, 64 * (h % 8):64 * (h % 8) + 64],
                            pt_bf[:, c, :], vv_b[:, c, 64 * h:64 * h + 64],
                            start=(c == 0), stop=(c == 7), sg=True))

            for b2 in range(2):
                W.add("vector", lambda v, b2=b2: v.tensor_copy(
                    z_sb[:, 512 * b2:512 * b2 + 512], zps_banks[b2][0:32, 0:512]))
            W.add("scalar", lambda a: a.activation(zb_sb[:], z_sb[:], AF.Identity,
                                                   bias=0.0, scale=1.0))
            for c in range(8):
                W.add("tensor", lambda t, c=c: t.transpose(
                    wpb[0][:, c, :], zb_sb[0:32, 128 * c:128 * c + 128],
                    id_bf[0:32, 0:32]))
            W.add("vector", lambda v: v.tensor_copy(zt_bf[:, :, 96:128], wpb[0][:]))
            W.add("tensor", lambda t: (t.wait_ge(S("wdx2"), 16), t.nop())[1])
            for fh in range(2):
                for dc in range(8):
                    W.add("tensor", lambda t, fh=fh, dc=dc: mm(
                        t, wps[2][:, 0:512], zt_bf[:, dc, :],
                        wob[:, dc, 512 * fh:512 * fh + 512],
                        start=(dc == 0), stop=(dc == 7)))
                W.add("vector", lambda v, fh=fh: v.tensor_copy(
                    winfull[:, 512 * fh:512 * fh + 512], wps[2][:, 0:512]))

            W_TOTAL = W.total()
            W_LEN = len(W.ops)

            def SEG(k):
                return min(W_LEN, (W_LEN * (k + 1) + 61) // 62)

            assert SEG(2 * (NXB + 2)) >= wmarks[2], (wmarks, W_LEN)

            wptr = {"scalar": 0, "vector": 0, "tensor": 0}

            def wseg(name, eng, hi):
                hi = min(hi, W_LEN)
                if wptr[name] < hi:
                    W.emit_range(name, eng, wptr[name], hi)
                    wptr[name] = hi

        w1b_r = E(nc.sbuf_tensor("w1b_r", [128, 8, 4096], BF16))
        w2b_r = E(nc.sbuf_tensor("w2b_r", [128, 32, D], BF16))

        if True:
            xtile = xsh.rearrange("(t p) f -> t p f", p=128)
            otile = out.rearrange("(t p) f -> t p f", p=128)

            with nc.Block() as blk:
                @blk.sync
                def _(sy):
                    for t in range(NT):
                        if t >= NB:
                            sy.wait_ge(S("dvgb"), t - NB + 1)
                        sy.dma_start(xin[t % NB][:], xtile[t]).then_inc(S(f"xld{t % 4}"), 16)

                @blk.scalar
                def _(a):
                    def rstd_smalls(b, scol, qcol, t1col, ocol, sem, base):
                        a.wait_ge(S(sem), base + 2)
                        a.activation(sv[b][:, t1col:t1col + 1], sv[b][:, scol:scol + 1],
                                     AF.Square, scale=1.0 / D).then_inc(S(sem), 1)
                        a.wait_ge(S(sem), base + 3)
                        a.activation(sv[b][:, t1col:t1col + 1], sv[b][:, t1col:t1col + 1],
                                     AF.Identity, bias=eps_t[:],
                                     scale=-1.0).then_inc(S(sem), 1)
                        a.wait_ge(S(sem), base + 4)
                        a.activation(sv[b][:, ocol:ocol + 1], sv[b][:, qcol:qcol + 1],
                                     AF.Ln, bias=sv[b][:, t1col:t1col + 1],
                                     scale=1.0 / D).then_inc(S(sem), 1)
                        a.wait_ge(S(sem), base + 5)
                        a.activation(sv[b][:, ocol:ocol + 1], sv[b][:, ocol:ocol + 1],
                                     AF.Exp, bias=0.0, scale=-0.5).then_inc(S(sem), 1)

                    def stage_a1(t):
                        b = t % NB
                        a.wait_ge(S(f"xld{t % 4}"), 16 * (t // 4 + 1))
                        if t >= NB:
                            a.wait_ge(S("dvexn"), 2 * (t - NB) + 2)
                        a.activation(sq_scr[:], xin[b][:], AF.Square,
                                     accum_out=sv[b][:, 1:2]).then_inc(S("act1"), 1)
                        a.wait_ge(S("act1"), 6 * t + 1)
                        a.activation(sq_scr[:], xin[b][:], AF.Identity,
                                     accum_out=sv[b][:, 0:1]).then_inc(S("act1"), 1)
                        rstd_smalls(b, 0, 1, 2, 3, "act1", 6 * t)

                    def stage_a2(t):
                        b = t % NB
                        pre = pre31 if t == NT - 1 else xn_t[b]
                        a.wait_ge(S("dvexn"), 2 * t + 2)
                        if t == NT - 1:
                            a.wait_ge(S("w31"), 1)
                        if t >= NB:
                            a.wait_ge(S("dvean"), 2 * (t - NB) + 2)
                        a.activation(sq_scr[:], pre[:], AF.Square,
                                     accum_out=rmur2[b][:, 3:4]).then_inc(S("acta"), 1)
                        if t == NT - 1:
                            a.wait_ge(S("acta"), 3 * t + 1)
                            a.activation(sq_scr[:], pre[:], AF.Identity,
                                         accum_out=rmur2[b][:, 2:3]).then_inc(S("x31"), 1)
                            a.wait_ge(S("x31"), 1)
                            a.activation(rmur2[b][:, 1:2], rmur2[b][:, 2:3],
                                         AF.Square, scale=1.0 / D).then_inc(S("x31"), 1)
                            a.wait_ge(S("x31"), 2)
                            a.activation(rmur2[b][:, 1:2], rmur2[b][:, 1:2],
                                         AF.Identity, bias=eps_t[:],
                                         scale=-1.0).then_inc(S("x31"), 1)
                            a.wait_ge(S("x31"), 3)
                            a.wait_ge(S("acta"), 3 * t + 1)
                            a.activation(rmur2[b][:, 0:1], rmur2[b][:, 3:4], AF.Ln,
                                         bias=rmur2[b][:, 1:2],
                                         scale=1.0 / D).then_inc(S("acta"), 1)
                        else:
                            a.wait_ge(S("acta"), 3 * t + 1)
                            a.activation(rmur2[b][:, 0:1], rmur2[b][:, 3:4], AF.Ln,
                                         bias=eps_t[:], scale=1.0 / D).then_inc(S("acta"), 1)
                        a.wait_ge(S("acta"), 3 * t + 2)
                        a.activation(rmur2[b][:, 0:1], rmur2[b][:, 0:1], AF.Exp,
                                     bias=0.0, scale=-0.5).then_inc(S("acta"), 1)

                    def stage_b1(u):
                        bu = u % NB
                        a.wait_ge(S("dvgb"), u + 1)
                        a.dma_start(otile[u], x1_t[bu][:]).then_inc(S(f"x1st{u % 4}"), 16)
                        if u >= NB:
                            a.wait_ge(S("dvexn2"), 2 * (u - NB) + 2)
                        a.activation(sq_scr[:], x1_t[bu][:], AF.Square,
                                     accum_out=sv[bu][:, 6:7]).then_inc(S("act3"), 1)
                        a.wait_ge(S("act3"), 6 * u + 1)
                        a.activation(sq_scr[:], x1_t[bu][:], AF.Identity,
                                     accum_out=sv[bu][:, 5:6]).then_inc(S("act3"), 1)
                        rstd_smalls(bu, 5, 6, 7, 8, "act3", 6 * u)

                    def stage_b2a(u):
                        bu = u % NB
                        a.wait_ge(S("petr"), 8 * u + 8)
                        if u >= NB:
                            a.wait_ge(S("pelg"), u - 1)
                        a.copy(xn2f[bu][:], tr8[:]).then_inc(S("aev"), 1)

                    def stage_b2b(u):
                        bu = u % NB
                        a.wait_ge(S("dvemx"), 2 * u + 2)
                        if u >= NB:
                            a.wait_ge(S("dvpr"), 8 * (u - NB) + 5)
                        a.activation(e8_t[bu][:], lgb[:, 8 * (u % 2):8 * (u % 2) + 8], AF.Exp,
                                     bias=sv[bu][:, 10:11], scale=1.0).then_inc(S("actsm"), 1)

                    a.wait_ge(S("sX"), n_sX[0])
                    a.wait_ge(S("sGp"), 1)
                    zsrc = bass.AP(zbig, 0, [[1024, 128], [0, 4], [1, 1024]])
                    for i in range(NE * SA2A // 512):
                        for dst in (snd_d, ysnd_d):
                            a.dma_start(
                                dst.rearrange("(n a p) f -> n p a f", a=4, p=128)[i],
                                zsrc).then_inc(S("zsnd"), 16)
                    a.dma_start(yrcv_d[NE * SA2A:NE * SA2A + 128, :],
                                zbig[:]).then_inc(S("zsnd"), 16)
                    for t in range(NT + 2):
                        wseg("scalar", a, SEG(2 * t))
                        if t < NT:
                            stage_a1(t)
                        if 1 <= t <= NT:
                            stage_b1(t - 1)
                        if t < NT:
                            stage_a2(t)
                        if t >= 2:
                            stage_b2a(t - 2)
                            stage_b2b(t - 2)
                        wseg("scalar", a, SEG(2 * t + 1))

                @blk.vector
                def _(v):
                    v.wait_ge(S("sX"), n_sX[0])
                    def stage_a(t):
                        b = t % NB
                        v.wait_ge(S("act1"), 6 * t + 6)
                        v.tensor_scalar(sv[b][:, 4:5], sv[b][:, 0:1], sv[b][:, 3:4],
                                        -1.0 / D, ALU.mult, ALU.mult).then_inc(S("dvexn"), 1)
                        v.wait_ge(S("dvexn"), 2 * t + 1)
                        if t >= NB:
                            v.wait_ge(S("acta"), 3 * (t - NB) + 3)
                        v.tensor_scalar(xn_t[b][:], xin[b][:], sv[b][:, 3:4],
                                        sv[b][:, 4:5], ALU.mult,
                                        ALU.add).then_inc(S("dvexn"), 1)
                        if t == NT - 1:
                            v.wait_ge(S("dvexn"), 2 * t + 2)
                            v.wait_ge(S("w"), W_TOTAL)
                            v.scalar_tensor_tensor(pre31[:], winfull[:], wap_sb[:],
                                                   xn_t[b][:], ALU.mult,
                                                   ALU.add).then_inc(S("w31"), 1)
                        v.wait_ge(S("acta"), 3 * t + 3)
                        v.wait_ge(S("w"), 8)
                        if t == NT - 1:
                            v.wait_ge(S("x31"), 1)
                            v.tensor_scalar(rmur2[b][:, 1:2], rmur2[b][:, 2:3],
                                            rmur2[b][:, 0:1], -1.0 / D, ALU.mult,
                                            ALU.mult).then_inc(S("w31"), 1)
                            v.wait_ge(S("w31"), 2)
                            v.tensor_scalar(xn_t[b][:], pre31[:], rmur2[b][:, 0:1],
                                            rmur2[b][:, 1:2], ALU.mult,
                                            ALU.add).then_inc(S("w31"), 1)
                            v.wait_ge(S("w31"), 3)
                            v.tensor_tensor(xn_t[b][:], xn_t[b][:], gam_bc[:],
                                            ALU.mult).then_inc(S("dvean"), 1)
                        else:
                            v.scalar_tensor_tensor(xn_t[b][:], xn_t[b][:],
                                                   rmur2[b][:, 0:1], gam_bc[:],
                                                   ALU.mult,
                                                   ALU.mult).then_inc(S("dvean"), 1)
                        v.wait_ge(S("dvean"), 2 * t + 1)
                        v.tensor_tensor(xn_t[b][:], xn_t[b][:], bet_bc[:],
                                        ALU.add).then_inc(S("dvean"), 1)
                        v.wait_ge(S("dvean"), 2 * t + 2)
                        if t >= NB:
                            v.wait_ge(S("act3"), 6 * (t - NB) + 2)
                            v.wait_ge(S(f"x1st{(t - NB) % 4}"),
                                      16 * ((t - NB) // 4 + 1))
                        v.tensor_tensor(x1_t[b][:], xin[b][:], xn_t[b][:],
                                        ALU.add).then_inc(S("dvgb"), 1)

                    def stage_b1(u):
                        bu = u % NB
                        v.wait_ge(S("act3"), 6 * u + 6)
                        v.tensor_scalar(sv[bu][:, 9:10], sv[bu][:, 5:6], sv[bu][:, 8:9],
                                        -1.0 / D, ALU.mult, ALU.mult).then_inc(S("dvexn2"), 1)
                        v.wait_ge(S("dvexn2"), 2 * u + 1)
                        if u >= NB:
                            v.wait_ge(S("dvxb"), u - NB + 1)
                        v.tensor_scalar(xn2_t[bu][:], x1_t[bu][:], sv[bu][:, 8:9],
                                        sv[bu][:, 9:10], ALU.mult,
                                        ALU.add).then_inc(S("dvexn2"), 1)

                    def stage_b2(t):
                        b = t % NB
                        v.wait_ge(S("pelg"), t + 1)
                        if t >= NB:
                            v.wait_ge(S("actsm"), t - NB + 1)
                        v.reduce_max(sv[b][:, 10:11], lgb[:, 8 * (t % 2):8 * (t % 2) + 8], AX.X).then_inc(S("dvemx"), 1)
                        v.wait_ge(S("dvemx"), 2 * t + 1)
                        v.tensor_scalar(sv[b][:, 10:11], sv[b][:, 10:11], -1.0,
                                        None, ALU.mult).then_inc(S("dvemx"), 1)
                        v.wait_ge(S("actsm"), t + 1)
                        v.tensor_reduce(sv[b][:, 11:12], e8_t[b][:], AX.X,
                                        ALU.add).then_inc(S("dvpr"), 1)
                        v.wait_ge(S("dvpr"), 8 * t + 1)
                        v.reciprocal(sv[b][:, 12:13], sv[b][:, 11:12]).then_inc(S("dvpr"), 1)
                        v.wait_ge(S("dvpr"), 8 * t + 2)
                        v.reduce_max(sv[b][:, 15:16], e8_t[b][:], AX.X).then_inc(S("dvpr"), 1)
                        v.wait_ge(S("dvpr"), 8 * t + 3)
                        v.tensor_scalar(sv[b][:, 13:14], sv[b][:, 15:16], sv[b][:, 12:13],
                                        None, ALU.mult).then_inc(S("dvpr"), 1)
                        v.tensor_scalar(cnd_t[b][:], e8_t[b][:], sv[b][:, 15:16],
                                        None, ALU.is_ge).then_inc(S("dvpr"), 1)
                        v.wait_ge(S("dvpr"), 8 * t + 5)
                        v.scalar_tensor_tensor(cnd_t[b][:], cnd_t[b][:], -16.0,
                                               io8p16[:], ALU.mult,
                                               ALU.add).then_inc(S("dvpr"), 1)
                        v.wait_ge(S("dvpr"), 8 * t + 6)
                        v.tensor_reduce(sv[b][:, 14:15], cnd_t[b][:], AX.X,
                                        ALU.min).then_inc(S("dvpr"), 1)
                        v.wait_ge(S("dvpr"), 8 * t + 7)
                        if t >= NB:
                            v.wait_ge(S("petot"), t - NB + 1)
                            v.wait_ge(S("pecnt"), t - NB + 1)
                        v.tensor_scalar(oh_t[b][:], io8_sb[:], sv[b][:, 14:15],
                                        None, ALU.is_equal).then_inc(S("dvpr"), 1)
                        v.wait_ge(S("dvexn2"), 2 * t + 2)
                        if t >= NXB:
                            v.wait_ge(S(f"poolsc{(t - NXB) % 8}"),
                                      16 * ((t - NXB) // 8 + 1))
                        v.tensor_scalar(xb_t[t % NXB][:], xn2_t[b][:], sv[b][:, 13:14],
                                        None, ALU.mult).then_inc(S("dvxb"), 1)

                    def stage_b3(t):
                        b = t % NB
                        v.wait_ge(S("pecnt"), t + 1)
                        v.wait_ge(S("petot"), t + 1)
                        v.tensor_tensor(rkm_t[b][:], cntt[:, 8 * (t % 2):8 * (t % 2) + 8], oh_t[b][:],
                                        ALU.mult).then_inc(S("dverk"), 1)
                        v.wait_ge(S("dverk"), 8 * t + 1)
                        v.tensor_reduce(sl_t[b][:, 0:1], rkm_t[b][:], AX.X,
                                        ALU.add).then_inc(S("dverk"), 1)
                        v.wait_ge(S("dverk"), 8 * t + 2)
                        v.scalar_tensor_tensor(sl_t[b][:, 1:2], sv[b][:, 14:15],
                                               float(SA2A), sl_t[b][:, 0:1],
                                               ALU.mult, ALU.add).then_inc(S("dverk"), 1)
                        v.wait_ge(S("dverk"), 8 * t + 3)
                        v.tensor_scalar(sl_t[b][:, 2:3], sl_t[b][:, 0:1], float(SA2A),
                                        None, ALU.is_ge).then_inc(S("dverk"), 1)
                        v.wait_ge(S("dverk"), 8 * t + 4)
                        v.scalar_tensor_tensor(sl_t[b][:, 3:4], sl_t[b][:, 2:3], SENT,
                                               sl_t[b][:, 1:2], ALU.mult,
                                               ALU.add).then_inc(S("dverk"), 1)
                        v.wait_ge(S("dverk"), 8 * t + 5)
                        v.tensor_copy(slot_sb[:, t:t + 1], sl_t[b][:, 3:4]).then_inc(S("dverk"), 1)
                        v.tensor_scalar(sl_t[b][:, 2:3], sl_t[b][:, 3:4],
                                        float(NE * SA2A), None,
                                        ALU.min).then_inc(S("dverk"), 1)
                        v.wait_ge(S("dverk"), 8 * t + 7)
                        v.tensor_copy(fslot_sb[:, t:t + 1], sl_t[b][:, 2:3]).then_inc(S("dverk"), 1)
                        v.wait_ge(S("petot"), t + 1)
                        v.tensor_copy(tsl[:], cntt[0:1, 256 + 8 * t:256 + 8 * t + 8]).then_inc(S("dvbr"), 1)
                        v.wait_ge(S("dvbr"), 2 * t + 1)
                        v.wait_ge(S("pecnt"), t + 1)
                        v.tensor_tensor(brun[:], brun[:], tsl[:],
                                        ALU.add).then_inc(S("dvbr"), 1)

                    for t in range(NT + 3):
                        wseg("vector", v, SEG(2 * t))
                        if t < NT:
                            stage_a(t)
                        if 1 <= t <= NT:
                            stage_b1(t - 1)
                        if 2 <= t <= NT + 1:
                            stage_b2(t - 2)
                        wseg("vector", v, SEG(2 * t + 1))
                        if t >= 3:
                            stage_b3(t - 3)

                @blk.tensor
                def _(t_eng):
                    t_eng.wait_ge(S("sX"), n_sX[0])
                    for it in range(NT + 2):
                        wseg("tensor", t_eng, SEG(2 * it + 2))
                        if 1 <= it <= NT:
                            v2 = it - 1
                            bv = v2 % NB
                            t_eng.wait_ge(S("aev"), v2 + 1)
                            if v2 >= 2:
                                t_eng.wait_ge(S("actsm"), v2 - 1)
                                t_eng.wait_ge(S("dvemx"), 2 * (v2 - 1))
                            h = None
                            for c in range(8):
                                h = t_eng.matmul(lgb[:, 8 * (v2 % 2):8 * (v2 % 2) + 8], xn2f[bv][:, c, :],
                                                 wr_sb[:, c, :],
                                                 start=(c == 0), stop=(c == 7),
                                                 skip_group_check=True)
                            h.then_inc(S("pelg"), 1)
                        if it < NT:
                            t = it
                            b = t % NB
                            for c in range(8):
                                if c == 0:
                                    t_eng.wait_ge(S("dvexn2"), 2 * t + 2)
                                    if t >= 1:
                                        t_eng.wait_ge(S("aev"), t)
                                t_eng.transpose(tr8[:, c, :],
                                                xn2_t[b][:, 128 * c:128 * c + 128],
                                                id_f[:]).then_inc(S("petr"), 1)
                        if it >= 2:
                            u = it - 2
                            bu = u % NB
                            t_eng.wait_ge(S("dvpr"), 8 * u + 8)
                            t_eng.wait_ge(S("dvbr"), 2 * u)
                            if u >= 2:
                                t_eng.wait_ge(S("dverk"), 8 * (u - 2) + 1)
                            t_eng.matmul(cntt[:, 8 * (u % 2):8 * (u % 2) + 8], lt_sb[:], oh_t[bu][:],
                                         start=True, stop=False,
                                         skip_group_check=True)
                            t_eng.matmul(cntt[:, 8 * (u % 2):8 * (u % 2) + 8], on1_sb[:], brun[:],
                                         start=False, stop=True,
                                         skip_group_check=True).then_inc(S("pecnt"), 1)
                            t_eng.matmul(cntt[0:1, 256 + 8 * u:256 + 8 * u + 8], onesc[:], oh_t[bu][:],
                                         start=True, stop=True,
                                         skip_group_check=True).then_inc(S("petot"), 1)

                @blk.gpsimd
                def _(g):
                    with g.register("bc_disp") as bc_disp:
                        g.reg_mov(bc_disp, NE * SA2A - 1)
                        g.wait_ge(S("w"), wmarks[0])
                        g.dma_start(wkpb[:],
                                    wkp.rearrange("(c p) f -> p c f", p=128)
                                    ).then_inc(S("wdx0"), 16)
                        g.wait_ge(S("w"), wmarks[1])
                        g.dma_start(wvb[:],
                                    wv.rearrange("(c p) f -> p c f", p=128)
                                    ).then_inc(S("wdx1"), 16)
                        g.wait_ge(S("w"), wmarks[2])
                        g.dma_start(wob[:],
                                    wo.rearrange("(c p) f -> p c f", p=128)
                                    ).then_inc(S("wdx2"), 16)
                        g.wait_ge(S("sGd"), n_sGd[0])
                        g.wait_ge(S("zsnd"), 16 * (2 * (NE * SA2A // 512) + 1))
                        for t in range(NT):
                            g.wait_ge(S("dverk"), 8 * t + 6)
                            g.wait_ge(S("dvxb"), t + 1)
                            g.indirect_dma_start(snd_d, IOA(slot_sb[:, t:t + 1], 0),
                                                 xb_t[t % NXB][:], None,
                                                 bounds_check=bc_disp,
                                                 oob_is_err=False).then_inc(S(f"poolsc{t % 8}"), 16)
                        g.wait_ge(S("w"), W_TOTAL)
                        for ss in range(8):
                            g.dma_start(w1b_r[:, :, 512 * ss:512 * ss + 512],
                                        w1my[:, 512 * ss:512 * ss + 512]
                                        .rearrange("(c p) f -> p c f", p=128)
                                        ).then_inc(S("wdm"), 16)
                        for qq in range(4):
                            g.dma_start(w2b_r[:, 8 * qq:8 * qq + 8, :],
                                        w2my[1024 * qq:1024 * qq + 1024, :]
                                        .rearrange("(c p) f -> p c f", p=128)
                                        ).then_inc(S("wdm2"), 16)

            P = Script(S("post"), S("postd"))
            P.add("vector", lambda v: v.tensor_copy(tot_sb[:], brun[:]))
            P.add("gpsimd", lambda g: g.dma_start(cc_in.ap()[0:1, 0:8], tot_sb[:]),
                  inc=16)
            P.add("gpsimd", lambda g: g.collective_compute(
                "AllGather", ALU.bypass, replica_groups=[list(range(8))],
                ins=[cc_in.ap().opt()], outs=[cc_out.ap().opt()]))
            P.add("gpsimd", lambda g: g.dma_start(cnt_all[:], cc_out.ap()[0:8, 0:8]),
                  inc=16)

            def _a2a1(g):
                with allow_ici_all_to_all():
                    return g.collective_compute(
                        "AllToAll", ALU.bypass, replica_groups=[list(range(8))],
                        ins=[snd_d.opt()], outs=[rcv_d.opt()])
            P.add("gpsimd", _a2a1)
            P.add("tensor", lambda t: t.matmul(cntt[0:8, 0:8], lt_sb[0:8, 0:8],
                                               cnt_all[:], start=True, stop=True,
                                               skip_group_check=True))
            P.add("vector", lambda v: v.tensor_copy(basel[:], cntt[0:8, 0:8]))
            P.add("vector", lambda v: v.tensor_tensor(basel[:], basel[:], myx_sb[:],
                                                      ALU.mult))
            P.add("vector", lambda v: v.tensor_reduce(bcol[:], basel[:], AX.X,
                                                      ALU.add))
            P.add("tensor", lambda t: t.transpose(cntt[0:1, 8:16], bcol[:],
                                                  id_f[0:8, 0:8]))
            P.add("vector", lambda v: v.tensor_copy(brow[:], cntt[0:1, 8:16]))
            P.add("tensor", lambda t: t.matmul(cntt[:, 0:8], on1_sb[:], brow[:],
                                               start=True, stop=True,
                                               skip_group_check=True))
            P.add("vector", lambda v: v.tensor_copy(bases_bc[:], cntt[:, 0:8]))
            P_TOTAL = P.total()

            with nc.Block() as blk:
                @blk.vector
                def _(v):
                    v.wait_ge(S("dvbr"), 2 * NT)
                    P.emit("vector", v)

                @blk.tensor
                def _(t):
                    P.emit("tensor", t)

                @blk.gpsimd
                def _(g):
                    g.wait_ge(S("sGd"), n_sGd[0])
                    for pp in range(8):
                        g.wait_ge(S(f"poolsc{pp}"), 16 * (NT // 8))
                    P.emit("gpsimd", g)

            pctx.close()

        if do_c:
            with contextlib.ExitStack() as cctx:
                Ce = cctx.enter_context
                CM, _coff = _man_alloc(sweep_base)
                xe_t = [CM(f"xe{i}", [128, D], BF16) for i in range(2)]
                xeT = [CM(f"xeT{i}", [128, 8, 512], BF16) for i in range(2)]
                h_sb = CM("h_sb", [128, 32, 512], BF16)
                ysnd_t = [CM(f"ysnd{i}", [128, D], BF16) for i in range(4)]
                cj_t = CM("cj_t", [128, 8], F32)
                cmp8_t = CM("cmp8_t", [128, 8], F32)
                bm8_t = CM("bm8_t", [128, 8], F32)
                assert _coff[0] <= window_base, (_coff[0], window_base)
                ht_ps = [Ce(nc.psum_tensor(f"ht_ps{i}", [128, 128], BF16)) for i in range(2)]
                h_ps = [Ce(nc.psum_tensor(f"h_ps{i}", [128, 512], F32)) for i in range(2)]
                y_ps = [[Ce(nc.psum_tensor(f"y_ps{i}_{j}", [128, 512], F32))
                         for j in range(2)] for i in range(2)]

                NHALF = CAP // 512
                NGRP = CAP // 128

                def _sweep_dma_gate(eng):
                    for pp in range(4):
                        eng.wait_ge(S(f"x1st{pp}"), 16 * (NT // 4))
                        eng.wait_ge(S(f"xld{pp}"), 16 * (NT // 4))
                    for pp in range(8):
                        eng.wait_ge(S(f"poolsc{pp}"), 16 * (NT // 8))

                with nc.Block() as blk:
                    @blk.vector
                    def _(v):
                        _sweep_dma_gate(v)
                        v.wait_ge(S("post"), P_TOTAL)
                        for g in range(NGRP):
                            b8 = 8 * g
                            v.tensor_scalar(cj_t[:, 0:1], pidx[:], float(128 * g),
                                            None, ALU.add).then_inc(S("cidx"), 1)
                            v.wait_ge(S("cidx"), b8 + 1)
                            v.tensor_scalar(cmp8_t[:], bases_bc[:], cj_t[:, 0:1],
                                            None, ALU.is_le).then_inc(S("cidx"), 1)
                            v.wait_ge(S("cidx"), b8 + 2)
                            v.tensor_reduce(cj_t[:, 1:2], cmp8_t[:], AX.X,
                                            ALU.add).then_inc(S("cidx"), 1)
                            v.wait_ge(S("cidx"), b8 + 3)
                            v.tensor_tensor(bm8_t[:], bases_bc[:], cmp8_t[:],
                                            ALU.mult).then_inc(S("cidx"), 1)
                            v.wait_ge(S("cidx"), b8 + 4)
                            v.tensor_reduce(cj_t[:, 2:3], bm8_t[:], AX.X,
                                            ALU.max).then_inc(S("cidx"), 1)
                            v.wait_ge(S("cidx"), b8 + 5)
                            v.scalar_tensor_tensor(cj_t[:, 3:4], cj_t[:, 1:2],
                                                   float(SA2A), cj_t[:, 2:3],
                                                   ALU.mult,
                                                   ALU.subtract).then_inc(S("cidx"), 1)
                            v.wait_ge(S("cidx"), b8 + 6)
                            v.tensor_scalar(cj_t[:, 3:4], cj_t[:, 3:4], cj_t[:, 0:1],
                                            -float(SA2A), ALU.add,
                                            ALU.add).then_inc(S("cidx"), 1)
                            v.wait_ge(S("cidx"), b8 + 7)
                            v.tensor_copy(srcq_sb[:, g:g + 1],
                                          cj_t[:, 3:4]).then_inc(S("cidx"), 1)

                    @blk.scalar
                    def _(a):
                        _sweep_dma_gate(a)
                        for hq in range(NHALF):
                            for k in range(32):
                                kk = 32 * hq + k
                                a.wait_ge(S("chm"), kk + 1)
                                s_, fc4 = divmod(k, 4)
                                fc = 4 * s_ + fc4
                                a.activation(h_sb[:, fc, :], h_ps[kk % 2][:],
                                             AF.Relu, bias=0.0,
                                             scale=1.0).then_inc(S("cha"), 1)

                    @blk.tensor
                    def _(t):
                        def emit_tr(hq):
                            if hq >= 2:
                                t.wait_ge(S("chm"), 32 * (hq - 1))
                            for gg in range(4):
                                gi = 4 * hq + gg
                                t.wait_ge(S(f"cg{gi % 2}"), 16 * (gi // 2 + 1))
                                for c in range(8):
                                    j = 8 * gi + c
                                    if j >= 2:
                                        t.wait_ge(S("cxv"), j - 1)
                                    t.transpose(ht_ps[j % 2][:],
                                                xe_t[gi % 2][:, 128 * c:128 * c + 128],
                                                id_bf[:]).then_inc(S("cxt"), 1)
                        _sweep_dma_gate(t)
                        t.wait_ge(S("wdm"), 16 * 8)
                        emit_tr(0)
                        for hq in range(NHALF):
                            if hq == 0:
                                pass
                            t.wait_ge(S("cxv"), 32 * (hq + 1))
                            for s_ in range(8):
                                for fc4 in range(4):
                                    k = 4 * s_ + fc4
                                    kk = 32 * hq + k
                                    if kk >= 2:
                                        t.wait_ge(S("cha"), kk - 1)
                                    h = None
                                    for dc in range(8):
                                        h = t.matmul(
                                            h_ps[kk % 2][:],
                                            w1b_r[:, dc, 512 * s_ + 128 * fc4:
                                                  512 * s_ + 128 * fc4 + 128],
                                            xeT[hq % 2][:, dc, :],
                                            start=(dc == 0), stop=(dc == 7))
                                    h.then_inc(S("chm"), 1)
                            if hq + 1 < NHALF:
                                emit_tr(hq + 1)
                            if hq == 0:
                                t.wait_ge(S("wdm2"), 16 * 4)
                            t.wait_ge(S("cha"), 32 * (hq + 1))
                            for ts in range(4):
                                tsg = 4 * hq + ts
                                for dh in range(2):
                                    if tsg >= 2:
                                        t.wait_ge(S("cyv"), 2 * (tsg - 2) + dh + 1)
                                    h = None
                                    for fc in range(32):
                                        h = t.matmul(
                                            y_ps[tsg % 2][dh][:],
                                            h_sb[:, fc, 128 * ts:128 * ts + 128],
                                            w2b_r[:, fc, 512 * dh:512 * dh + 512],
                                            start=(fc == 0), stop=(fc == 31))
                                    h.then_inc(S("cym"), 1)

                    @blk.vector
                    def _(v):
                        for hq in range(NHALF + 1):
                            if hq < NHALF:
                                for gg in range(6):
                                    if gg >= 2:
                                        gi = 4 * hq + gg - 2
                                        for c in range(8):
                                            j = 8 * gi + c
                                            v.wait_ge(S("cxt"), j + 1)
                                            if hq >= 2 and gg == 2 and c == 0:
                                                v.wait_ge(S("chm"), 32 * (hq - 1))
                                            v.tensor_copy(
                                                xeT[hq % 2][:, c, 128 * (gg - 2):128 * (gg - 2) + 128],
                                                ht_ps[j % 2][:]).then_inc(S("cxv"), 1)
                                    if gg < 4:
                                        gi = 4 * hq + gg
                                        if gi >= 2:
                                            v.wait_ge(S("cxt"), 8 * (gi - 1))
                                        v.memset(xe_t[gi % 2][:], 0.0).then_inc(S("cme"), 1)
                            if hq >= 1:
                                for ts in range(4):
                                    tsg = 4 * (hq - 1) + ts
                                    for dh in range(2):
                                        f = 2 * tsg + dh
                                        v.wait_ge(S("cym"), f + 1)
                                        if tsg >= 4 and dh == 0:
                                            v.wait_ge(S(f"cys{tsg % 4}"),
                                                      16 * (tsg // 4))
                                        v.tensor_copy(
                                            ysnd_t[tsg % 4][:, 512 * dh:512 * dh + 512],
                                            y_ps[tsg % 2][dh][:]).then_inc(S("cyv"), 1)

                    @blk.gpsimd
                    def _(g):
                        with g.register("bc_src") as bc_src:
                            g.reg_mov(bc_src, NE * SA2A - 1)
                            g.wait_ge(S("post"), P_TOTAL)
                            for hq in range(NHALF + 1):
                                if hq < NHALF:
                                    for gg in range(4):
                                        gi = 4 * hq + gg
                                        g.wait_ge(S("cme"), gi + 1)
                                        g.wait_ge(S("cidx"), 8 * (gi + 1))
                                        g.indirect_dma_start(
                                            xe_t[gi % 2][:], None, rcv_d,
                                            IOA(srcq_sb[:, gi:gi + 1], 0),
                                            bounds_check=bc_src,
                                            oob_is_err=False).then_inc(S(f"cg{gi % 2}"), 16)
                                if hq >= 1:
                                    for ts in range(4):
                                        tsg = 4 * (hq - 1) + ts
                                        g.wait_ge(S("cyv"), 2 * tsg + 2)
                                        g.indirect_dma_start(
                                            ysnd_d, IOA(srcq_sb[:, tsg:tsg + 1], 0),
                                            ysnd_t[tsg % 4][:], None,
                                            bounds_check=bc_src,
                                            oob_is_err=False).then_inc(S(f"cys{tsg % 4}"), 16)
                            for pp in range(4):
                                g.wait_ge(S(f"cys{pp}"), 16 * 8)
                            with allow_ici_all_to_all():
                                g.collective_compute(
                                    "AllToAll", ALU.bypass,
                                    replica_groups=[list(range(8))],
                                    ins=[ysnd_d.opt()],
                                    outs=[yrcv_d[0:NE * SA2A, :].opt()]).then_inc(S("caa"), 1)

            with contextlib.ExitStack() as c2ctx:
                CM2, _c2off = _man_alloc(sweep_base)
                ot_t = [CM2(f"ot{i}", [128, D], F32) for i in range(4)]
                yg_t = [CM2(f"yg{i}", [128, D], BF16) for i in range(4)]
                fin_t = [CM2(f"fint{i}", [128, D], F32) for i in range(4)]
                assert _c2off[0] <= window_base, (_c2off[0], window_base)
                with nc.Block() as blk:
                    @blk.gpsimd
                    def _(g):
                        with g.register("bc_f") as bc_f:
                            g.reg_mov(bc_f, NE * SA2A + 127)
                            g.wait_ge(S("caa"), 1)
                            for t2 in range(NT):
                                if t2 >= 4:
                                    g.wait_ge(S("ffa"), t2 - 3)
                                g.indirect_dma_start(
                                    yg_t[t2 % 4][:], None, yrcv_d,
                                    IOA(fslot_sb[:, t2:t2 + 1], 0),
                                    bounds_check=bc_f,
                                    oob_is_err=False).then_inc(S(f"fyg{t2 % 4}"), 16)

                    @blk.sync
                    def _(sy):
                        for pp in range(2):
                            sy.wait_ge(S(f"cg{pp}"), 16 * 16)
                        for pp in range(4):
                            sy.wait_ge(S(f"cys{pp}"), 16 * 8)
                        for pp in range(4):
                            sy.wait_ge(S(f"x1st{pp}"), 16 * (NT // 4))
                        for t2 in range(NT):
                            if t2 >= 4:
                                sy.wait_ge(S("ffa"), t2 - 3)
                            sy.dma_start(ot_t[t2 % 4][:],
                                         otile[t2]).then_inc(S(f"fld{t2 % 4}"), 16)
                            if t2 >= 2:
                                t3 = t2 - 2
                                sy.wait_ge(S("ffa"), t3 + 1)
                                sy.dma_start(otile[t3],
                                             fin_t[t3 % 4][:]).then_inc(S(f"fo{t3 % 4}"), 16)
                        for t3 in (NT - 2, NT - 1):
                            sy.wait_ge(S("ffa"), t3 + 1)
                            sy.dma_start(otile[t3],
                                         fin_t[t3 % 4][:]).then_inc(S(f"fo{t3 % 4}"), 16)

                    @blk.vector
                    def _(v):
                        for pp in range(2):
                            v.wait_ge(S(f"cg{pp}"), 16 * 16)
                        for pp in range(4):
                            v.wait_ge(S(f"cys{pp}"), 16 * 8)
                        for t2 in range(NT):
                            v.wait_ge(S(f"fld{t2 % 4}"), 16 * (t2 // 4 + 1))
                            v.wait_ge(S(f"fyg{t2 % 4}"), 16 * (t2 // 4 + 1))
                            if t2 >= 4:
                                v.wait_ge(S(f"fo{t2 % 4}"), 16 * (t2 // 4))
                            v.tensor_tensor(fin_t[t2 % 4][:], ot_t[t2 % 4][:],
                                            yg_t[t2 % 4][:],
                                            ALU.add).then_inc(S("ffa"), 1)

        with nc.Block() as blk:
            @blk.sync
            def _(sy):
                for pp in range(4):
                    sy.wait_ge(S(f"x1st{pp}"), 16 * (NT // 4))
                for pp in range(8):
                    sy.wait_ge(S(f"poolsc{pp}"), 16 * (NT // 8))
                if do_c:
                    for pp in range(4):
                        sy.wait_ge(S(f"fo{pp}"), 16 * (NT // 4))
                if debug:
                    sy.dma_start(dbg_sml[:, 0:8], bases_bc[:]).then_inc(S("fin"), 16)
                    sy.dma_start(dbg_sml[0:8, 8:16], cnt_all[:]).then_inc(S("fin"), 16)
                    sy.dma_start(dbg_sml[0:1, 16:24], tot_sb[:]).then_inc(S("fin"), 16)
                    sy.dma_start(dbg_sml[0:1, 24:32],
                                 brun[0:1, 0:8]).then_inc(S("fin"), 16)
                    sy.dma_start(dbg_win, winfull[:]).then_inc(S("fin"), 16)
                    sy.dma_start(dbg2, dbgt[:]).then_inc(S("fin"), 16)
                    sy.wait_ge(S("fin"), 96)

    return nc


_NCACHE = {}


def _consts():
    idbf = np.eye(128, dtype=ml_dtypes.bfloat16)
    idf32 = np.eye(128, dtype=np.float32)
    lt = (np.arange(128)[:, None] < np.arange(128)[None, :]).astype(np.float32)
    io8 = np.broadcast_to(np.arange(8, dtype=np.float32), (128, 8)).copy()
    on1 = np.ones((1, 128), dtype=np.float32)
    s = np.arange(32)[:, None, None]
    j = np.arange(32)[None, :, None]
    k = np.arange(32)[None, None, :]
    wm = np.where((j <= s) & (k <= s), 0.0, -1.0e30).astype(np.float32).reshape(32, 1024)
    rrep = (np.arange(32)[:, None] == (np.arange(128)[None, :] % 32)).astype(np.float32)
    rjs = np.zeros((32, 8, 128), dtype=np.float32)
    for c in range(8):
        rjs[:, c, :] = (np.arange(32)[:, None] == (4 * c + np.arange(128)[None, :] // 32))
    return dict(idbf=idbf, idf32=idf32, ltm=lt, iota8=io8,
                iom9c=(io8 - 9.0).astype(np.float32),
                io8p16c=(io8 + 16.0).astype(np.float32), ones1=on1, wmask=wm,
                rrep=rrep, rj=rjs.reshape(32, 1024).astype(np.float32))


def kernel(L0_attn_x, L0_attn_W_Q, L0_attn_W_K, L0_attn_W_V, L0_attn_W_Kp,
           L0_attn_W_O, L0_attn_ln_gamma, L0_attn_ln_beta, L0_ffn_Wr,
           L0_ffn_W1, L0_ffn_W2, _debug=False, _do_c=True, _c_stage=3):
    x = np.ascontiguousarray(np.asarray(L0_attn_x, dtype=np.float32))
    B, Sq, _ = x.shape
    xf = x.reshape(B * Sq, D)
    shared = dict(
        wq=np.ascontiguousarray(np.asarray(L0_attn_W_Q, np.float32)),
        wk=np.ascontiguousarray(np.asarray(L0_attn_W_K, np.float32)),
        wkp=np.ascontiguousarray(np.asarray(L0_attn_W_Kp, np.float32)),
        wv=np.ascontiguousarray(np.asarray(L0_attn_W_V, np.float32)),
        wo=np.ascontiguousarray(np.asarray(L0_attn_W_O, np.float32)),
        gam=np.ascontiguousarray(np.asarray(L0_attn_ln_gamma, np.float32)).reshape(1, D),
        bet=np.ascontiguousarray(np.asarray(L0_attn_ln_beta, np.float32)).reshape(1, D),
        wr=np.ascontiguousarray(np.asarray(L0_ffn_Wr, np.float32)),
        **_consts(),
    )
    w1a = np.asarray(L0_ffn_W1, np.float32)
    w2a = np.asarray(L0_ffn_W2, np.float32)
    in_maps = []
    for c in range(8):
        winap = np.zeros((128, 1), np.float32)
        if c % 2 == 1:
            winap[96:128] = 1.0
        m = dict(shared)
        m["xsh"] = np.ascontiguousarray(xf[4096 * c:4096 * (c + 1)])
        m["winx"] = np.ascontiguousarray(x[c // 2, Sq - 32:Sq, :])
        m["winap"] = winap
        m["coresel"] = (np.arange(8) < c).astype(np.float32).reshape(8, 1)
        m["w1my"] = np.ascontiguousarray(w1a[c])
        m["w2my"] = np.ascontiguousarray(w2a[c])
        m["myexp"] = np.broadcast_to(
            (np.arange(8) == c).astype(np.float32), (8, 8)).copy()
        in_maps.append(m)

    key = (_debug, _do_c, _c_stage)
    if key not in _NCACHE:
        _NCACHE[key] = build_nc(debug=_debug, do_c=_do_c, c_stage=_c_stage)
    nc = _NCACHE[key]
    res = run_bass_kernel_spmd(nc, in_maps, core_ids=list(range(8)))
    kernel._last_results = res.results
    outs = np.concatenate([np.asarray(r["out"]) for r in res.results], axis=0)
    return outs.reshape(B, Sq, D).astype(np.float32)
